# version 03 (final)
# baseline (speedup 1.0000x reference)
import contextlib
import os
import numpy as np


def _dbg(phase, arrs):
    if os.environ.get("KDBG"):
        fin = [bool(np.isfinite(np.asarray(a, np.float32)).all())
               for a in arrs]
        mx = [float(np.abs(np.asarray(a, np.float32)).max()) for a in arrs]
        print("KDBG", phase, fin, [f"{m:.3e}" for m in mx], flush=True)
import ml_dtypes

import concourse.bass as bass
import concourse.mybir as mybir
from concourse.bass_utils import run_bass_kernel_spmd

F32 = mybir.dt.float32
BF16 = mybir.dt.bfloat16
AF = mybir.ActivationFunctionType
ALU = mybir.AluOpType
AX = mybir.AxisListType

BF = ml_dtypes.bfloat16


ENGINES = ("T", "V", "S", "G", "Y")


class Op:
    __slots__ = ("eng", "fn", "dma", "force", "idx", "need_inc", "inc_val",
                 "sem_target", "waits", "tag", "_deps", "dma_slot")

    def __init__(self, eng, fn, dma, force, idx, tag=""):
        self.eng = eng
        self.fn = fn
        self.dma = dma
        self.force = force
        self.idx = idx
        self.need_inc = dma
        self.inc_val = 16 if dma else 1
        self.sem_target = None
        self.waits = []
        self.tag = tag


class Kb:

    PARANOID = False
    PARANOID_TAGS = None

    def __init__(self, nc):
        self.nc = nc
        self.es = contextlib.ExitStack()
        self.ops = {e: [] for e in ENGINES}
        self.last_writer = {}
        self.readers = {}
        self.sem_handles = {}
        self._sem_count = 0
        self.const_cache = {}
        self.finalized = False
        self.alias_deps = {}
        self.dma_counts = {}

    def sb(self, name, shape, dt=F32):
        return self.es.enter_context(self.nc.sbuf_tensor(name, shape, dt))

    def ps(self, name, shape, dt=F32):
        return self.es.enter_context(self.nc.psum_tensor(name, shape, dt))

    def sem(self, key):
        if key not in self.sem_handles:
            self.sem_handles[key] = self.es.enter_context(
                self.nc.semaphore(f"sem_{key}"))
        return self.sem_handles[key]

    def alias(self, new_name, old_keys):
        self.alias_deps[new_name] = list(old_keys)

    NDMA_SLOTS = 8

    def op(self, eng, fn, reads=(), writes=(), dma=False, force=False, tag=""):
        o = Op(eng, fn, dma, force, len(self.ops[eng]), tag)
        if dma:
            o.dma_slot = self.dma_counts.get(eng, 0)
            self.dma_counts[eng] = o.dma_slot + 1
        self.ops[eng].append(o)
        deps = list(getattr(self, "barrier_deps", ()))
        par = Kb.PARANOID and (Kb.PARANOID is True or eng in Kb.PARANOID)
        if par and Kb.PARANOID_TAGS is not None:
            par = any(tag.startswith(t) for t in Kb.PARANOID_TAGS)
        if par:
            for e2 in ENGINES:
                lst = self.ops[e2]
                for d in lst[-3:]:
                    deps.append(d)
        for k in writes:
            ad = self.alias_deps.get(k[0] if isinstance(k, tuple) else k)
            if ad:
                for ok in ad:
                    w2 = self.last_writer.get(ok)
                    if w2 is not None:
                        deps.append(w2)
                    deps.extend(self.readers.get(ok, ()))
        for k in reads:
            w = self.last_writer.get(k)
            if w is not None:
                deps.append(w)
            self.readers.setdefault(k, []).append(o)
        for k in writes:
            w = self.last_writer.get(k)
            if w is not None:
                deps.append(w)
            for r in self.readers.get(k, ()):
                deps.append(r)
            self.readers[k] = []
            self.last_writer[k] = o
        o._deps = []
        seen = set()
        for d in deps:
            if d is o:
                continue
            if d.eng == eng and not d.dma and not d.force and not dma:
                continue
            if d.eng == eng and not d.dma and not d.force and dma:
                pass
            key = id(d)
            if key in seen:
                continue
            seen.add(key)
            o._deps.append(d)
            d.need_inc = True
        return o

    def dma(self, eng, dst_ap, src_ap, reads, writes, tag=""):
        return self.op(eng, lambda E, d=dst_ap, s=src_ap: E.dma_start(d, s),
                       reads=reads, writes=writes, dma=True, tag=tag)

    def set_setup_barrier(self):
        self.barrier_deps = [o for e in ENGINES for o in self.ops[e]]

    def finalize_into_block(self, block):
        assert not self.finalized
        self.finalized = True
        NS = Kb.NDMA_SLOTS
        for e in ENGINES:
            eng_c = 0
            for o in self.ops[e]:
                if o.dma:
                    o.sem_target = 16 * (o.dma_slot // NS + 1)
                elif o.need_inc:
                    eng_c += 1
                    o.sem_target = eng_c
        for e in ENGINES:
            hw = {}
            for o in self.ops[e]:
                tgts = {}
                for d in o._deps:
                    if d.dma:
                        skey = f"{d.eng}D{d.dma_slot % NS}"
                    else:
                        skey = d.eng
                    assert d.sem_target is not None
                    tgts[skey] = max(tgts.get(skey, 0), d.sem_target)
                for skey, v in sorted(tgts.items()):
                    if hw.get(skey, 0) >= v:
                        continue
                    hw[skey] = v
                    o.waits.append((skey, v))

        emitted = {}

        def emit(E, e):
            for o in self.ops[e]:
                for skey, v in o.waits:
                    E.wait_ge(self.sem(skey), v)
                inst = o.fn(E)
                if o.dma:
                    inst.then_inc(self.sem(f"{e}D{o.dma_slot % NS}"), 16)
                elif o.need_inc:
                    inst.then_inc(self.sem(o.eng), o.inc_val)
            nd = self.dma_counts.get(e, 0)
            for s in range(min(nd, NS)):
                tot = 16 * ((nd - 1 - s) // NS + 1)
                E.wait_ge(self.sem(f"{e}D{s}"), tot)
            emitted[e] = True

        @block.tensor
        def _(E):
            emit(E, "T")

        @block.vector
        def _(E):
            emit(E, "V")

        @block.scalar
        def _(E):
            emit(E, "S")

        @block.gpsimd
        def _(E):
            emit(E, "G")

        @block.sync
        def _(E):
            emit(E, "Y")

    def stats(self):
        return {e: len(self.ops[e]) for e in ENGINES}



class TBuf:

    def __init__(self, kb, name, shape, dt=F32, psum=False):
        self.name = name
        self.shape = shape
        self.dt = dt
        self.h = (kb.ps if psum else kb.sb)(name, shape, dt)

    def k(self, *sub):
        return (self.name,) + sub

    def __getitem__(self, idx):
        return self.h[idx]


def ceil_div(a, b):
    return (a + b - 1) // b


TT = 512


class FM:

    def __init__(self, kb, name, nch, W, dt=F32, back=None):
        self.name, self.nch, self.W, self.dt = name, nch, W, dt
        if back is None:
            self.h = kb.sb(name, [128, nch * W], dt)
        else:
            self.h = back

    def c(self, ch, t0=0, t1=None, p0=0, p1=128):
        t1 = self.W if t1 is None else t1
        return self.h[p0:p1, ch * self.W + t0: ch * self.W + t1]

    def keys(self, ch=None, t0=0, t1=None):
        t1 = self.W if t1 is None else t1
        chs = range(self.nch) if ch is None else [ch]
        return [(self.name, c, i) for c in chs
                for i in range(t0 // TT, ceil_div(t1, TT))]


class TM:

    def __init__(self, kb, name, ntch, W, dt=F32):
        self.name, self.ntch, self.W, self.dt = name, ntch, W, dt
        self.h = kb.sb(name, [128, ntch * W], dt)

    def c(self, tc, c0=0, c1=None, p0=0, p1=128):
        c1 = self.W if c1 is None else c1
        return self.h[p0:p1, tc * self.W + c0: tc * self.W + c1]

    def keys(self, tc=None, c0=0, c1=None):
        tcs = range(self.ntch) if tc is None else [tc]
        return [(self.name, t, 0) for t in tcs]


class Row:

    def __init__(self, kb, name, P, W, dt=F32):
        self.name, self.P, self.W, self.dt = name, P, W, dt
        self.h = kb.sb(name, [P, W], dt)

    def r(self, p0=0, p1=None, t0=0, t1=None):
        p1 = self.P if p1 is None else p1
        t1 = self.W if t1 is None else t1
        return self.h[p0:p1, t0:t1]

    def keys(self):
        return [(self.name, 0, 0)]


class Psum:
    def __init__(self, kb, name, shape=(128, TT), dt=F32):
        self.name = name
        self.h = kb.ps(name, list(shape), dt)

    def keys(self):
        return [(self.name, 0, 0)]




class Consts:

    SPECS = {
        "ones_bf": ("c_ones_bf", [128, 128], "BF16", "CONST_ones",
                    [128, 128], "G"),
        "ones_f": ("c_ones_f", [128, 128], "F32", "CONST_ones",
                   [128, 128], "Y"),
        "blockones_h": ("c_bo", [128, 8 * 16], "BF16", "CONST_blockones",
                        [128, 8 * 16], "G"),
        "sel128": ("c_sel", [16, 16 * 128], "BF16", "CONST_sel",
                   [16, 16 * 128], "G"),
        "sel128f": ("c_self", [16, 16 * 128], "F32", "CONST_sel",
                    [16, 16 * 128], "Y"),
        "sel16": ("c_sel16", [1, 256], "BF16", "CONST_sel16", [1, 256], "G"),
        "sel16f": ("c_sel16f", [1, 256], "F32", "CONST_sel16", [1, 256], "Y"),
        "eye_f": ("c_eyef", [128, 128], "F32", "CONST_eye", [128, 128], "Y"),
        "eye_bf": ("c_eyeb", [128, 128], "BF16", "CONST_eye",
                   [128, 128], "G"),
        "iota8": ("c_iota8", [128, 8], "F32", "CONST_iota8", [128, 8], "Y"),
        "eps_sb": ("c_eps", [128, 4], "F32", "CONST_eps", [128, 4], "Y"),
    }

    PRETOUCH = None
    RECORD = None

    def __init__(self, kb, nc):
        self._kb, self._nc = kb, nc
        self.EPS_COLS = {1e-5: 0, 1e-6: 1, 1e-9: 2, 1e-12: 3}
        self.blockones = self
        self._drams = {}
        if Consts.PRETOUCH:
            self.touch(*Consts.PRETOUCH)

    def touch(self, *names):
        for n in (names or Consts.SPECS.keys()):
            getattr(self, n)

    def __getattr__(self, name):
        spec = Consts.SPECS.get(name)
        if spec is None:
            raise AttributeError(name)
        if Consts.RECORD is not None and name not in Consts.RECORD:
            Consts.RECORD.append(name)
        sbn, shape, dts, dn, dsh, q = spec
        dt = BF16 if dts == "BF16" else F32
        kb = self._kb
        if dn not in self._drams:
            self._drams[dn] = self._nc.dram_tensor(
                dn, dsh, F32, kind="ExternalInput").ap()
        h = kb.sb(sbn, shape, dt)
        o = kb.dma(q, h[:, :], self._drams[dn], reads=[],
                   writes=[(sbn, 0, 0)])
        bd = getattr(kb, "barrier_deps", None)
        if bd is not None:
            bd.append(o)
        object.__setattr__(self, name, h)
        return h

    def eps_ap(self, val, p=1):
        c = self.EPS_COLS[val]
        return self.eps_sb[0:p, c:c + 1]

    def c2(self, c, nh):
        return self.blockones_h[:, c * 16:c * 16 + nh]

    @staticmethod
    def host_inputs():
        bo = np.zeros((128, 8 * 16), np.float32)
        for c in range(8):
            bo[0:64, c * 16 + 2 * c] = 1.0
            bo[64:128, c * 16 + 2 * c + 1] = 1.0
        sel = np.zeros((16, 16 * 128), np.float32)
        for h in range(16):
            sel[h, h * 128:(h + 1) * 128] = 1.0
        sel16 = np.zeros((1, 256), np.float32)
        for h in range(16):
            sel16[0, h * 16 + h] = 1.0
        epsm = np.zeros((128, 4), np.float32)
        epsm[:, 0] = 1e-5; epsm[:, 1] = 1e-6
        epsm[:, 2] = 1e-9; epsm[:, 3] = 1e-12
        return {"CONST_ones": np.ones((128, 128), np.float32),
                "CONST_blockones": bo, "CONST_sel": sel,
                "CONST_sel16": sel16, "CONST_eps": epsm,
                "CONST_eye": np.eye(128, dtype=np.float32),
                "CONST_iota8": np.tile(np.arange(8, dtype=np.float32),
                                       (128, 1))}


class Scratch:

    def __init__(self, kb, Tw, attn=True, cmb=True, rows16=True, tt2=True):
        self.Tw = Tw
        self.tt_f32 = TBufSimple(kb, "scr_tt", [128, TT], F32)
        self.row_mu = Row(kb, "scr_mu", 1, TT)
        self.row_rstd = Row(kb, "scr_rstd", 1, TT)
        self.row_t1 = Row(kb, "scr_t1", 1, TT)
        if rows16:
            self.row_t2 = Row(kb, "scr_t2", 1, TT)
        if rows16:
            self.hrow_a = Row(kb, "scr_ha", 16, Tw, BF16)
        if attn:
            self.hrow_b = Row(kb, "scr_hb", 16, Tw, BF16)
            self.hrow_f = Row(kb, "scr_hf", 16, Tw, F32)
            self.dstage = [Row(kb, f"scr_ds{i}", 1, TT) for i in range(2)]
            self.d_hrow = Row(kb, "scr_dh", 16, Tw)
            self.ptile = [TBufSimple(kb, f"scr_pt{i}", [128, TT], BF16)
                          for i in range(2)]
        if rows16:
            self.hcol_a = Row(kb, "scr_hca", 16, 8)
        self.cmb_rt = TBufSimple(kb, "scr_crt", [128, 12], F32)
        self.cmb_a = TBufSimple(kb, "scr_ca", [128, 32], F32)
        if cmb:
            self.cmb_sa = TBufSimple(kb, "cmbsa", [128, 16 * 12], F32)
            self.cmb_ea = TBufSimple(kb, "cmbea", [128, 16 * 12], F32)
        if tt2:
            self.tt2 = TBufSimple(kb, "scr_tt2", [128, TT], F32)
        if not rows16 and tt2:
            class _RV:
                def __init__(s, h):
                    s.h = h
                def r(s, p0=0, p1=1, t0=0, t1=None):
                    return s.h[p0:p1, t0:(TT if t1 is None else t1)]
                def keys(s):
                    return [("scr_tt2", 0, 0)]
            self.row_t2 = _RV(self.tt2.h)
        self.pb = [Psum(kb, f"psum{i}") for i in range(8)]


class TBufSimple:
    def __init__(self, kb, name, shape, dt=F32):
        self.name = name
        self.h = kb.sb(name, shape, dt)

    def keys(self, *a):
        return [(self.name, 0, 0)]


def load_weight_fm(kb, dst_fm, dram_ap, nrows, ncols, eng="G", tag="w"):
    nch = ceil_div(nrows, 128)
    for c in range(nch):
        p = min(128, nrows - c * 128)
        kb.dma(eng, dst_fm.c(c, 0, ncols, 0, p),
               dram_ap[c * 128:c * 128 + p, :],
               reads=[], writes=dst_fm.keys(c, 0, ncols), tag=f"{tag}{c}")


def load_row(kb, kbuf, dram_ap, name=None, eng="Y"):
    kb.dma(eng, kbuf.h[:, :], dram_ap, reads=[], writes=kbuf.keys())



def emit_ln_fm(kb, C, sc, x, xn, xsq, xb=None, nch=8, Tw=None, eps=1e-5,
               gb=None, out_dram=None, tag="ln"):
    Tw = sc.Tw if Tw is None else Tw
    D = nch * 128
    if xb is None:
        xb = x
    for c in range(nch):
        if xb is not x:
            kb.op("S", lambda E, a=xb.c(c, 0, Tw), b=x.c(c, 0, Tw):
                  E.copy(a, b),
                  reads=x.keys(c, 0, Tw), writes=xb.keys(c, 0, Tw),
                  tag=tag + "cast")
        kb.op("S", lambda E, a=xsq.c(c, 0, Tw), b=xb.c(c, 0, Tw):
              E.activation(a, b, AF.Square),
              reads=xb.keys(c, 0, Tw), writes=xsq.keys(c, 0, Tw),
              tag=tag + "sq")
    nt = ceil_div(Tw, TT)
    for it in range(nt):
        t0, t1 = it * TT, min(Tw, (it + 1) * TT)
        w = t1 - t0
        ps = sc.pb[6]
        ps2 = sc.pb[7]
        for c in range(nch):
            kb.op("T", lambda E, p=ps.h[0:1, 0:w], o=C.ones_bf[0:128, 0:1],
                  r=xb.c(c, t0, t1), s=(c == 0), e=(c == nch - 1):
                  E.matmul(p, o, r, start=s, stop=e),
                  reads=xb.keys(c, t0, t1), writes=ps.keys(), tag=tag + "sum")
        for c in range(nch):
            kb.op("T", lambda E, p=ps2.h[0:1, 0:w], o=C.ones_bf[0:128, 0:1],
                  r=xsq.c(c, t0, t1), s=(c == 0), e=(c == nch - 1):
                  E.matmul(p, o, r, start=s, stop=e),
                  reads=xsq.keys(c, t0, t1), writes=ps2.keys(), tag=tag + "sq2")
        mu, rstd, tmp1, tmp2 = sc.row_mu, sc.row_rstd, sc.row_t1, sc.row_t2
        kb.op("S", lambda E, a=mu.r(0, 1, 0, w), b=ps.h[0:1, 0:w]:
              E.mul(a, b, 1.0 / D),
              reads=ps.keys(), writes=mu.keys(), tag=tag + "mu")
        kb.op("S", lambda E, a=tmp1.r(0, 1, 0, w), b=ps2.h[0:1, 0:w]:
              E.mul(a, b, 1.0 / D),
              reads=ps2.keys(), writes=tmp1.keys(), tag=tag + "ex2")
        kb.op("V", lambda E, a=tmp2.r(0, 1, 0, w), b=mu.r(0, 1, 0, w):
              E.tensor_mul(a, b, b),
              reads=mu.keys(), writes=tmp2.keys(), tag=tag + "musq")
        kb.op("V", lambda E, a=tmp1.r(0, 1, 0, w), b=tmp1.r(0, 1, 0, w),
              c2=tmp2.r(0, 1, 0, w): E.tensor_sub(a, b, c2),
              reads=tmp2.keys(), writes=tmp1.keys(), tag=tag + "var")
        kb.op("S", lambda E, a=tmp1.r(0, 1, 0, w), b=tmp1.r(0, 1, 0, w):
              E.activation(a, b, AF.Sqrt, bias=C.eps_ap(eps)),
              reads=[("c_eps", 0, 0)], writes=tmp1.keys(), tag=tag + "sd")
        kb.op("V", lambda E, a=rstd.r(0, 1, 0, w), b=tmp1.r(0, 1, 0, w):
              E.reciprocal(a, b),
              reads=tmp1.keys(), writes=rstd.keys(), force=True,
              tag=tag + "rstd")
        pbm = sc.pb[5]
        pbr = sc.pb[4]
        kb.op("T", lambda E, p=pbm.h[:, 0:w], o=C.ones_f[0:1, 0:128],
              r=mu.r(0, 1, 0, w): E.matmul(p, o, r, start=True, stop=True),
              reads=mu.keys(), writes=pbm.keys(), tag=tag + "bcm")
        kb.op("T", lambda E, p=pbr.h[:, 0:w], o=C.ones_f[0:1, 0:128],
              r=rstd.r(0, 1, 0, w): E.matmul(p, o, r, start=True, stop=True),
              reads=rstd.keys(), writes=pbr.keys(), tag=tag + "bcr")
        for c in range(nch):
            kb.op("V", lambda E, a=sc.tt_f32.h[:, 0:w], b=x.c(c, t0, t1),
                  m=pbm.h[:, 0:w]: E.tensor_sub(a, b, m),
                  reads=x.keys(c, t0, t1) + pbm.keys(),
                  writes=sc.tt_f32.keys(), tag=tag + "sub")
            if gb is None:
                kb.op("V", lambda E, a=xn.c(c, t0, t1), b=sc.tt_f32.h[:, 0:w],
                      m=pbr.h[:, 0:w]: E.tensor_mul(a, b, m),
                      reads=sc.tt_f32.keys() + pbr.keys(),
                      writes=xn.keys(c, t0, t1), tag=tag + "mul")
                if out_dram is not None:
                    kb.op("V", lambda E, a=sc.tt2.h[:, 0:w],
                          b=sc.tt_f32.h[:, 0:w], m=pbr.h[:, 0:w]:
                          E.tensor_mul(a, b, m),
                          reads=sc.tt_f32.keys() + pbr.keys(),
                          writes=sc.tt2.keys(), tag=tag + "mulf")
                    kb.dma("Y", out_dram[c * 128:(c + 1) * 128, t0:t1],
                           sc.tt2.h[:, 0:w], reads=sc.tt2.keys(), writes=[],
                           tag=tag + "zs")
            else:
                gcol, bcol = gb
                kb.op("V", lambda E, a=sc.tt_f32.h[:, 0:w],
                      b=sc.tt_f32.h[:, 0:w], m=pbr.h[:, 0:w]:
                      E.tensor_mul(a, b, m),
                      reads=pbr.keys(), writes=sc.tt_f32.keys(), tag=tag + "mulr")
                kb.op("V", lambda E, a=xn.c(c, t0, t1), b=sc.tt_f32.h[:, 0:w],
                      g=gcol[:, c:c + 1], bb=bcol[:, c:c + 1]:
                      E.tensor_scalar(a, b, g, bb, ALU.mult, ALU.add),
                      reads=sc.tt_f32.keys() + [("wrow", 0, 0)],
                      writes=xn.keys(c, t0, t1), tag=tag + "gb")



def emit_proj_fm(kb, C, sc, out, inp, wfm, nchi, ncho, Tw=None, bias_col=None,
                 psum_ids=(0, 1), evac=None, evac_eng="V", acc_extra=None,
                 tag="proj"):
    Tw = sc.Tw if Tw is None else Tw
    nt = ceil_div(Tw, TT)
    nb = len(psum_ids)
    for m in range(ncho):
        for it in range(nt):
            t0, t1 = it * TT, min(Tw, (it + 1) * TT)
            w = t1 - t0
            ps = sc.pb[psum_ids[(m * nt + it) % nb]]
            for c in range(nchi):
                kb.op("T", lambda E, p=ps.h[:, 0:w],
                      l=wfm.c(c, m * 128, (m + 1) * 128),
                      r=inp.c(c, t0, t1), s=(c == 0),
                      e=(c == nchi - 1 and acc_extra is None):
                      E.matmul(p, l, r, start=s, stop=e),
                      reads=wfm.keys(c, m * 128, (m + 1) * 128) +
                      inp.keys(c, t0, t1),
                      writes=ps.keys(), tag=tag + "mm")
            if acc_extra is not None:
                acc_extra(kb, m, it, ps)
            if evac is not None:
                evac(kb, m, t0, t1, ps)
            elif bias_col is not None:
                kb.op("S", lambda E, a=out.c(m, t0, t1), p=ps.h[:, 0:w],
                      b=bias_col[:, m:m + 1]:
                      E.activation(a, p, AF.Identity, bias=b),
                      reads=ps.keys() + [("wrow", 0, 0)],
                      writes=out.keys(m, t0, t1), tag=tag + "ev")
            elif evac_eng == "S":
                kb.op("S", lambda E, a=out.c(m, t0, t1), p=ps.h[:, 0:w]:
                      E.copy(a, p),
                      reads=ps.keys(), writes=out.keys(m, t0, t1),
                      tag=tag + "ev")
            else:
                kb.op("V", lambda E, a=out.c(m, t0, t1), p=ps.h[:, 0:w]:
                      E.tensor_copy(a, p),
                      reads=ps.keys(), writes=out.keys(m, t0, t1),
                      tag=tag + "ev")


def emit_proj_tm(kb, C, sc, out_tm, inp, wfm, nchi, ncols, Tw=None,
                 bias_row=None, psum_ids=(0, 1), colmap=None, ones_cols=None,
                 tag="ptm"):
    Tw = sc.Tw if Tw is None else Tw
    ntc = Tw // 128
    nnb = ceil_div(ncols, TT)
    nb = len(psum_ids)
    for tc in range(ntc):
        for nbk in range(nnb):
            n0, n1 = nbk * TT, min(ncols, (nbk + 1) * TT)
            w = n1 - n0
            ps = sc.pb[psum_ids[(tc * nnb + nbk) % nb]]
            for c in range(nchi):
                kb.op("T", lambda E, p=ps.h[:, 0:w],
                      l=inp.c(c, tc * 128, (tc + 1) * 128),
                      r=wfm.c(c, n0, n1), s=(c == 0),
                      e=(c == nchi - 1 and bias_row is None):
                      E.matmul(p, l, r, start=s, stop=e),
                      reads=wfm.keys(c, n0, n1) +
                      inp.keys(c, tc * 128, (tc + 1) * 128),
                      writes=ps.keys(), tag=tag + "mm")
            if bias_row is not None:
                kb.op("T", lambda E, p=ps.h[:, 0:w], o=C.ones_bf[0:1, 0:128],
                      r=bias_row[0:1, n0:n1]:
                      E.matmul(p, o, r, start=False, stop=True),
                      reads=[("wrow", 0, 0)], writes=ps.keys(),
                      tag=tag + "bias")
            if colmap is not None:
                dst = colmap(tc, n0, n1)
                psrc = ps.h[:, 0:w].rearrange("p (h w) -> p h w", w=64)
            else:
                dst = out_tm.c(tc, n0, n1)
                psrc = ps.h[:, 0:w]
            kb.op("V", lambda E, a=dst, p=psrc: E.tensor_copy(a, p),
                  reads=ps.keys(), writes=out_tm.keys(tc), tag=tag + "ev")


def emit_head_norms(kb, C, sc, a_fm, nch, nh, out_hrow, sq_scratch, Tw=None,
                    psum_id=6, tag="hn"):
    Tw = sc.Tw if Tw is None else Tw
    sq = sq_scratch
    for c in range(nch):
        kb.op("S", lambda E, a=sq.c(c, 0, Tw), b=a_fm.c(c, 0, Tw):
              E.activation(a, b, AF.Square),
              reads=a_fm.keys(c, 0, Tw), writes=sq.keys(c, 0, Tw),
              tag=tag + "sq")
    nt = ceil_div(Tw, TT)
    for it in range(nt):
        t0, t1 = it * TT, min(Tw, (it + 1) * TT)
        w = t1 - t0
        ps = sc.pb[psum_id]
        for c in range(nch):
            lhs = (C.ones_bf[0:128, 0:1] if nh == 1 else C.c2(c, nh))
            kb.op("T", lambda E, p=ps.h[0:nh, 0:w],
                  l=lhs, r=sq.c(c, t0, t1),
                  s=(c == 0), e=(c == nch - 1):
                  E.matmul(p, l, r, start=s, stop=e),
                  reads=sq.keys(c, t0, t1), writes=ps.keys(), tag=tag + "mm")
        kb.op("S", lambda E, a=out_hrow.r(0, nh, t0, t1), p=ps.h[0:nh, 0:w]:
              E.activation(a, p, AF.Sqrt),
              reads=ps.keys(), writes=out_hrow.keys(), tag=tag + "sqrt")



def emit_attn(kb, C, sc, *, nh, q_fm, nchq, negB_hrow, kv_srcs, slabK, slabV,
              ctx_acc, ctx_bf, Tw=None, mask_fn=None, kt_filter=None,
              scale_pre_exp=None, kv_head=None, divide=True, tag="at"):
    Tw = sc.Tw if Tw is None else Tw
    nt = ceil_div(Tw, TT)
    nkv = nh if kv_head is None else (max(kv_head(h) for h in range(nh)) + 1)
    kvh = (lambda h: h) if kv_head is None else kv_head
    first_src = [True] * (nh * nt)
    d_first = [True] * nt
    scl = 1.0 if scale_pre_exp is None else scale_pre_exp
    for si, (k_load, v_load, nkt) in enumerate(kv_srcs):
        k_load(kb, slabK)
        v_load(kb, slabV)
        last_src = (si == len(kv_srcs) - 1)
        for h in range(nh):
            kh = kvh(h)
            hc, hp = kh // 2, (kh % 2) * 64
            qc, qp = h // 2, (h % 2) * 64
            for it in range(nt):
                t0, t1 = it * TT, min(Tw, (it + 1) * TT)
                w = t1 - t0
                cks = list(range(nkt) if kt_filter is None
                           else kt_filter(si, h, it))
                if not cks:
                    continue
                ppv = sc.pb[(h + it) % 2]
                for j, ck in enumerate(cks):
                    pss = sc.pb[2 + (j % 2)]
                    kap = slabK.c(hc, ck * 128, (ck + 1) * 128, hp, hp + 64)
                    kb.op("T", lambda E, p=pss.h[:, 0:w], l=kap,
                          r=q_fm.c(qc, t0, t1, qp, qp + 64):
                          E.matmul(p, l, r, start=True, stop=False),
                          reads=[(slabK.name, hc, ck)] + q_fm.keys(qc, t0, t1),
                          writes=pss.keys(), tag=tag + "s")
                    kb.op("T", lambda E, p=pss.h[:, 0:w],
                          l=C.sel128[:, h * 128:(h + 1) * 128],
                          r=negB_hrow.h[0:16, t0:t1]:
                          E.matmul(p, l, r, start=False, stop=True),
                          reads=negB_hrow.keys(), writes=pss.keys(),
                          tag=tag + "b")
                    pt = sc.ptile[j % 2]
                    kb.op("S", lambda E, a=pt.h[:, 0:w], p=pss.h[:, 0:w],
                          s2=scl: E.activation(a, p, AF.Exp, scale=s2),
                          reads=pss.keys(), writes=pt.keys(), tag=tag + "e")
                    if mask_fn is not None:
                        mk = mask_fn(si, h, it, ck)
                        if mk is not None:
                            map_, mkeys = mk
                            kb.op("V", lambda E, a=pt.h[:, 0:w], m=map_:
                                  E.tensor_mul(a, a, m),
                                  reads=mkeys, writes=pt.keys(),
                                  tag=tag + "m")
                    vap = slabV.h[:, ck * (65 * nkv) + kh * 65:
                                  ck * (65 * nkv) + kh * 65 + 65]
                    kb.op("T", lambda E, p=ppv.h[0:65, 0:w], l=vap,
                          r=pt.h[:, 0:w], s=(j == 0), e=(j == len(cks) - 1):
                          E.matmul(p, l, r, start=s, stop=e),
                          reads=[(slabV.name, ck, 0)] + pt.keys(),
                          writes=ppv.keys(), tag=tag + "pv")
                fs = first_src[h * nt + it]
                if fs:
                    kb.op("V", lambda E, a=ctx_acc.c(qc, t0, t1, qp, qp + 64),
                          p=ppv.h[0:64, 0:w]: E.tensor_copy(a, p),
                          reads=ppv.keys(),
                          writes=[(ctx_acc.name, qc, it, qp)], tag=tag + "c0")
                else:
                    kb.op("V", lambda E, a=ctx_acc.c(qc, t0, t1, qp, qp + 64),
                          p=ppv.h[0:64, 0:w]: E.tensor_add(a, a, p),
                          reads=ppv.keys() + [(ctx_acc.name, qc, it, qp)],
                          writes=[(ctx_acc.name, qc, it, qp)], tag=tag + "c1")
                first_src[h * nt + it] = False
                ds = sc.dstage[h % 2]
                kb.op("S", lambda E, a=ds.h[0:1, 0:w], p=ppv.h[64:65, 0:w]:
                      E.copy(a, p),
                      reads=ppv.keys(), writes=ds.keys(), tag=tag + "dc")
                pd = sc.pb[4 + it]
                kb.op("T", lambda E, p=pd.h[0:16, 0:w],
                      l=C.sel16f[0:1, h * 16:(h + 1) * 16], r=ds.h[0:1, 0:w],
                      s=d_first[it]:
                      E.matmul(p, l, r, start=s, stop=True,
                               skip_group_check=True),
                      reads=ds.keys(), writes=pd.keys(), tag=tag + "dm")
                d_first[it] = False
    for it in range(nt):
        t0, t1 = it * TT, min(Tw, (it + 1) * TT)
        w = t1 - t0
        pd = sc.pb[4 + it]
        kb.op("V", lambda E, a=sc.d_hrow.h[0:16, t0:t1], p=pd.h[0:16, 0:w]:
              E.tensor_copy(a, p),
              reads=pd.keys(), writes=[(sc.d_hrow.name, 0, it)],
              tag=tag + "dx")
    if divide:
        emit_ctx_divide(kb, C, sc, nh, ctx_acc, ctx_bf, Tw, tag=tag)


def emit_ctx_divide(kb, C, sc, nh, ctx_acc, ctx_bf, Tw, tag="cd"):
    nt = ceil_div(Tw, TT)
    dkeys = [(sc.d_hrow.name, 0, i) for i in range(nt)]
    kb.op("V", lambda E, a=sc.d_hrow.h[0:nh, 0:Tw]: E.reciprocal(a, a),
          reads=dkeys, writes=dkeys, force=True, tag=tag + "rc")
    nchq = ctx_acc.nch
    for c in range(nchq):
        for it in range(nt):
            t0, t1 = it * TT, min(Tw, (it + 1) * TT)
            w = t1 - t0
            pbc = sc.pb[4 + (it % 2)]
            h0, h1 = 2 * c, 2 * c + 1
            kb.op("T", lambda E, p=pbc.h[0:64, 0:w],
                  l=C.sel128f[:, h0 * 128:h0 * 128 + 64],
                  r=sc.d_hrow.h[0:16, t0:t1]:
                  E.matmul(p, l, r, start=True, stop=True),
                  reads=dkeys, writes=pbc.keys(), tag=tag + "b1")
            kb.op("T", lambda E, p=pbc.h[64:128, 0:w],
                  l=C.sel128f[:, h1 * 128:h1 * 128 + 64],
                  r=sc.d_hrow.h[0:16, t0:t1]:
                  E.matmul(p, l, r, start=True, stop=True),
                  reads=dkeys, writes=pbc.keys(), tag=tag + "b2")
            kb.op("V", lambda E, a=ctx_bf.c(c, t0, t1),
                  b=ctx_acc.c(c, t0, t1), p=pbc.h[:, 0:w]:
                  E.tensor_mul(a, b, p),
                  reads=pbc.keys() + [(ctx_acc.name, c, it, 0),
                                      (ctx_acc.name, c, it, 64)],
                  writes=ctx_bf.keys(c, t0, t1), tag=tag + "dv")



def load_fm(kb, dst_fm, dram_ap, nrows, Tw, eng="Y", tag="lfm", dt_cast=False):
    e = "G" if dt_cast else eng
    nch = ceil_div(nrows, 128)
    for c in range(nch):
        p = min(128, nrows - c * 128)
        kb.dma(e, dst_fm.c(c, 0, Tw, 0, p), dram_ap[c * 128:c * 128 + p, :],
               reads=[], writes=dst_fm.keys(c, 0, Tw), tag=f"{tag}{c}")


def store_fm(kb, dram_ap, src_fm, nrows, Tw, eng="Y", tag="sfm",
             read_keys=None):
    nch = ceil_div(nrows, 128)
    for c in range(nch):
        p = min(128, nrows - c * 128)
        kb.dma(eng, dram_ap[c * 128:c * 128 + p, :], src_fm.c(c, 0, Tw, 0, p),
               reads=src_fm.keys(c, 0, Tw) if read_keys is None else read_keys,
               writes=[], tag=f"{tag}{c}")


def store_tm(kb, dram_ap, src_tm, ntc, W, tag="stm"):
    for tc in range(ntc):
        kb.dma("Y", dram_ap[tc * 128:(tc + 1) * 128, :], src_tm.c(tc),
               reads=src_tm.keys(tc), writes=[], tag=f"{tag}{tc}")


def load_tm(kb, dst_tm, dram_ap, ntc, W, tag="ltm"):
    for tc in range(ntc):
        kb.dma("Y", dst_tm.c(tc), dram_ap[tc * 128:(tc + 1) * 128, :],
               reads=[], writes=dst_tm.keys(tc), tag=f"{tag}{tc}")



def emit_ln_add_residual(kb, C, sc, y, xres, xsq, xb=None, nch=8, Tw=None,
                         eps=1e-5, xres_dram=None, tag="lnr"):
    Tw = sc.Tw if Tw is None else Tw
    D = nch * 128
    if xb is None:
        xb = y
    for c in range(nch):
        if xb is not y:
            kb.op("S", lambda E, a=xb.c(c, 0, Tw), b=y.c(c, 0, Tw):
                  E.copy(a, b),
                  reads=y.keys(c, 0, Tw), writes=xb.keys(c, 0, Tw),
                  tag=tag + "c")
        kb.op("S", lambda E, a=xsq.c(c, 0, Tw), b=xb.c(c, 0, Tw):
              E.activation(a, b, AF.Square),
              reads=xb.keys(c, 0, Tw), writes=xsq.keys(c, 0, Tw), tag=tag + "q")
    nt = ceil_div(Tw, TT)
    for it in range(nt):
        t0, t1 = it * TT, min(Tw, (it + 1) * TT)
        w = t1 - t0
        ps, ps2 = sc.pb[6], sc.pb[7]
        for c in range(nch):
            kb.op("T", lambda E, p=ps.h[0:1, 0:w], o=C.ones_bf[0:128, 0:1],
                  r=xb.c(c, t0, t1), s=(c == 0), e=(c == nch - 1):
                  E.matmul(p, o, r, start=s, stop=e),
                  reads=xb.keys(c, t0, t1), writes=ps.keys(), tag=tag + "s1")
        for c in range(nch):
            kb.op("T", lambda E, p=ps2.h[0:1, 0:w], o=C.ones_bf[0:128, 0:1],
                  r=xsq.c(c, t0, t1), s=(c == 0), e=(c == nch - 1):
                  E.matmul(p, o, r, start=s, stop=e),
                  reads=xsq.keys(c, t0, t1), writes=ps2.keys(), tag=tag + "s2")
        mu, rstd, tmp1, tmp2 = sc.row_mu, sc.row_rstd, sc.row_t1, sc.row_t2
        kb.op("S", lambda E, a=mu.r(0, 1, 0, w), b=ps.h[0:1, 0:w]:
              E.mul(a, b, 1.0 / D), reads=ps.keys(), writes=mu.keys(),
              tag=tag + "mu")
        kb.op("S", lambda E, a=tmp1.r(0, 1, 0, w), b=ps2.h[0:1, 0:w]:
              E.mul(a, b, 1.0 / D), reads=ps2.keys(), writes=tmp1.keys(),
              tag=tag + "e2")
        kb.op("V", lambda E, a=tmp2.r(0, 1, 0, w), b=mu.r(0, 1, 0, w):
              E.tensor_mul(a, b, b), reads=mu.keys(), writes=tmp2.keys(),
              tag=tag + "m2")
        kb.op("V", lambda E, a=tmp1.r(0, 1, 0, w), b=tmp1.r(0, 1, 0, w),
              c2=tmp2.r(0, 1, 0, w): E.tensor_sub(a, b, c2),
              reads=tmp2.keys(), writes=tmp1.keys(), tag=tag + "v")
        kb.op("S", lambda E, a=tmp1.r(0, 1, 0, w), b=tmp1.r(0, 1, 0, w):
              E.activation(a, b, AF.Sqrt, bias=C.eps_ap(eps)),
              reads=[("c_eps", 0, 0)], writes=tmp1.keys(), tag=tag + "sd")
        kb.op("V", lambda E, a=rstd.r(0, 1, 0, w), b=tmp1.r(0, 1, 0, w):
              E.reciprocal(a, b), reads=tmp1.keys(), writes=rstd.keys(),
              force=True, tag=tag + "rs")
        pbm, pbr = sc.pb[5], sc.pb[4]
        kb.op("T", lambda E, p=pbm.h[:, 0:w], o=C.ones_f[0:1, 0:128],
              r=mu.r(0, 1, 0, w): E.matmul(p, o, r, start=True, stop=True),
              reads=mu.keys(), writes=pbm.keys(), tag=tag + "bm")
        kb.op("T", lambda E, p=pbr.h[:, 0:w], o=C.ones_f[0:1, 0:128],
              r=rstd.r(0, 1, 0, w): E.matmul(p, o, r, start=True, stop=True),
              reads=rstd.keys(), writes=pbr.keys(), tag=tag + "br")
        for c in range(nch):
            kb.op("V", lambda E, a=sc.tt_f32.h[:, 0:w], b=y.c(c, t0, t1),
                  m=pbm.h[:, 0:w]: E.tensor_sub(a, b, m),
                  reads=y.keys(c, t0, t1) + pbm.keys(),
                  writes=sc.tt_f32.keys(), tag=tag + "sb")
            kb.op("V", lambda E, a=sc.tt_f32.h[:, 0:w],
                  b=sc.tt_f32.h[:, 0:w], m=pbr.h[:, 0:w]: E.tensor_mul(a, b, m),
                  reads=pbr.keys(), writes=sc.tt_f32.keys(), tag=tag + "ml")
            if xres_dram is None:
                kb.op("V", lambda E, a=xres.c(c, t0, t1), b=xres.c(c, t0, t1),
                      m=sc.tt_f32.h[:, 0:w]: E.tensor_add(a, b, m),
                      reads=sc.tt_f32.keys() + xres.keys(c, t0, t1),
                      writes=xres.keys(c, t0, t1), tag=tag + "ad")
            else:
                xin_d, xout_d = xres_dram
                xt = xres[(c * nt + it) % len(xres)]
                kb.dma("Y", xt.h[:, 0:w], xin_d[c * 128:(c + 1) * 128, t0:t1],
                       reads=xt.keys(), writes=xt.keys(), tag=tag + "xl")
                kb.op("V", lambda E, a=xt.h[:, 0:w], b=xt.h[:, 0:w],
                      m=sc.tt_f32.h[:, 0:w]: E.tensor_add(a, b, m),
                      reads=sc.tt_f32.keys() + xt.keys(),
                      writes=xt.keys(), tag=tag + "ad")
                kb.dma("Y", xout_d[c * 128:(c + 1) * 128, t0:t1],
                       xt.h[:, 0:w], reads=xt.keys(), writes=[],
                       tag=tag + "xs")


DM = 1024
NCH = 8
TC = 2048
NH = 16
VW = 65


def _new_nc():
    return bass.Bass("TRN2", target_bir_lowering=False, debug=False)


def _finish(nc, kb, block_cm):
    kb.finalize_into_block(block_cm)
    return nc


def v_colmap(V, nh=16):
    view = V.h[:, :].rearrange("p (tc h w) -> p tc h w", h=nh, w=VW)

    def cm(tc, n0, n1):
        assert n0 % 64 == 0 and (n1 - n0) % 64 == 0
        h0 = n0 // 64
        nheads = (n1 - n0) // 64
        return view[:, tc, h0:h0 + nheads, 0:64]
    return cm


def psum_headview(ps, w, nh):
    return ps.h[:, 0:w].rearrange("p (h w) -> p h w", w=64)


def emit_v_ones(kb, V, ntc=16, nh=16):
    view = V.h[:, :].rearrange("p (tc h w) -> p tc h w", h=nh, w=VW)
    kb.op("G", lambda E, a=view[:, :, :, 64:65]: E.memset(a, 1.0),
          reads=[], writes=[(V.name, tc, 0) for tc in range(ntc)],
          force=True, tag="vones")


def build_l0a_qkv():
    nc = _new_nc()
    xT = nc.dram_tensor("xT", [DM, TC], F32, kind="ExternalInput").ap()
    qwT = nc.dram_tensor("qwT", [DM, DM], F32, kind="ExternalInput").ap()
    kwT = nc.dram_tensor("kwT", [DM, DM], F32, kind="ExternalInput").ap()
    vwT = nc.dram_tensor("vwT", [DM, DM], F32, kind="ExternalInput").ap()
    qb = nc.dram_tensor("qb", [128, NCH], F32, kind="ExternalInput").ap()
    kbb = nc.dram_tensor("kbb", [128, NCH], F32, kind="ExternalInput").ap()
    vb = nc.dram_tensor("vb", [1, DM], F32, kind="ExternalInput").ap()
    q_d = nc.dram_tensor("q_d", [DM, TC], BF16, kind="ExternalOutput").ap()
    k_d = nc.dram_tensor("k_d", [DM, TC], BF16, kind="ExternalOutput").ap()
    v_d = nc.dram_tensor("v_d", [TC, NH * VW], BF16,
                         kind="ExternalOutput").ap()
    kmax_d = nc.dram_tensor("kmax_d", [16, 1], F32, kind="ExternalOutput").ap()

    kb = Kb(nc)
    C = Consts(kb, nc)
    kb.set_setup_barrier()
    sc = Scratch(kb, TC)
    SX = FM(kb, "sx", NCH, TC, BF16)
    XN = FM(kb, "xn", NCH, TC, BF16)
    K = FM(kb, "slabk", NCH, TC, BF16)
    V = TM(kb, "slabv", 16, NH * VW, BF16)
    W = FM(kb, "w", NCH, DM, BF16)
    qbc = kb.sb("qbc", [128, NCH], F32)
    kbc = kb.sb("kbc", [128, NCH], F32)
    vbr = kb.sb("vbr", [1, DM], BF16)
    kb.dma("Y", qbc[:, :], qb, reads=[], writes=[("wrow", 0, 0)])
    kb.dma("Y", kbc[:, :], kbb, reads=[], writes=[("wrow", 0, 0)])
    kb.dma("G", vbr[:, :], vb, reads=[], writes=[("wrow", 0, 0)])

    load_fm(kb, SX, xT, DM, TC, dt_cast=True, tag="lx")
    emit_ln_fm(kb, C, sc, SX, XN, xsq=XN, xb=None, tag="ln")
    load_weight_fm(kb, W, qwT, DM, DM, tag="wq")
    emit_proj_fm(kb, C, sc, SX, XN, W, NCH, NCH, bias_col=qbc[:, :],
                 psum_ids=(0, 1), tag="q")
    kb.alias("w2", [("w", c, i) for c in range(NCH) for i in range(2)])
    W2 = FM(kb, "w2", NCH, DM, BF16, back=W.h)
    load_weight_fm(kb, W2, kwT, DM, DM, tag="wk")
    emit_proj_fm(kb, C, sc, K, XN, W2, NCH, NCH, bias_col=kbc[:, :],
                 psum_ids=(2, 3), tag="k")
    kb.alias("w3", [("w2", c, i) for c in range(NCH) for i in range(2)])
    W3 = FM(kb, "w3", NCH, DM, BF16, back=W.h)
    load_weight_fm(kb, W3, vwT, DM, DM, tag="wv")
    emit_v_ones(kb, V)
    emit_proj_tm(kb, C, sc, V, XN, W3, NCH, DM, bias_row=vbr[:, :],
                 psum_ids=(4, 5), colmap=v_colmap(V), tag="v")
    emit_head_norms(kb, C, sc, K, NCH, 16, sc.hrow_a, sq_scratch=XN, tag="kn")
    kb.op("V", lambda E, a=sc.hcol_a.h[0:16, 0:1], b=sc.hrow_a.h[0:16, 0:TC]:
          E.reduce_max(a, b, axis=AX.X),
          reads=sc.hrow_a.keys(), writes=sc.hcol_a.keys(), force=True,
          tag="kmax")
    store_fm(kb, q_d, SX, DM, TC, tag="oq")
    store_fm(kb, k_d, K, DM, TC, tag="ok")
    store_tm(kb, v_d, V, 16, NH * VW, tag="ov")
    kb.dma("Y", kmax_d, sc.hcol_a.h[0:16, 0:1], reads=sc.hcol_a.keys(),
           writes=[], tag="okm")
    with nc.Block() as block:
        _finish(nc, kb, block)
    return nc


def _stage_kv_loader(kb, k_dram, v_dram, Tk, nchk=NCH):
    nkt = Tk // 128

    def k_load(kb2, slabK):
        for c in range(nchk):
            kb2.dma("Y", slabK.c(c, 0, Tk),
                    k_dram[c * 128:(c + 1) * 128, :],
                    reads=[(slabK.name, c, ck) for ck in range(nkt)],
                    writes=[(slabK.name, c, ck) for ck in range(nkt)],
                    tag="stk")

    def v_load(kb2, slabV):
        for tc in range(nkt):
            kb2.dma("Y", slabV.c(tc), v_dram[tc * 128:(tc + 1) * 128, :],
                    reads=[(slabV.name, tc, 0)],
                    writes=[(slabV.name, tc, 0)], tag="stv")

    return (k_load, v_load, nkt)


def build_l0a_attn(scale_pre_exp=0.125, out_bias=True):
    nc = _new_nc()
    q_d = nc.dram_tensor("q_d", [DM, TC], BF16, kind="ExternalInput").ap()
    ks_d = nc.dram_tensor("ks_d", [DM, TC], BF16, kind="ExternalInput").ap()
    kp_d = nc.dram_tensor("kp_d", [DM, TC], BF16, kind="ExternalInput").ap()
    vs_d = nc.dram_tensor("vs_d", [TC, NH * VW], BF16,
                          kind="ExternalInput").ap()
    vp_d = nc.dram_tensor("vp_d", [TC, NH * VW], BF16,
                          kind="ExternalInput").ap()
    km2 = nc.dram_tensor("km2", [16, 2], F32, kind="ExternalInput").ap()
    owT = nc.dram_tensor("owT", [DM, DM], F32, kind="ExternalInput").ap()
    ob = nc.dram_tensor("ob", [128, NCH], F32, kind="ExternalInput").ap()
    xin = nc.dram_tensor("xin", [DM, TC], F32, kind="ExternalInput").ap()
    xout = nc.dram_tensor("xout", [DM, TC], F32, kind="ExternalOutput").ap()

    kb = Kb(nc)
    C = Consts(kb, nc)
    kb.set_setup_barrier()
    sc = Scratch(kb, TC)
    Q = FM(kb, "slabq", NCH, TC, BF16)
    K = FM(kb, "slabk", NCH, TC, BF16)
    V = TM(kb, "slabv", 16, NH * VW, BF16)
    CA = FM(kb, "ctxa", NCH, TC, BF16)
    W = FM(kb, "w", NCH, DM, BF16)
    obc = kb.sb("obc", [128, NCH], F32)
    km_sb = kb.sb("kmsb", [16, 2], F32)
    kb.dma("Y", obc[:, :], ob, reads=[], writes=[("wrow", 0, 0)])
    kb.dma("Y", km_sb[:, :], km2, reads=[], writes=[("kmrow", 0, 0)])

    load_fm(kb, Q, q_d, DM, TC, tag="lq")
    load_weight_fm(kb, W, owT, DM, DM, tag="wo")
    emit_head_norms(kb, C, sc, Q, NCH, 16, sc.hrow_a, sq_scratch=K, tag="qn")
    kb.op("V", lambda E, a=sc.hcol_a.h[0:16, 0:1], b=km_sb[0:16, 0:2]:
          E.reduce_max(a, b, axis=AX.X),
          reads=[("kmrow", 0, 0)], writes=sc.hcol_a.keys(), force=True,
          tag="gkm")
    kb.op("V", lambda E, a=sc.hrow_a.h[0:16, 0:TC], b=sc.hrow_a.h[0:16, 0:TC],
          s=sc.hcol_a.h[0:16, 0:1]:
          E.tensor_scalar(a, b, s, -1.0, ALU.mult, ALU.mult),
          reads=sc.hrow_a.keys() + sc.hcol_a.keys(),
          writes=sc.hrow_a.keys(), tag="negB")

    kls, vls, nkt = _stage_kv_loader(kb, ks_d, vs_d, TC)
    klp, vlp, _ = _stage_kv_loader(kb, kp_d, vp_d, TC)
    kb.alias("ctxbf", [(K.name, c, ck) for c in range(NCH)
                       for ck in range(16)])
    CTX = FM(kb, "ctxbf", NCH, TC, BF16, back=K.h)
    emit_attn(kb, C, sc, nh=16, q_fm=Q, nchq=NCH, negB_hrow=sc.hrow_a,
              kv_srcs=[(kls, vls, nkt), (klp, vlp, nkt)],
              slabK=K, slabV=V, ctx_acc=CA, ctx_bf=CTX,
              scale_pre_exp=scale_pre_exp, tag="at")
    kb.alias("yslab", [(V.name, tc, 0) for tc in range(16)])
    Y = FM(kb, "yslab", NCH, TC, BF16, back=V.h)
    emit_proj_fm(kb, C, sc, Y, CTX, W, NCH, NCH,
                 bias_col=obc[:, :] if out_bias else None,
                 psum_ids=(0, 1), tag="op")
    kb.alias("xsq2", [(Q.name, c, it) for c in range(NCH) for it in range(4)])
    XSQ = FM(kb, "xsq2", NCH, TC, BF16, back=Q.h)
    xtiles = [TBufSimple(kb, f"xt{i}", [128, TT], F32) for i in range(2)]
    emit_ln_add_residual(kb, C, sc, Y, xtiles, xsq=XSQ, xb=None,
                         xres_dram=(xin, xout), tag="lnr")
    with nc.Block() as block:
        _finish(nc, kb, block)
    return nc


class FMOff:

    def __init__(self, fm, off):
        self.fm, self.off = fm, off
        self.name, self.W = fm.name, fm.W

    def c(self, ch, t0=0, t1=None, p0=0, p1=128):
        return self.fm.c(self.off + ch, t0, t1, p0, p1)

    def keys(self, ch=None, t0=0, t1=None):
        if ch is None:
            return self.fm.keys(None, t0, t1)
        return self.fm.keys(self.off + ch, t0, t1)


def emit_rope_mm(kb, C, sc, q_fm, rope_sb, nch, Tw, psum_ids=(6, 7),
                 tag="rp"):
    nt = ceil_div(Tw, TT)
    for c in range(nch):
        for it in range(nt):
            t0, t1 = it * TT, min(Tw, (it + 1) * TT)
            w = t1 - t0
            ps = sc.pb[psum_ids[(c * nt + it) % len(psum_ids)]]
            kb.op("T", lambda E, p=ps.h[:, 0:w], l=rope_sb.c(c, 0, 128),
                  r=q_fm.c(c, t0, t1):
                  E.matmul(p, l, r, start=True, stop=True),
                  reads=rope_sb.keys(c, 0, 128) + q_fm.keys(c, t0, t1),
                  writes=ps.keys(), tag=tag + "mm")
            kb.op("V", lambda E, a=q_fm.c(c, t0, t1), p=ps.h[:, 0:w]:
                  E.tensor_copy(a, p),
                  reads=ps.keys(), writes=q_fm.keys(c, t0, t1),
                  tag=tag + "ev")


def build_qkv_generic(name, rope_q=False, rope_k=False, wT_names=None):
    nc = _new_nc()
    xT = nc.dram_tensor("xT", [DM, TC], F32, kind="ExternalInput").ap()
    qwT = nc.dram_tensor("qwT", [DM, DM], F32, kind="ExternalInput").ap()
    kwT = nc.dram_tensor("kwT", [DM, DM], F32, kind="ExternalInput").ap()
    vwT = nc.dram_tensor("vwT", [DM, DM], F32, kind="ExternalInput").ap()
    qb = nc.dram_tensor("qb", [128, NCH], F32, kind="ExternalInput").ap()
    kbb = nc.dram_tensor("kbb", [128, NCH], F32, kind="ExternalInput").ap()
    vb = nc.dram_tensor("vb", [1, DM], F32, kind="ExternalInput").ap()
    if rope_q or rope_k:
        ropeM = nc.dram_tensor("ropeM", [2 * NCH * 128, 128], F32,
                               kind="ExternalInput").ap()
    q_d = nc.dram_tensor("q_d", [DM, TC], BF16, kind="ExternalOutput").ap()
    k_d = nc.dram_tensor("k_d", [DM, TC], BF16, kind="ExternalOutput").ap()
    v_d = nc.dram_tensor("v_d", [TC, NH * VW], BF16,
                         kind="ExternalOutput").ap()
    kmax_d = nc.dram_tensor("kmax_d", [16, 1], F32, kind="ExternalOutput").ap()

    kb = Kb(nc)
    C = Consts(kb, nc)
    kb.set_setup_barrier()
    sc = Scratch(kb, TC)
    SX = FM(kb, "sx", NCH, TC, BF16)
    XN = FM(kb, "xn", NCH, TC, BF16)
    K = FM(kb, "slabk", NCH, TC, BF16)
    V = TM(kb, "slabv", 16, NH * VW, BF16)
    W = FM(kb, "w", NCH, DM, BF16)
    qbc = kb.sb("qbc", [128, NCH], F32)
    kbc = kb.sb("kbc", [128, NCH], F32)
    vbr = kb.sb("vbr", [1, DM], BF16)
    kb.dma("Y", qbc[:, :], qb, reads=[], writes=[("wrow", 0, 0)])
    kb.dma("Y", kbc[:, :], kbb, reads=[], writes=[("wrow", 0, 0)])
    kb.dma("G", vbr[:, :], vb, reads=[], writes=[("wrow", 0, 0)])
    if rope_q or rope_k:
        RP = FM(kb, "ropesb", 2 * NCH, 128, BF16)
        load_weight_fm(kb, RP, ropeM, 2 * NCH * 128, 128, tag="rp")

    load_fm(kb, SX, xT, DM, TC, dt_cast=True, tag="lx")
    emit_ln_fm(kb, C, sc, SX, XN, xsq=XN, xb=None, tag="ln")
    load_weight_fm(kb, W, qwT, DM, DM, tag="wq")
    emit_proj_fm(kb, C, sc, SX, XN, W, NCH, NCH, bias_col=qbc[:, :],
                 psum_ids=(0, 1), tag="q")
    if rope_q:
        emit_rope_mm(kb, C, sc, SX, FMOff(RP, 0), NCH, TC, tag="rpq")
    kb.alias("w2", [("w", c, i) for c in range(NCH) for i in range(2)])
    W2 = FM(kb, "w2", NCH, DM, BF16, back=W.h)
    load_weight_fm(kb, W2, kwT, DM, DM, tag="wk")
    emit_proj_fm(kb, C, sc, K, XN, W2, NCH, NCH, bias_col=kbc[:, :],
                 psum_ids=(2, 3), tag="k")
    if rope_k:
        emit_rope_mm(kb, C, sc, K, FMOff(RP, NCH), NCH, TC, tag="rpk")
    kb.alias("w3", [("w2", c, i) for c in range(NCH) for i in range(2)])
    W3 = FM(kb, "w3", NCH, DM, BF16, back=W.h)
    load_weight_fm(kb, W3, vwT, DM, DM, tag="wv")
    emit_v_ones(kb, V)
    emit_proj_tm(kb, C, sc, V, XN, W3, NCH, DM, bias_row=vbr[:, :],
                 psum_ids=(4, 5), colmap=v_colmap(V), tag="v")
    emit_head_norms(kb, C, sc, K, NCH, 16, sc.hrow_a, sq_scratch=XN, tag="kn")
    kb.op("V", lambda E, a=sc.hcol_a.h[0:16, 0:1], b=sc.hrow_a.h[0:16, 0:TC]:
          E.reduce_max(a, b, axis=AX.X),
          reads=sc.hrow_a.keys(), writes=sc.hcol_a.keys(), force=True,
          tag="kmax")
    store_fm(kb, q_d, SX, DM, TC, tag="oq")
    store_fm(kb, k_d, K, DM, TC, tag="ok")
    store_tm(kb, v_d, V, 16, NH * VW, tag="ov")
    kb.dma("Y", kmax_d, sc.hcol_a.h[0:16, 0:1], reads=sc.hcol_a.keys(),
           writes=[], tag="okm")
    with nc.Block() as block:
        _finish(nc, kb, block)
    return nc


def build_l3a_qkv():
    nc = _new_nc()
    xT = nc.dram_tensor("xT", [DM, TC], F32, kind="ExternalInput").ap()
    qwT = nc.dram_tensor("qwT", [DM, DM], F32, kind="ExternalInput").ap()
    kwT = nc.dram_tensor("kwT", [DM, DM], F32, kind="ExternalInput").ap()
    vwT = nc.dram_tensor("vwT", [DM, DM], F32, kind="ExternalInput").ap()
    qb = nc.dram_tensor("qb", [128, NCH], F32, kind="ExternalInput").ap()
    kbb = nc.dram_tensor("kbb", [128, NCH], F32, kind="ExternalInput").ap()
    vb = nc.dram_tensor("vb", [1, DM], F32, kind="ExternalInput").ap()
    q_d = nc.dram_tensor("q_d", [DM, TC], BF16, kind="ExternalOutput").ap()
    k_d = nc.dram_tensor("k_d", [DM, TC], BF16, kind="ExternalOutput").ap()
    v_d = nc.dram_tensor("v_d", [TC, DM], BF16, kind="ExternalOutput").ap()
    kmax_d = nc.dram_tensor("kmax_d", [1, 1], F32, kind="ExternalOutput").ap()

    kb = Kb(nc)
    C = Consts(kb, nc)
    kb.set_setup_barrier()
    sc = Scratch(kb, TC)
    SX = FM(kb, "sx", NCH, TC, BF16)
    XN = FM(kb, "xn", NCH, TC, BF16)
    K = FM(kb, "slabk", NCH, TC, BF16)
    V = TM(kb, "slabv", 16, DM, BF16)
    W = FM(kb, "w", NCH, DM, BF16)
    qbc = kb.sb("qbc", [128, NCH], F32)
    kbc = kb.sb("kbc", [128, NCH], F32)
    vbr = kb.sb("vbr", [1, DM], BF16)
    kb.dma("Y", qbc[:, :], qb, reads=[], writes=[("wrow", 0, 0)])
    kb.dma("Y", kbc[:, :], kbb, reads=[], writes=[("wrow", 0, 0)])
    kb.dma("G", vbr[:, :], vb, reads=[], writes=[("wrow", 0, 0)])

    load_fm(kb, SX, xT, DM, TC, dt_cast=True, tag="lx")
    emit_ln_fm(kb, C, sc, SX, XN, xsq=XN, xb=None, tag="ln")
    load_weight_fm(kb, W, qwT, DM, DM, tag="wq")
    emit_proj_fm(kb, C, sc, SX, XN, W, NCH, NCH, bias_col=qbc[:, :],
                 psum_ids=(0, 1), tag="q")
    kb.alias("w2", [("w", c, i) for c in range(NCH) for i in range(2)])
    W2 = FM(kb, "w2", NCH, DM, BF16, back=W.h)
    load_weight_fm(kb, W2, kwT, DM, DM, tag="wk")
    emit_proj_fm(kb, C, sc, K, XN, W2, NCH, NCH, bias_col=kbc[:, :],
                 psum_ids=(2, 3), tag="k")
    kb.alias("w3", [("w2", c, i) for c in range(NCH) for i in range(2)])
    W3 = FM(kb, "w3", NCH, DM, BF16, back=W.h)
    load_weight_fm(kb, W3, vwT, DM, DM, tag="wv")
    emit_proj_tm(kb, C, sc, V, XN, W3, NCH, DM, bias_row=vbr[:, :],
                 psum_ids=(4, 5), tag="v")
    emit_head_norms(kb, C, sc, K, NCH, 1, sc.hrow_a, sq_scratch=XN, tag="kn")
    kb.op("V", lambda E, a=sc.hcol_a.h[0:1, 0:1], b=sc.hrow_a.h[0:1, 0:TC]:
          E.reduce_max(a, b, axis=AX.X),
          reads=sc.hrow_a.keys(), writes=sc.hcol_a.keys(), force=True,
          tag="kmax")
    store_fm(kb, q_d, SX, DM, TC, tag="oq")
    store_fm(kb, k_d, K, DM, TC, tag="ok")
    store_tm(kb, v_d, V, 16, DM, tag="ov")
    kb.dma("Y", kmax_d, sc.hcol_a.h[0:1, 0:1], reads=sc.hcol_a.keys(),
           writes=[], tag="okm")
    with nc.Block() as block:
        _finish(nc, kb, block)
    return nc


def emit_attn_bighead(kb, C, sc, *, q_fm, negB_hrow, kv_srcs, slabK, slabV,
                      PT, ctx_acc, ctx_bf, Tw=None, scale_pre_exp=None,
                      mask_fn=None, tag="bh"):
    Tw = sc.Tw if Tw is None else Tw
    nt = ceil_div(Tw, TT)
    nch = q_fm.nch
    scl = 1.0 if scale_pre_exp is None else scale_pre_exp
    first = [True] * nt
    for si, (k_load, v_load, nkt) in enumerate(kv_srcs):
        k_load(kb, slabK)
        v_load(kb, slabV)
        for it in range(nt):
            t0, t1 = it * TT, min(Tw, (it + 1) * TT)
            w = t1 - t0
            pdd = sc.pb[4 + it]
            for ck in range(nkt):
                pss = sc.pb[2 + (ck % 2)]
                for c in range(nch):
                    kb.op("T", lambda E, p=pss.h[:, 0:w],
                          l=slabK.c(c, ck * 128, (ck + 1) * 128),
                          r=q_fm.c(c, t0, t1), s=(c == 0), e=False:
                          E.matmul(p, l, r, start=s, stop=e),
                          reads=[(slabK.name, c, ck)] + q_fm.keys(c, t0, t1),
                          writes=pss.keys(), tag=tag + "s")
                kb.op("T", lambda E, p=pss.h[:, 0:w],
                      l=C.sel128[:, 0:128], r=negB_hrow.h[0:16, t0:t1]:
                      E.matmul(p, l, r, start=False, stop=True),
                      reads=negB_hrow.keys(), writes=pss.keys(), tag=tag + "b")
                kb.op("S", lambda E, a=PT.c(ck, 0, w), p=pss.h[:, 0:w],
                      s2=scl: E.activation(a, p, AF.Exp, scale=s2),
                      reads=pss.keys(), writes=PT.keys(ck, 0, w),
                      tag=tag + "e")
                if mask_fn is not None:
                    mk = mask_fn(si, it, ck)
                    if mk is not None:
                        map_, mkeys = mk
                        kb.op("V", lambda E, a=PT.c(ck, 0, w), m=map_:
                              E.tensor_mul(a, a, m),
                              reads=mkeys, writes=PT.keys(ck, 0, w),
                              tag=tag + "m")
                kb.op("T", lambda E, p=pdd.h[0:1, 0:w],
                      l=C.ones_bf[0:128, 0:1], r=PT.c(ck, 0, w),
                      s=(ck == 0 and si == 0):
                      E.matmul(p, l, r, start=s, stop=True,
                               skip_group_check=True),
                      reads=PT.keys(ck, 0, w), writes=pdd.keys(),
                      tag=tag + "dm")
            for m in range(nch):
                ppv = sc.pb[m % 2]
                for ck in range(nkt):
                    kb.op("T", lambda E, p=ppv.h[:, 0:w],
                          l=slabV.c(ck, m * 128, (m + 1) * 128),
                          r=PT.c(ck, 0, w), s=(ck == 0), e=(ck == nkt - 1):
                          E.matmul(p, l, r, start=s, stop=e),
                          reads=[(slabV.name, ck, 0)] + PT.keys(ck, 0, w),
                          writes=ppv.keys(), tag=tag + "pv")
                if first[it]:
                    kb.op("V", lambda E, a=ctx_acc.c(m, t0, t1),
                          p=ppv.h[:, 0:w]: E.tensor_copy(a, p),
                          reads=ppv.keys(),
                          writes=[(ctx_acc.name, m, it, 0),
                                  (ctx_acc.name, m, it, 64)], tag=tag + "c0")
                else:
                    kb.op("V", lambda E, a=ctx_acc.c(m, t0, t1),
                          p=ppv.h[:, 0:w]: E.tensor_add(a, a, p),
                          reads=ppv.keys() + [(ctx_acc.name, m, it, 0)],
                          writes=[(ctx_acc.name, m, it, 0),
                                  (ctx_acc.name, m, it, 64)], tag=tag + "c1")
            first[it] = False
    for it in range(nt):
        t0, t1 = it * TT, min(Tw, (it + 1) * TT)
        w = t1 - t0
        kb.op("V", lambda E, a=sc.d_hrow.h[0:1, t0:t1],
              p=sc.pb[4 + it].h[0:1, 0:w]: E.tensor_copy(a, p),
              reads=sc.pb[4 + it].keys(), writes=[(sc.d_hrow.name, 0, it)],
              tag=tag + "dx")
    dkeys = [(sc.d_hrow.name, 0, i) for i in range(nt)]
    kb.op("V", lambda E, a=sc.d_hrow.h[0:1, 0:Tw]: E.reciprocal(a, a),
          reads=dkeys, writes=dkeys, force=True, tag=tag + "rc")
    for m in range(nch):
        for it in range(nt):
            t0, t1 = it * TT, min(Tw, (it + 1) * TT)
            w = t1 - t0
            pbc = sc.pb[4 + (it % 2)]
            kb.op("T", lambda E, p=pbc.h[:, 0:w], o=C.ones_f[0:1, 0:128],
                  r=sc.d_hrow.h[0:1, t0:t1]:
                  E.matmul(p, o, r, start=True, stop=True),
                  reads=dkeys, writes=pbc.keys(), tag=tag + "bc")
            kb.op("V", lambda E, a=ctx_bf.c(m, t0, t1),
                  b=ctx_acc.c(m, t0, t1), p=pbc.h[:, 0:w]:
                  E.tensor_mul(a, b, p),
                  reads=pbc.keys() + [(ctx_acc.name, m, it, 0)],
                  writes=ctx_bf.keys(m, t0, t1), tag=tag + "dv")


def build_l3a_attn(core_h):
    nc = _new_nc()
    q_d = nc.dram_tensor("q_d", [DM, TC], BF16, kind="ExternalInput").ap()
    ks_d = nc.dram_tensor("ks_d", [DM, TC], BF16, kind="ExternalInput").ap()
    kp_d = nc.dram_tensor("kp_d", [DM, TC], BF16, kind="ExternalInput").ap()
    vs_d = nc.dram_tensor("vs_d", [TC, DM], BF16, kind="ExternalInput").ap()
    vp_d = nc.dram_tensor("vp_d", [TC, DM], BF16, kind="ExternalInput").ap()
    km2 = nc.dram_tensor("km2", [1, 2], F32, kind="ExternalInput").ap()
    eyem = nc.dram_tensor("eyem", [4 * 128, TT], F32,
                          kind="ExternalInput").ap()
    xin = nc.dram_tensor("xin", [DM, TC], F32, kind="ExternalInput").ap()
    xout = nc.dram_tensor("xout", [DM, TC], F32, kind="ExternalOutput").ap()

    kb = Kb(nc)
    C = Consts(kb, nc)
    kb.set_setup_barrier()
    sc = Scratch(kb, TC)
    Q = FM(kb, "slabq", NCH, TC, BF16)
    K = FM(kb, "slabk", NCH, TC, BF16)
    V = TM(kb, "slabv", 16, DM, BF16)
    CA = FM(kb, "ctxa", NCH, TC, BF16)
    PT = FM(kb, "ptslab", 16, TT, BF16)
    EY = FM(kb, "eyesb", 4, TT, BF16)
    km_sb = kb.sb("kmsb", [1, 2], F32)
    kb.dma("Y", km_sb[:, :], km2, reads=[], writes=[("kmrow", 0, 0)])
    load_fm(kb, EY, eyem, 4 * 128, TT, dt_cast=True, tag="ey")

    load_fm(kb, Q, q_d, DM, TC, tag="lq")
    kb.op("V", lambda E, a=sc.hrow_a.h[0:16, 0:TC]: E.memset(a, 0.0),
          reads=[], writes=sc.hrow_a.keys(), force=True, tag="z")
    emit_head_norms(kb, C, sc, Q, NCH, 1, sc.hrow_a, sq_scratch=K, tag="qn")
    kb.op("V", lambda E, a=sc.hcol_a.h[0:1, 0:1], b=km_sb[0:1, 0:2]:
          E.reduce_max(a, b, axis=AX.X),
          reads=[("kmrow", 0, 0)], writes=sc.hcol_a.keys(), force=True,
          tag="gkm")
    kb.op("V", lambda E, a=sc.hrow_a.h[0:1, 0:TC], b=sc.hrow_a.h[0:1, 0:TC],
          s=sc.hcol_a.h[0:1, 0:1]:
          E.tensor_scalar(a, b, s, -1.0, ALU.mult, ALU.mult),
          reads=sc.hrow_a.keys() + sc.hcol_a.keys(),
          writes=sc.hrow_a.keys(), tag="negB")
    kls, vls, nkt = _stage_kv_loader(kb, ks_d, vs_d, TC)
    klp, vlp, _ = _stage_kv_loader(kb, kp_d, vp_d, TC)

    def mask_fn(si, it, ck):
        if si != 0:
            return None
        off = ck * 128 - it * TT
        if off < -96 or off >= TT:
            return None
        oi = off // 128 + 1
        if off % 128 != 0 or off < 0 or off // 128 >= 4:
            return None
        c = off // 128
        return (EY.c(c, 0, TT), EY.keys(c))

    emit_attn_bighead(kb, C, sc, q_fm=Q, negB_hrow=sc.hrow_a,
                      kv_srcs=[(kls, vls, nkt), (klp, vlp, nkt)],
                      slabK=K, slabV=V, PT=PT, ctx_acc=CA, ctx_bf=CA,
                      scale_pre_exp=1.0 / DM, mask_fn=mask_fn, tag="bh")
    kb.alias("xsq2", [(Q.name, c, it) for c in range(NCH) for it in range(4)])
    XSQ = FM(kb, "xsq2", NCH, TC, BF16, back=Q.h)
    xtiles = [TBufSimple(kb, f"xt{i}", [128, TT], F32) for i in range(2)]
    emit_ln_add_residual(kb, C, sc, CA, xtiles, xsq=XSQ, xb=None,
                         xres_dram=(xin, xout), tag="lnr")
    with nc.Block() as block:
        _finish(nc, kb, block)
    return nc


def build_l0f():
    nc = _new_nc()
    xT = nc.dram_tensor("xT", [DM, TC], F32, kind="ExternalInput").ap()
    wr_d = nc.dram_tensor("wr", [DM, DM], F32, kind="ExternalInput").ap()
    wi_d = nc.dram_tensor("wi", [DM, DM], F32, kind="ExternalInput").ap()
    bias_d = nc.dram_tensor("fbias", [128, NCH], F32,
                            kind="ExternalInput").ap()
    C_d = nc.dram_tensor("trigC", [DM, DM], BF16, kind="ExternalInput").ap()
    S_d = nc.dram_tensor("trigS", [DM, DM], BF16, kind="ExternalInput").ap()
    nS_d = nc.dram_tensor("trigNS", [DM, DM], BF16, kind="ExternalInput").ap()
    Cn_d = nc.dram_tensor("trigCn", [DM, DM], BF16, kind="ExternalInput").ap()
    Sn_d = nc.dram_tensor("trigSn", [DM, DM], BF16, kind="ExternalInput").ap()
    xout = nc.dram_tensor("xout", [DM, TC], F32, kind="ExternalOutput").ap()

    kb = Kb(nc)
    C = Consts(kb, nc)
    kb.set_setup_barrier()
    sc = Scratch(kb, TC)
    SX = FM(kb, "sx", NCH, TC, BF16)
    XN = FM(kb, "xn", NCH, TC, BF16)
    T1 = FM(kb, "t1", NCH, DM, BF16)
    T2 = FM(kb, "t2", NCH, DM, BF16)
    PT_ = FM(kb, "pslab", NCH, DM, BF16)
    QT = FM(kb, "qslab", NCH, DM, BF16)
    M = FM(kb, "mslab", NCH, DM, BF16)
    WR = FM(kb, "wr_sb", NCH, DM, BF16, back=SX.h)
    WI = FM(kb, "wi_sb", NCH, DM, BF16, back=XN.h)
    bcol = kb.sb("bcol", [128, NCH], F32)
    kb.dma("Y", bcol[:, :], bias_d, reads=[], writes=[("wrow", 0, 0)])
    load_weight_fm(kb, WR, wr_d, DM, DM, tag="wr")
    load_weight_fm(kb, WI, wi_d, DM, DM, tag="wi")
    load_fm(kb, T1, C_d, DM, DM, eng="Y", tag="tc")
    load_fm(kb, T2, S_d, DM, DM, eng="Y", tag="ts")

    def mm2(out, lhs1, rhs1, lhs2, rhs2, tag):
        for m in range(NCH):
            for jt in range(2):
                j0, j1 = jt * TT, (jt + 1) * TT
                ps = sc.pb[(m * 2 + jt) % 2]
                for c in range(NCH):
                    kb.op("T", lambda E, p=ps.h[:, 0:TT],
                          l=lhs1.c(c, m * 128, (m + 1) * 128),
                          r=rhs1.c(c, j0, j1), s=(c == 0):
                          E.matmul(p, l, r, start=s, stop=False),
                          reads=lhs1.keys(c) + rhs1.keys(c),
                          writes=ps.keys(), tag=tag + "a")
                for c in range(NCH):
                    kb.op("T", lambda E, p=ps.h[:, 0:TT],
                          l=lhs2.c(c, m * 128, (m + 1) * 128),
                          r=rhs2.c(c, j0, j1), e=(c == NCH - 1):
                          E.matmul(p, l, r, start=False, stop=e),
                          reads=lhs2.keys(c) + rhs2.keys(c),
                          writes=ps.keys(), tag=tag + "b")
                kb.op("V", lambda E, a=out.c(m, j0, j1), p=ps.h[:, 0:TT]:
                      E.tensor_copy(a, p),
                      reads=ps.keys(), writes=out.keys(m, j0, j1),
                      tag=tag + "ev")

    mm2(PT_, WR, T1, WI, T2, "p")
    kb.alias("negs", [("t2", c, i) for c in range(NCH) for i in range(2)])
    NS = FM(kb, "negs", NCH, DM, BF16, back=T2.h)
    load_fm(kb, NS, nS_d, DM, DM, eng="Y", tag="tns")
    mm2(QT, WI, T1, WR, NS, "q")
    kb.alias("cn", [("t1", c, i) for c in range(NCH) for i in range(2)])
    CN = FM(kb, "cn", NCH, DM, BF16, back=T1.h)
    load_fm(kb, CN, Cn_d, DM, DM, eng="Y", tag="tcn")
    kb.alias("sn", [("negs", c, i) for c in range(NCH) for i in range(2)])
    SN = FM(kb, "sn", NCH, DM, BF16, back=T2.h)
    load_fm(kb, SN, Sn_d, DM, DM, eng="Y", tag="tsn")
    mm2(M, PT_, CN, QT, SN, "m")
    kb.alias("sx", [("wr_sb", c, i) for c in range(NCH) for i in range(2)])
    kb.alias("xn", [("wi_sb", c, i) for c in range(NCH) for i in range(2)])
    load_fm(kb, SX, xT, DM, TC, dt_cast=True, tag="lx")
    emit_ln_fm(kb, C, sc, SX, XN, xsq=XN, xb=None, tag="ln")
    kb.alias("yslab", [("sx", c, i) for c in range(NCH) for i in range(4)])
    Y = FM(kb, "yslab", NCH, TC, BF16, back=SX.h)
    emit_proj_fm(kb, C, sc, Y, XN, M, NCH, NCH, bias_col=bcol[:, :],
                 psum_ids=(2, 3), tag="y")
    kb.alias("xsq2", [("xn", c, i) for c in range(NCH) for i in range(4)])
    XSQ = FM(kb, "xsq2", NCH, TC, BF16, back=XN.h)
    xtiles = [TBufSimple(kb, f"xt{i}", [128, TT], F32) for i in range(2)]
    emit_ln_add_residual(kb, C, sc, Y, xtiles, xsq=XSQ, xb=None,
                         xres_dram=(xT, xout), tag="lnr")
    with nc.Block() as block:
        _finish(nc, kb, block)
    return nc


def l0f_host_consts():
    i = np.arange(DM)
    ang = 2.0 * np.pi * np.outer(i, i) / DM
    ij = (np.outer(i, i) % DM).astype(np.float64)
    ang = 2.0 * np.pi * ij / DM
    Cm = np.cos(ang); Sm = np.sin(ang)
    bf = lambda a: a.astype(BF)
    return {"trigC": bf(Cm), "trigS": bf(Sm), "trigNS": bf(-Sm),
            "trigCn": bf(Cm / DM), "trigSn": bf(-Sm / DM)}


def emit_rmsnorm_fm(kb, C, sc, z, zn, xsq, wcol, nch=8, Tw=None, eps=1e-6,
                    tag="rms"):
    Tw = sc.Tw if Tw is None else Tw
    D = nch * 128
    for c in range(nch):
        kb.op("S", lambda E, a=xsq.c(c, 0, Tw), b=z.c(c, 0, Tw):
              E.activation(a, b, AF.Square),
              reads=z.keys(c, 0, Tw), writes=xsq.keys(c, 0, Tw), tag=tag + "q")
    nt = ceil_div(Tw, TT)
    for it in range(nt):
        t0, t1 = it * TT, min(Tw, (it + 1) * TT)
        w = t1 - t0
        ps = sc.pb[6]
        for c in range(nch):
            kb.op("T", lambda E, p=ps.h[0:1, 0:w], o=C.ones_bf[0:128, 0:1],
                  r=xsq.c(c, t0, t1), s=(c == 0), e=(c == nch - 1):
                  E.matmul(p, o, r, start=s, stop=e),
                  reads=xsq.keys(c, t0, t1), writes=ps.keys(), tag=tag + "s")
        mu, tmp1 = sc.row_mu, sc.row_t1
        kb.op("S", lambda E, a=tmp1.r(0, 1, 0, w), b=ps.h[0:1, 0:w]:
              E.activation(a, b, AF.Sqrt, bias=C.eps_ap(eps), scale=1.0 / D),
              reads=ps.keys() + [("c_eps", 0, 0)], writes=tmp1.keys(),
              tag=tag + "sd")
        kb.op("V", lambda E, a=mu.r(0, 1, 0, w), b=tmp1.r(0, 1, 0, w):
              E.reciprocal(a, b), reads=tmp1.keys(), writes=mu.keys(),
              force=True, tag=tag + "rc")
        pbr = sc.pb[7]
        kb.op("T", lambda E, p=pbr.h[:, 0:w], o=C.ones_f[0:1, 0:128],
              r=mu.r(0, 1, 0, w): E.matmul(p, o, r, start=True, stop=True),
              reads=mu.keys(), writes=pbr.keys(), tag=tag + "bc")
        for c in range(nch):
            kb.op("V", lambda E, a=sc.tt_f32.h[:, 0:w], b=z.c(c, t0, t1),
                  m=pbr.h[:, 0:w]: E.tensor_mul(a, b, m),
                  reads=z.keys(c, t0, t1) + pbr.keys(),
                  writes=sc.tt_f32.keys(), tag=tag + "m1")
            kb.op("V", lambda E, a=zn.c(c, t0, t1), b=sc.tt_f32.h[:, 0:w],
                  g=wcol[:, c:c + 1]:
                  E.tensor_scalar_mul(a, b, g),
                  reads=sc.tt_f32.keys() + [("wrow", 0, 0)],
                  writes=zn.keys(c, t0, t1), tag=tag + "m2")


def emit_l3f_combine(kb, C, sc, zn, wrt_sb, comb_tm, loop, dbg_snaps=None,
                     tag="cmb"):
    ntc = 16
    SA, EA = sc.cmb_sa, sc.cmb_ea
    sak, eak = [("cmbsa", 0, 0)], [("cmbea", 0, 0)]
    A = sc.cmb_a.h
    ak = [("cmba", 0, 0)]
    for tc in range(ntc):
        ps = sc.pb[5]
        for c in range(NCH):
            kb.op("T", lambda E, p=ps.h[:, 0:12],
                  l=zn.c(c, tc * 128, (tc + 1) * 128),
                  r=wrt_sb[:, c * 12:(c + 1) * 12], s=(c == 0),
                  e=(c == NCH - 1): E.matmul(p, l, r, start=s, stop=e),
                  reads=zn.keys(c) + [("wrt", 0, 0)], writes=ps.keys(),
                  tag=tag + "mm")
        ck = ps.keys()
        o = tc * 12
        kb.op("V", lambda E, a=A[:, 0:1], b=ps.h[:, 0:4]:
              E.reduce_max(a, b, axis=AX.X, negate=True), reads=ck, writes=ak,
              force=True, tag=tag + "nm")
        kb.op("V", lambda E, a=SA.h[:, o:o + 4], b=ps.h[:, 0:4], s2=A[:, 0:1]:
              E.tensor_scalar(a, b, s2, None, ALU.add),
              reads=ck + ak, writes=sak, force=True, tag=tag + "sh")
        for g in range(4):
            kb.op("V", lambda E, a=A[:, 1 + g:2 + g],
                  b=ps.h[:, 4 + 2 * g:6 + 2 * g], g=g:
                  E.reduce_max(a, b, axis=AX.X, negate=True),
                  reads=ck, writes=ak, force=True, tag=tag + "nmg")
            kb.op("V", lambda E, a=SA.h[:, o + 4 + 2 * g:o + 6 + 2 * g],
                  b=ps.h[:, 4 + 2 * g:6 + 2 * g], s2=A[:, 1 + g:2 + g], g=g:
                  E.tensor_scalar(a, b, s2, None, ALU.add),
                  reads=ck + ak, writes=sak, force=True, tag=tag + "shg")
    kb.op("S", lambda E, a=EA.h[:, 0:192], b=SA.h[:, 0:192]:
          E.activation(a, b, AF.Exp), reads=sak, writes=eak, tag=tag + "exp")
    for tc in range(ntc):
        o = tc * 12
        V = lambda fn, rd, wr: kb.op("V", fn, reads=rd, writes=wr, tag=tag,
                                     force=True)
        V(lambda E, ea=EA.h[:, o:o + 4]: E.reduce_sum(A[:, 4:5], ea,
                                                      axis=AX.X), eak, ak)
        V(lambda E: E.reciprocal(A[:, 5:6], A[:, 4:5]), ak, ak)
        V(lambda E, ea=EA.h[:, o:o + 4]: E.tensor_scalar_mul(A[:, 0:4], ea,
                                                             A[:, 5:6]),
          eak + ak, ak)
        V(lambda E: E.reduce_max(A[:, 14:15], A[:, 0:4], axis=AX.X), ak, ak)
        V(lambda E: E.tensor_scalar(A[:, 6:10], A[:, 0:4], A[:, 14:15], None,
                                    ALU.is_ge), ak, ak)
        V(lambda E: E.tensor_scalar(A[:, 15:16], A[:, 6:7], -1.0, 1.0,
                                    ALU.mult, ALU.add), ak, ak)
        for j in range(1, 4):
            V(lambda E, j=j: E.tensor_mul(A[:, 6 + j:7 + j], A[:, 6 + j:7 + j],
                                          A[:, 15:16]), ak, ak)
            if j < 3:
                V(lambda E, j=j: E.tensor_scalar(A[:, 16:17], A[:, 6 + j:7 + j],
                                                 -1.0, 1.0, ALU.mult, ALU.add),
                  ak, ak)
                V(lambda E: E.tensor_mul(A[:, 15:16], A[:, 15:16],
                                         A[:, 16:17]), ak, ak)
        V(lambda E: E.tensor_scalar(A[:, 17:21], A[:, 6:10], -1.0, 1.0,
                                    ALU.mult, ALU.add), ak, ak)
        V(lambda E: E.tensor_mul(A[:, 17:21], A[:, 17:21], A[:, 0:4]), ak, ak)
        V(lambda E: E.reduce_max(A[:, 21:22], A[:, 17:21], axis=AX.X), ak, ak)
        V(lambda E: E.tensor_scalar(A[:, 10:14], A[:, 17:21], A[:, 21:22],
                                    None, ALU.is_ge), ak, ak)
        V(lambda E: E.tensor_scalar(A[:, 15:16], A[:, 10:11], -1.0, 1.0,
                                    ALU.mult, ALU.add), ak, ak)
        for j in range(1, 4):
            V(lambda E, j=j: E.tensor_mul(A[:, 10 + j:11 + j],
                                          A[:, 10 + j:11 + j], A[:, 15:16]),
              ak, ak)
            if j < 3:
                V(lambda E, j=j: E.tensor_scalar(A[:, 16:17],
                                                 A[:, 10 + j:11 + j], -1.0,
                                                 1.0, ALU.mult, ALU.add),
                  ak, ak)
                V(lambda E: E.tensor_mul(A[:, 15:16], A[:, 15:16],
                                         A[:, 16:17]), ak, ak)
        V(lambda E: E.tensor_scalar_mul(A[:, 22:26], A[:, 6:10], A[:, 14:15]),
          ak, ak)
        V(lambda E: E.tensor_scalar_mul(A[:, 10:14], A[:, 10:14],
                                        A[:, 21:22]), ak, ak)
        V(lambda E: E.tensor_add(A[:, 22:26], A[:, 22:26], A[:, 10:14]),
          ak, ak)
        V(lambda E: E.tensor_tensor(A[:, 26:27], A[:, 14:15], A[:, 21:22],
                                    ALU.add), ak, ak)
        V(lambda E: E.tensor_scalar(A[:, 26:27], A[:, 26:27],
                                    C.eps_ap(1e-9, 128), None, ALU.add),
          ak + [("c_eps", 0, 0)], ak)
        V(lambda E: E.reciprocal(A[:, 26:27], A[:, 26:27]), ak, ak)
        V(lambda E: E.tensor_scalar_mul(A[:, 22:26], A[:, 22:26],
                                        A[:, 26:27]), ak, ak)
        for g in range(4):
            og = o + 4 + 2 * g
            V(lambda E, og=og: E.reduce_sum(A[:, 27:28], EA.h[:, og:og + 2],
                                            axis=AX.X), eak, ak)
            V(lambda E: E.reciprocal(A[:, 28:29], A[:, 27:28]), ak, ak)
            V(lambda E: E.tensor_scalar(A[:, 29:30], A[:, 28:29],
                                        C.eps_ap(1e-9, 128), None, ALU.add),
              ak + [("c_eps", 0, 0)], ak)
            V(lambda E: E.reciprocal(A[:, 29:30], A[:, 29:30]), ak, ak)
            V(lambda E: E.tensor_mul(A[:, 28:29], A[:, 28:29], A[:, 29:30]),
              ak, ak)
            V(lambda E, og=og: E.tensor_scalar(A[:, 30:32], SA.h[:, og:og + 2],
                                               0.0, None, ALU.is_ge),
              sak, ak)
            V(lambda E: E.tensor_scalar(A[:, 16:17], A[:, 30:31], -1.0, 1.0,
                                        ALU.mult, ALU.add), ak, ak)
            V(lambda E: E.tensor_mul(A[:, 31:32], A[:, 31:32], A[:, 16:17]),
              ak, ak)
            V(lambda E, g=g: E.tensor_mul(A[:, 28:29], A[:, 28:29],
                                          A[:, 22 + g:23 + g]), ak, ak)
            V(lambda E, g=g, tc=tc: E.tensor_scalar_mul(
                comb_tm.h[:, tc * 8 + 2 * g:tc * 8 + 2 * g + 2],
                A[:, 30:32], A[:, 28:29]),
              ak, [("comb", tc, 0)], )


def build_l3f(dbg=False):
    nc = _new_nc()
    xT = nc.dram_tensor("xT", [DM, TC], F32, kind="ExternalInput").ap()
    w1T_d = nc.dram_tensor("w1T", [8 * DM, 2048], F32,
                           kind="ExternalInput").ap()
    w2T_d = nc.dram_tensor("w2T", [8 * 2048, DM], F32,
                           kind="ExternalInput").ap()
    wrt_d = nc.dram_tensor("wrt", [DM, 12], F32, kind="ExternalInput").ap()
    nw_d = nc.dram_tensor("nw", [128, NCH], F32, kind="ExternalInput").ap()
    xout = nc.dram_tensor("xout", [DM, TC], F32, kind="ExternalOutput").ap()
    if dbg:
        comb_d = nc.dram_tensor("comb_d", [TC, 8], F32,
                                kind="ExternalOutput").ap()
        zn_d = nc.dram_tensor("zn_d", [DM, TC], BF16,
                              kind="ExternalOutput").ap()
        z1_d = nc.dram_tensor("z1_d", [DM, TC], BF16,
                              kind="ExternalOutput").ap()
        mix_d = nc.dram_tensor("mix_d", [DM, TC], BF16,
                               kind="ExternalOutput").ap()
        sa_d = nc.dram_tensor("sa_d", [128, 192], F32,
                              kind="ExternalOutput").ap()
        ea_d = nc.dram_tensor("ea_d", [128, 192], F32,
                              kind="ExternalOutput").ap()

    kb = Kb(nc)
    C = Consts(kb, nc)
    kb.set_setup_barrier()
    sc = Scratch(kb, TC, attn=False)
    
    Z = FM(kb, "zslab", NCH, TC, BF16)
    ZN = FM(kb, "znslab", NCH, TC, BF16)
    SX = FM(kb, "sx", NCH, TC, BF16)
    MIX = FM(kb, "mix", NCH, TC, BF16)
    W1 = [FM(kb, "w1s0", NCH, 1024, BF16)]
    W2 = FM(kb, "w2s", NCH, 1024, BF16)
    comb = TM(kb, "comb", 16, 8, F32)
    wrt_sb = kb.sb("wrtsb", [128, 8 * 12], BF16)
    nwc = kb.sb("nwc", [128, NCH], F32)
    kb.dma("Y", nwc[:, :], nw_d, reads=[], writes=[("wrow", 0, 0)])
    for c in range(NCH):
        kb.dma("G", wrt_sb[:, c * 12:(c + 1) * 12],
               wrt_d[c * 128:(c + 1) * 128, :], reads=[],
               writes=[("wrt", 0, 0)], tag="wrt")

    load_fm(kb, SX, xT, DM, TC, dt_cast=True, tag="lx")
    emit_ln_fm(kb, C, sc, SX, Z, xsq=Z, xb=None, tag="ln0")

    HH = FM(kb, "hh", NCH, TC, BF16, back=SX.h)
    kb.alias("hh", [("sx", c, i) for c in range(NCH) for i in range(4)])

    for loop in range(2):
        emit_rmsnorm_fm(kb, C, sc, Z, ZN, xsq=MIX, wcol=nwc[:, :],
                        eps=1e-6, tag=f"rm{loop}")
        emit_l3f_combine(kb, C, sc, ZN, wrt_sb, comb, loop, tag=f"cb{loop}")
        for e in range(8):
            crow = sc.row_t2
            ps = sc.pb[5]
            for it4 in range(4):
                for tcc in range(4):
                    tc = it4 * 4 + tcc
                    kb.op("T", lambda E, p=ps.h[0:1, tcc * 128:(tcc + 1) * 128],
                          l=comb.h[:, tc * 8 + e:tc * 8 + e + 1],
                          r=C.eye_f[0:128, 0:128]:
                          E.matmul(p, l, r, start=True, stop=True,
                                   skip_group_check=True),
                          reads=[("comb", tc, 0)] + [("c_eyef", 0, 0)],
                          writes=ps.keys(), tag=f"ct{loop}")
                kb.op("V", lambda E, a=sc.hrow_a.h[0:1, it4 * TT:(it4 + 1) * TT],
                      p=ps.h[0:1, 0:TT]: E.tensor_copy(a, p),
                      reads=ps.keys(), writes=sc.hrow_a.keys(),
                      tag=f"cr{loop}")
            for fh in range(2):
                w1s = W1[0]
                for c in range(NCH):
                    kb.dma("G", w1s.c(c, 0, 1024),
                           w1T_d[e * DM + c * 128:e * DM + (c + 1) * 128,
                                 fh * 1024:(fh + 1) * 1024],
                           reads=[], writes=w1s.keys(c), tag="w1l")
                    kb.dma("G", W2.c(c, 0, 1024),
                           w2T_d[e * 2048 + fh * 1024 + c * 128:
                                 e * 2048 + fh * 1024 + (c + 1) * 128, :],
                           reads=[], writes=W2.keys(c), tag="w2l")
                for m in range(NCH):
                    for it in range(4):
                        t0, t1 = it * TT, (it + 1) * TT
                        ps2 = sc.pb[(m * 4 + it) % 2]
                        for c in range(NCH):
                            kb.op("T", lambda E, p=ps2.h[:, 0:TT],
                                  l=w1s.c(c, m * 128, (m + 1) * 128),
                                  r=ZN.c(c, t0, t1), s=(c == 0),
                                  e2=(c == NCH - 1):
                                  E.matmul(p, l, r, start=s, stop=e2),
                                  reads=w1s.keys(c) + ZN.keys(c, t0, t1),
                                  writes=ps2.keys(), tag="hmm")
                        pbc = sc.pb[4]
                        kb.op("T", lambda E, p=pbc.h[:, 0:TT],
                              o=C.ones_bf[0:1, 0:128],
                              r=sc.hrow_a.h[0:1, t0:t1]:
                              E.matmul(p, o, r, start=True, stop=True),
                              reads=sc.hrow_a.keys(), writes=pbc.keys(),
                              tag="cbc")
                        kb.op("S", lambda E, a=sc.tt_f32.h[:, 0:TT],
                              p=ps2.h[:, 0:TT]:
                              E.activation(a, p, AF.Silu),
                              reads=ps2.keys(), writes=sc.tt_f32.keys(),
                              tag="silu")
                        kb.op("V", lambda E, a=HH.c(m, t0, t1),
                              b=sc.tt_f32.h[:, 0:TT], p=pbc.h[:, 0:TT]:
                              E.tensor_mul(a, b, p),
                              reads=sc.tt_f32.keys() + pbc.keys(),
                              writes=HH.keys(m, t0, t1), tag="hsc")
                first = (e == 0 and fh == 0)
                for m in range(NCH):
                    for it in range(4):
                        t0, t1 = it * TT, (it + 1) * TT
                        ps3 = sc.pb[2 + (m * 4 + it) % 2]
                        for c in range(NCH):
                            kb.op("T", lambda E, p=ps3.h[:, 0:TT],
                                  l=W2.c(c, m * 128, (m + 1) * 128),
                                  r=HH.c(c, t0, t1), s=(c == 0),
                                  e2=(c == NCH - 1):
                                  E.matmul(p, l, r, start=s, stop=e2),
                                  reads=W2.keys(c) + HH.keys(c, t0, t1),
                                  writes=ps3.keys(), tag="ymm")
                        if first:
                            kb.op("V", lambda E, a=MIX.c(m, t0, t1),
                                  p=ps3.h[:, 0:TT]: E.tensor_copy(a, p),
                                  reads=ps3.keys(),
                                  writes=MIX.keys(m, t0, t1), tag="mx0")
                        else:
                            kb.op("V", lambda E, a=MIX.c(m, t0, t1),
                                  p=ps3.h[:, 0:TT]: E.tensor_add(a, a, p),
                                  reads=ps3.keys() + MIX.keys(m, t0, t1),
                                  writes=MIX.keys(m, t0, t1), tag="mxa")
        if dbg and loop == 0:
            kb.dma("Y", sa_d, sc.cmb_sa.h[:, :], reads=[("cmbsa", 0, 0)],
                   writes=[], tag="osa")
            kb.dma("Y", ea_d, sc.cmb_ea.h[:, :], reads=[("cmbea", 0, 0)],
                   writes=[], tag="oea")
            store_tm(kb, comb_d, comb, 16, 8, tag="ocmb")
            store_fm(kb, zn_d, ZN, DM, TC, tag="ozn")
            store_fm(kb, mix_d, MIX, DM, TC, tag="omix")
        for m in range(NCH):
            for it in range(4):
                t0, t1 = it * TT, (it + 1) * TT
                kb.op("V", lambda E, a=Z.c(m, t0, t1), b=Z.c(m, t0, t1),
                      m2=MIX.c(m, t0, t1): E.tensor_add(a, b, m2),
                      reads=MIX.keys(m, t0, t1) + Z.keys(m, t0, t1),
                      writes=Z.keys(m, t0, t1), tag=f"zu{loop}")
        if dbg and loop == 0:
            store_fm(kb, z1_d, Z, DM, TC, tag="oz1")
    kb.alias("xsq3", [("hh", c, i) for c in range(NCH) for i in range(4)])
    XSQ = FM(kb, "xsq3", NCH, TC, BF16, back=SX.h)
    xtiles = [TBufSimple(kb, f"xt{i}", [128, TT], F32) for i in range(2)]
    emit_ln_add_residual(kb, C, sc, Z, xtiles, xsq=XSQ, xb=None,
                         xres_dram=(xT, xout), tag="lnr")
    with nc.Block() as block:
        _finish(nc, kb, block)
    return nc


def emit_rowscale_fm(kb, C, sc, a_fm, rows_hrow, out_fm, nch, nh, Tw=None,
                     gain_col=None, psum_ids=(4, 5), tag="rsc"):
    Tw = sc.Tw if Tw is None else Tw
    nt = ceil_div(Tw, TT)
    rk = rows_hrow.keys()
    for c in range(nch):
        h0, h1 = 2 * c, 2 * c + 1
        for it in range(nt):
            t0, t1 = it * TT, min(Tw, (it + 1) * TT)
            w = t1 - t0
            pbc = sc.pb[psum_ids[it % len(psum_ids)]]
            kb.op("T", lambda E, p=pbc.h[0:64, 0:w],
                  l=C.sel128f[:, h0 * 128:h0 * 128 + 64],
                  r=rows_hrow.h[0:16, t0:t1]:
                  E.matmul(p, l, r, start=True, stop=True),
                  reads=rk, writes=pbc.keys(), tag=tag + "b1")
            kb.op("T", lambda E, p=pbc.h[64:128, 0:w],
                  l=C.sel128f[:, h1 * 128:h1 * 128 + 64],
                  r=rows_hrow.h[0:16, t0:t1]:
                  E.matmul(p, l, r, start=True, stop=True),
                  reads=rk, writes=pbc.keys(), tag=tag + "b2")
            if gain_col is None:
                kb.op("V", lambda E, a=out_fm.c(c, t0, t1),
                      b=a_fm.c(c, t0, t1), p=pbc.h[:, 0:w]:
                      E.tensor_mul(a, b, p),
                      reads=pbc.keys() + a_fm.keys(c, t0, t1),
                      writes=out_fm.keys(c, t0, t1), tag=tag + "m")
            else:
                kb.op("V", lambda E, a=sc.tt_f32.h[:, 0:w],
                      b=a_fm.c(c, t0, t1), p=pbc.h[:, 0:w]:
                      E.tensor_mul(a, b, p),
                      reads=pbc.keys() + a_fm.keys(c, t0, t1),
                      writes=sc.tt_f32.keys(), tag=tag + "m1")
                gc = c if gain_col.shape[1] > 1 else 0
                kb.op("V", lambda E, a=out_fm.c(c, t0, t1),
                      b=sc.tt_f32.h[:, 0:w], g=gain_col[:, gc:gc + 1]:
                      E.tensor_scalar_mul(a, b, g),
                      reads=sc.tt_f32.keys() + [("wrow", 0, 0)],
                      writes=out_fm.keys(c, t0, t1), tag=tag + "m2")


def emit_headwise_rms(kb, C, sc, a_fm, nch, nh, gain_col, sq_scratch, Tw=None,
                      eps=1e-6, l2mode=False, eps_l2=1e-12, out_fm=None,
                      rows=None, tag="hrms"):
    Tw = sc.Tw if Tw is None else Tw
    out_fm = a_fm if out_fm is None else out_fm
    rows = sc.hrow_f if rows is None else rows
    if nh < 16:
        kb.op("V", lambda E, a=rows.h[0:16, 0:Tw]: E.memset(a, 1.0),
              reads=[], writes=rows.keys(), force=True, tag=tag + "z")
    emit_head_norms(kb, C, sc, a_fm, nch, nh, rows,
                    sq_scratch=sq_scratch, Tw=Tw, tag=tag + "n")
    if l2mode:
        kb.op("V", lambda E, a=rows.h[0:nh, 0:Tw]:
              E.tensor_scalar_max(a, a, eps_l2),
              reads=rows.keys(), writes=rows.keys(), force=True,
              tag=tag + "mx")
    else:
        kb.op("V", lambda E, a=rows.h[0:nh, 0:Tw],
              b=rows.h[0:nh, 0:Tw]: E.tensor_mul(a, b, b),
              reads=rows.keys(), writes=rows.keys(), force=True,
              tag=tag + "sq")
        kb.op("S", lambda E, a=rows.h[0:nh, 0:Tw],
              b=rows.h[0:nh, 0:Tw]:
              E.activation(a, b, AF.Sqrt, bias=C.eps_ap(eps, nh),
                           scale=1.0 / 64),
              reads=rows.keys() + [("c_eps", 0, 0)],
              writes=rows.keys(), tag=tag + "sd")
    kb.op("V", lambda E, a=rows.h[0:nh, 0:Tw]: E.reciprocal(a, a),
          reads=rows.keys(), writes=rows.keys(), force=True,
          tag=tag + "rc")
    emit_rowscale_fm(kb, C, sc, a_fm, rows, out_fm, nch, nh, Tw=Tw,
                     gain_col=gain_col, tag=tag + "s")


def emit_rope_tables(kb, C, sc, a_fm, cos_sb, sin_sb, rot_sb, nch, Tw=None,
                     psum_ids=(6, 7), ckeys=("c_rcos", "c_rsin", "c_rot"),
                     tag="rpt"):
    Tw = sc.Tw if Tw is None else Tw
    nt = ceil_div(Tw, TT)
    for c in range(nch):
        for it in range(nt):
            t0, t1 = it * TT, min(Tw, (it + 1) * TT)
            w = t1 - t0
            ps = sc.pb[psum_ids[(c * nt + it) % len(psum_ids)]]
            kb.op("T", lambda E, p=ps.h[:, 0:w], l=rot_sb,
                  r=a_fm.c(c, t0, t1):
                  E.matmul(p, l, r, start=True, stop=True),
                  reads=a_fm.keys(c, t0, t1) + [(ckeys[2], 0, 0)],
                  writes=ps.keys(), tag=tag + "mm")
            kb.op("V", lambda E, a=sc.tt_f32.h[:, 0:w], p=ps.h[:, 0:w],
                  m=sin_sb[:, t0:t1]: E.tensor_mul(a, p, m),
                  reads=ps.keys() + [(ckeys[1], 0, 0)],
                  writes=sc.tt_f32.keys(), tag=tag + "m1")
            kb.op("V", lambda E, a=a_fm.c(c, t0, t1), b=a_fm.c(c, t0, t1),
                  m=cos_sb[:, t0:t1]: E.tensor_mul(a, b, m),
                  reads=[(ckeys[0], 0, 0)] + a_fm.keys(c, t0, t1),
                  writes=a_fm.keys(c, t0, t1), tag=tag + "m2")
            kb.op("V", lambda E, a=a_fm.c(c, t0, t1), b=a_fm.c(c, t0, t1),
                  m=sc.tt_f32.h[:, 0:w]: E.tensor_add(a, b, m),
                  reads=sc.tt_f32.keys() + a_fm.keys(c, t0, t1),
                  writes=a_fm.keys(c, t0, t1), tag=tag + "m3")


def build_l2f_p1():
    nc = _new_nc()
    xT = nc.dram_tensor("xT", [DM, TC], F32, kind="ExternalInput").ap()
    WqT = nc.dram_tensor("WqT", [DM, DM], F32, kind="ExternalInput").ap()
    WkT = nc.dram_tensor("WkT", [DM, 256], F32, kind="ExternalInput").ap()
    WvT = nc.dram_tensor("WvT", [DM, 256], F32, kind="ExternalInput").ap()
    bq = nc.dram_tensor("bq", [128, NCH], F32, kind="ExternalInput").ap()
    bk = nc.dram_tensor("bk", [128, 2], F32, kind="ExternalInput").ap()
    bv = nc.dram_tensor("bv", [1, 256], F32, kind="ExternalInput").ap()
    ln1 = nc.dram_tensor("ln1", [128, NCH], F32, kind="ExternalInput").ap()
    qnw = nc.dram_tensor("qnw", [128, 1], F32, kind="ExternalInput").ap()
    knw = nc.dram_tensor("knw", [128, 1], F32, kind="ExternalInput").ap()
    ropeC = nc.dram_tensor("ropeC", [128, TC], F32, kind="ExternalInput").ap()
    ropeS = nc.dram_tensor("ropeS", [128, TC], F32, kind="ExternalInput").ap()
    rot64 = nc.dram_tensor("rot64", [128, 128], F32,
                           kind="ExternalInput").ap()
    u0_d = nc.dram_tensor("u0_d", [DM, TC], BF16, kind="ExternalOutput").ap()
    q_d = nc.dram_tensor("q_d", [DM, TC], BF16, kind="ExternalOutput").ap()
    k_d = nc.dram_tensor("k_d", [256, TC], BF16, kind="ExternalOutput").ap()
    v_d = nc.dram_tensor("v_d", [TC, 4 * VW], BF16,
                         kind="ExternalOutput").ap()
    kmax_d = nc.dram_tensor("kmax_d", [16, 1], F32,
                            kind="ExternalOutput").ap()

    kb = Kb(nc)
    C = Consts(kb, nc)
    kb.set_setup_barrier()
    sc = Scratch(kb, TC)
    SX = FM(kb, "sx", NCH, TC, BF16)
    U0 = FM(kb, "u0", NCH, TC, BF16)
    HN = FM(kb, "hn", NCH, TC, BF16)
    K = FM(kb, "slabk", 2, TC, BF16)
    V = TM(kb, "slabv", 16, 4 * VW, BF16)
    W = FM(kb, "w", NCH, DM, BF16)
    cosb = kb.sb("cosb", [128, TC], BF16)
    sinb = kb.sb("sinb", [128, TC], BF16)
    rotb = kb.sb("rotb", [128, 128], BF16)
    qbc = kb.sb("qbc", [128, NCH], F32)
    kbc = kb.sb("kbc", [128, 2], F32)
    vbr = kb.sb("vbr", [1, 256], BF16)
    ln1c = kb.sb("ln1c", [128, NCH], F32)
    qnc = kb.sb("qnc", [128, 1], F32)
    knc = kb.sb("knc", [128, 1], F32)
    for dst, src, ky, eng in [(qbc, bq, "wrow", "Y"), (kbc, bk, "wrow", "Y"),
                              (vbr, bv, "wrow", "G"), (ln1c, ln1, "wrow", "Y"),
                              (qnc, qnw, "wrow", "Y"), (knc, knw, "wrow", "Y"),
                              (cosb, ropeC, "c_rcos", "G"),
                              (sinb, ropeS, "c_rsin", "G"),
                              (rotb, rot64, "c_rot", "G")]:
        kb.dma(eng, dst[:, :], src, reads=[], writes=[(ky, 0, 0)])

    load_fm(kb, SX, xT, DM, TC, dt_cast=True, tag="lx")
    emit_ln_fm(kb, C, sc, SX, U0, xsq=U0, xb=None, tag="ln")
    emit_rmsnorm_fm(kb, C, sc, U0, HN, xsq=SX, wcol=ln1c[:, :], eps=1e-6,
                    tag="rm")
    store_fm(kb, u0_d, U0, DM, TC, tag="ou0")
    load_weight_fm(kb, W, WqT, DM, DM, tag="wq")
    kb.alias("qs", [("sx", c, i) for c in range(NCH) for i in range(4)])
    Q = FM(kb, "qs", NCH, TC, BF16, back=SX.h)
    emit_proj_fm(kb, C, sc, Q, HN, W, NCH, NCH, bias_col=qbc[:, :],
                 psum_ids=(0, 1), tag="q")
    emit_rope_tables(kb, C, sc, Q, cosb, sinb, rotb[:, :], NCH, tag="rq")
    emit_headwise_rms(kb, C, sc, Q, NCH, 16, qnc, sq_scratch=U0, Tw=TC,
                      tag="qn")
    store_fm(kb, q_d, Q, DM, TC, tag="oq")
    kb.alias("w2", [("w", c, i) for c in range(NCH) for i in range(2)])
    W2 = FM(kb, "w2", NCH, 256, BF16, back=W.h)
    load_weight_fm(kb, W2, WkT, DM, 256, tag="wk")
    emit_proj_fm(kb, C, sc, K, HN, W2, NCH, 2, bias_col=kbc[:, :],
                 psum_ids=(2, 3), tag="k")
    emit_rope_tables(kb, C, sc, K, cosb, sinb, rotb[:, :], 2, tag="rk")
    emit_headwise_rms(kb, C, sc, K, 2, 4, knc, sq_scratch=U0, Tw=TC, tag="kn")
    kb.alias("w3", [("w2", c, i) for c in range(NCH) for i in range(2)])
    W3 = FM(kb, "w3", NCH, 256, BF16, back=W.h)
    load_weight_fm(kb, W3, WvT, DM, 256, tag="wv")
    emit_v_ones(kb, V, ntc=16, nh=4)
    emit_proj_tm(kb, C, sc, V, HN, W3, NCH, 256, bias_row=vbr[:, :],
                 psum_ids=(4, 5), colmap=v_colmap(V, nh=4), tag="v")
    emit_head_norms(kb, C, sc, K, 2, 4, sc.hrow_a, sq_scratch=U0, Tw=TC,
                    tag="km")
    kb.op("V", lambda E, a=sc.hcol_a.h[0:4, 0:1], b=sc.hrow_a.h[0:4, 0:TC]:
          E.reduce_max(a, b, axis=AX.X),
          reads=sc.hrow_a.keys(), writes=sc.hcol_a.keys(), force=True,
          tag="kmax")
    store_fm(kb, k_d, K, 256, TC, tag="ok")
    store_tm(kb, v_d, V, 16, 4 * VW, tag="ov")
    kb.dma("Y", kmax_d[0:4, :], sc.hcol_a.h[0:4, 0:1],
           reads=sc.hcol_a.keys(), writes=[], tag="okm")
    with nc.Block() as block:
        _finish(nc, kb, block)
    return nc

def build_l2f_p2():
    nc = _new_nc()
    u0_d = nc.dram_tensor("u0_d", [DM, TC], BF16, kind="ExternalInput").ap()
    q_d = nc.dram_tensor("q_d", [DM, TC], BF16, kind="ExternalInput").ap()
    ks_d = nc.dram_tensor("ks_d", [256, TC], BF16, kind="ExternalInput").ap()
    kp_d = nc.dram_tensor("kp_d", [256, TC], BF16, kind="ExternalInput").ap()
    vs_d = nc.dram_tensor("vs_d", [TC, 4 * VW], BF16,
                          kind="ExternalInput").ap()
    vp_d = nc.dram_tensor("vp_d", [TC, 4 * VW], BF16,
                          kind="ExternalInput").ap()
    km2 = nc.dram_tensor("km2", [16, 2], F32, kind="ExternalInput").ap()
    trilm = nc.dram_tensor("trilm", [4 * 128, TT], F32,
                           kind="ExternalInput").ap()
    Wg_d = nc.dram_tensor("Wg", [64, 64], F32, kind="ExternalInput").ap()
    bg_d = nc.dram_tensor("bg", [128, 1], F32, kind="ExternalInput").ap()
    Wo_d = nc.dram_tensor("Wo", [DM, DM], F32, kind="ExternalInput").ap()
    bo_d = nc.dram_tensor("bo", [128, NCH], F32, kind="ExternalInput").ap()
    xi_d = nc.dram_tensor("xi_d", [DM, TC], BF16, kind="ExternalOutput").ap()

    kb = Kb(nc)
    C = Consts(kb, nc)
    kb.set_setup_barrier()
    sc = Scratch(kb, TC)
    Q = FM(kb, "slabq", NCH, TC, BF16)
    K = FM(kb, "slabk", NCH, TC, BF16)
    V = TM(kb, "slabv", 16, 16 * VW, BF16)
    CA = FM(kb, "ctxa", NCH, TC, BF16)
    u0t = [TBufSimple(kb, f"u0t{i}", [128, TT], BF16) for i in range(2)]
    TRI = FM(kb, "tri", 4, TT, BF16)
    wg_sb = kb.sb("wgsb", [128, 64], BF16)
    bg_sb = kb.sb("bgsb", [128, 1], F32)
    bo_sb = kb.sb("bosb", [128, NCH], F32)
    km_sb = kb.sb("kmsb", [16, 2], F32)
    kb.dma("G", wg_sb[0:64, :], Wg_d, reads=[], writes=[("wgsb", 0, 0)])
    kb.dma("G", wg_sb[64:128, :], Wg_d, reads=[], writes=[("wgsb", 0, 0)])
    kb.dma("Y", bg_sb[:, :], bg_d, reads=[], writes=[("wrow", 0, 0)])
    kb.dma("Y", bo_sb[:, :], bo_d, reads=[], writes=[("wrow", 0, 0)])
    kb.dma("Y", km_sb[:, :], km2, reads=[], writes=[("kmrow", 0, 0)])
    load_fm(kb, TRI, trilm, 4 * 128, TT, dt_cast=True, tag="tri")
    load_fm(kb, Q, q_d, DM, TC, tag="lq")
    emit_head_norms(kb, C, sc, Q, NCH, 16, sc.hrow_a, sq_scratch=K, tag="qn")
    kb.op("V", lambda E, a=sc.hcol_a.h[0:16, 0:1], b=km_sb[0:16, 0:2]:
          E.reduce_max(a, b, axis=AX.X),
          reads=[("kmrow", 0, 0)], writes=sc.hcol_a.keys(), force=True,
          tag="gkm")
    kb.op("V", lambda E, a=sc.hrow_a.h[0:16, 0:TC], b=sc.hrow_a.h[0:16, 0:TC],
          s=sc.hcol_a.h[0:16, 0:1]:
          E.tensor_scalar(a, b, s, -1.0, ALU.mult, ALU.mult),
          reads=sc.hrow_a.keys() + sc.hcol_a.keys(),
          writes=sc.hrow_a.keys(), tag="negB")
    def mk_kexp(k_dram):
        def k_load(kb2, slabK):
            for h in range(16):
                kh = h // 4
                hc, hp = h // 2, (h % 2) * 64
                kb2.dma("Y", slabK.c(hc, 0, TC, hp, hp + 64),
                        k_dram[kh * 64:(kh + 1) * 64, :],
                        reads=[(slabK.name, hc, ck) for ck in range(16)],
                        writes=[(slabK.name, hc, ck) for ck in range(16)],
                        tag="stk")
        return k_load

    def mk_vexp(v_dram):
        def v_load(kb2, slabV):
            for tc in range(16):
                src = bass.AP(v_dram.tensor, tc * 128 * (4 * VW),
                              [[4 * VW, 128], [VW, 4], [0, 4], [1, VW]])
                dst = slabV.h[:, tc * 16 * VW:(tc + 1) * 16 * VW].rearrange(
                    "p (kh r w) -> p kh r w", kh=4, r=4)
                kb2.dma("Y", dst, src,
                        reads=[(slabV.name, tc, 0)],
                        writes=[(slabV.name, tc, 0)], tag="stv")
        return v_load

    kls, vls, nkt = mk_kexp(ks_d), mk_vexp(vs_d), 16
    klp, vlp = mk_kexp(kp_d), mk_vexp(vp_d)

    def kt_filter(si, h, it):
        if si == 0:
            return range(0, min(16, 4 * it + 4))
        return range(16)

    def mask_fn(si, h, it, ck):
        if si != 0:
            return None
        off = ck * 128 - it * TT
        if off < 0 or off >= TT or off % 128 != 0:
            return None
        c = off // 128
        return (TRI.c(c, 0, TT), TRI.keys(c))

    emit_attn(kb, C, sc, nh=16, q_fm=Q, nchq=NCH, negB_hrow=sc.hrow_a,
              kv_srcs=[(kls, vls, nkt), (klp, vlp, nkt)],
              slabK=K, slabV=V, ctx_acc=CA, ctx_bf=CA,
              scale_pre_exp=0.125,
              kt_filter=kt_filter, mask_fn=mask_fn, tag="at")
    for h in range(16):
        hc, hp = h // 2, (h % 2) * 64
        for it in range(4):
            t0, t1 = it * TT, (it + 1) * TT
            ps = sc.pb[2 + (it % 2)]
            kb.op("T", lambda E, p=ps.h[0:64, 0:TT],
                  l=wg_sb[hp:hp + 64, :],
                  r=CA.c(hc, t0, t1, hp, hp + 64):
                  E.matmul(p, l, r, start=True, stop=True),
                  reads=[("wgsb", 0, 0), (CA.name, hc, it, hp)],
                  writes=ps.keys(), tag="gmm")
            kb.op("S", lambda E, a=sc.tt_f32.h[hp:hp + 64, 0:TT],
                  p=ps.h[0:64, 0:TT], b=bg_sb[hp:hp + 64, 0:1]:
                  E.activation(a, p, AF.Sigmoid, bias=b),
                  reads=ps.keys() + [("wrow", 0, 0)],
                  writes=sc.tt_f32.keys(), tag="gsig")
            kb.op("V", lambda E, a=CA.c(hc, t0, t1, hp, hp + 64),
                  b=CA.c(hc, t0, t1, hp, hp + 64),
                  m=sc.tt_f32.h[hp:hp + 64, 0:TT]: E.tensor_mul(a, b, m),
                  reads=sc.tt_f32.keys() + [(CA.name, hc, it, hp)],
                  writes=[(CA.name, hc, it, hp)], tag="gm")
    kb.alias("ctxr", [(CA.name, c, it, hp) for c in range(NCH)
                      for it in range(4) for hp in (0, 64)])
    CAr = FM(kb, "ctxr", NCH, TC, BF16, back=CA.h)
    kb.alias("w", [(K.name, c, ck) for c in range(NCH) for ck in range(16)])
    W = FM(kb, "w", NCH, DM, BF16, back=K.h)
    load_weight_fm(kb, W, Wo_d, DM, DM, tag="wo")
    kb.alias("yslab", [(V.name, tc, 0) for tc in range(16)])
    Y = FM(kb, "yslab", NCH, TC, BF16, back=V.h)
    emit_proj_fm(kb, C, sc, Y, CAr, W, NCH, NCH, bias_col=bo_sb[:, :],
                 psum_ids=(0, 1), tag="op")
    for c in range(NCH):
        for it in range(4):
            t0, t1 = it * TT, (it + 1) * TT
            ut = u0t[(c * 4 + it) % 2]
            kb.dma("Y", ut.h[:, 0:TT], u0_d[c * 128:(c + 1) * 128, t0:t1],
                   reads=ut.keys(), writes=ut.keys(), tag="lu0")
            kb.op("V", lambda E, a=Y.c(c, t0, t1), b=Y.c(c, t0, t1),
                  m=ut.h[:, 0:TT]: E.tensor_add(a, b, m),
                  reads=ut.keys() + Y.keys(c, t0, t1),
                  writes=Y.keys(c, t0, t1), tag="res")
    store_fm(kb, xi_d, Y, DM, TC, tag="oxi")
    with nc.Block() as block:
        _finish(nc, kb, block)
    return nc

def build_l2f_p3(dbg=False):
    nc = _new_nc()
    xT = nc.dram_tensor("xT", [DM, TC], F32, kind="ExternalInput").ap()
    xi_d = nc.dram_tensor("xi_d", [DM, TC], BF16, kind="ExternalInput").ap()
    ln2 = nc.dram_tensor("ln2", [128, NCH], F32, kind="ExternalInput").ap()
    WrT = nc.dram_tensor("WrT", [DM, 8], F32, kind="ExternalInput").ap()
    brr = nc.dram_tensor("brr", [1, 8], F32, kind="ExternalInput").ap()
    W1_d = nc.dram_tensor("W1", [8 * DM, 4096], F32,
                          kind="ExternalInput").ap()
    W2_d = nc.dram_tensor("W2", [8 * 2048, DM], F32,
                          kind="ExternalInput").ap()
    b1c_d = nc.dram_tensor("b1c", [128, 8 * 32], F32,
                           kind="ExternalInput").ap()
    b2r_d = nc.dram_tensor("b2r", [8, DM], F32, kind="ExternalInput").ap()
    xout = nc.dram_tensor("xout", [DM, TC], F32, kind="ExternalOutput").ap()
    if dbg:
        mix_d = nc.dram_tensor("mix_d", [DM, TC], BF16,
                               kind="ExternalOutput").ap()
        oh_d = nc.dram_tensor("oh_d", [TC, 8], F32,
                              kind="ExternalOutput").ap()
        lg_d = nc.dram_tensor("lg_d", [TC, 8], F32,
                              kind="ExternalOutput").ap()
        h2_d = nc.dram_tensor("h2_d", [DM, TC], BF16,
                              kind="ExternalOutput").ap()

    kb = Kb(nc)
    C = Consts(kb, nc)
    kb.set_setup_barrier()
    sc = Scratch(kb, TC, attn=False, cmb=False, rows16=False)
    XI = FM(kb, "xi", NCH, TC, BF16)
    H2 = FM(kb, "h2", NCH, TC, BF16)
    ACT_ = FM(kb, "acts", NCH, TC, BF16)
    MIX = FM(kb, "mix", NCH, TC, BF16)
    W1a = FM(kb, "w1a", NCH, 1024, BF16)
    W2s = FM(kb, "w2s", NCH, 1024, BF16)
    oh_tm = TM(kb, "ohall", 16, 8, F32)
    lg_tm = TM(kb, "lgall", 16, 8, F32)
    crow = Row(kb, "crow", 1, TC, BF16)
    wrt_sb = kb.sb("wrtsb", [128, 8 * 8], BF16)
    br_sb = kb.sb("brsb", [1, 8], BF16)
    ln2c = kb.sb("ln2c", [128, NCH], F32)
    b1sb = kb.sb("b1sb", [128, 8 * 32], F32)
    b2sb = kb.sb("b2sb", [1, DM], BF16)
    kb.dma("Y", ln2c[:, :], ln2, reads=[], writes=[("wrow", 0, 0)])
    kb.dma("Y", b1sb[:, :], b1c_d, reads=[], writes=[("b1k", 0, 0)])
    kb.dma("G", br_sb[:, :], brr, reads=[], writes=[("wrow", 0, 0)])
    for c in range(NCH):
        kb.dma("G", wrt_sb[:, c * 8:(c + 1) * 8],
               WrT[c * 128:(c + 1) * 128, :], reads=[],
               writes=[("wrt", 0, 0)], tag="wrt")

    load_fm(kb, XI, xi_d, DM, TC, tag="lxi")
    emit_rmsnorm_fm(kb, C, sc, XI, H2, xsq=MIX, wcol=ln2c[:, :], eps=1e-6,
                    tag="rm2")
    A = sc.cmb_a.h
    ak = [("cmba", 0, 0)]
    for tc in range(16):
        ps = sc.pb[5]
        for c in range(NCH):
            kb.op("T", lambda E, p=ps.h[:, 0:8],
                  l=H2.c(c, tc * 128, (tc + 1) * 128),
                  r=wrt_sb[:, c * 8:(c + 1) * 8], s=(c == 0), e=False:
                  E.matmul(p, l, r, start=s, stop=False),
                  reads=H2.keys(c) + [("wrt", 0, 0)], writes=ps.keys(),
                  tag="rmm")
        kb.op("T", lambda E, p=ps.h[:, 0:8], o=C.ones_bf[0:1, 0:128],
              r=br_sb[0:1, 0:8]:
              E.matmul(p, o, r, start=False, stop=True),
              reads=[("wrow", 0, 0)], writes=ps.keys(), tag="rb")
        ck = ps.keys()
        V_ = lambda fn, rd, wr: kb.op("V", fn, reads=rd, writes=wr,
                                      force=True, tag="oh")
        V_(lambda E: E.reduce_max(A[:, 8:9], ps.h[:, 0:8], axis=AX.X,
                                  negate=True), ck, ak)
        V_(lambda E: E.tensor_scalar(A[:, 12:20], ps.h[:, 0:8], A[:, 8:9],
                                     None, ALU.add), ck + ak, ak)
        V_(lambda E: E.tensor_scalar(A[:, 0:8], A[:, 12:20], 0.0,
                                     None, ALU.is_ge), ak, ak)
        V_(lambda E: E.tensor_scalar(A[:, 9:10], A[:, 0:1], -1.0, 1.0,
                                     ALU.mult, ALU.add), ak, ak)
        for j in range(1, 8):
            V_(lambda E, j=j: E.tensor_mul(A[:, 0 + j:1 + j],
                                           A[:, 0 + j:1 + j], A[:, 9:10]),
               ak, ak)
            if j < 7:
                V_(lambda E, j=j: E.tensor_scalar(A[:, 10:11],
                                                  A[:, 0 + j:1 + j], -1.0,
                                                  1.0, ALU.mult, ALU.add),
                   ak, ak)
                V_(lambda E: E.tensor_mul(A[:, 9:10], A[:, 9:10],
                                          A[:, 10:11]), ak, ak)
        kb.op("V", lambda E, a=oh_tm.c(tc, 0, 8), b=A[:, 0:8]:
              E.tensor_copy(a, b), reads=ak, writes=[("ohall", tc, 0)],
              tag="ohw")
        kb.op("V", lambda E, a=lg_tm.c(tc, 0, 8), b=ps.h[:, 0:8]:
              E.tensor_copy(a, b), reads=ck, writes=[("lgall", tc, 0)],
              tag="lgw")
    for e in range(8):
        kb.dma("G", b2sb[0:1, :], b2r_d[e:e + 1, :],
               reads=[("b2k", 0, 0)], writes=[("b2k", 0, 0)], tag="b2l")
        ps = sc.pb[5]
        for it4 in range(4):
            for tcc in range(4):
                tc = it4 * 4 + tcc
                kb.op("T", lambda E, p=ps.h[0:1, tcc * 128:(tcc + 1) * 128],
                      l=oh_tm.h[:, tc * 8 + e:tc * 8 + e + 1],
                      r=C.eye_f[0:128, 0:128]:
                      E.matmul(p, l, r, start=True, stop=True,
                               skip_group_check=True),
                      reads=[("ohall", tc, 0), ("c_eyef", 0, 0)],
                      writes=ps.keys(), tag="ct")
            kb.op("V", lambda E, a=crow.h[0:1, it4 * TT:(it4 + 1) * TT],
                  p=ps.h[0:1, 0:TT]: E.tensor_copy(a, p),
                  reads=ps.keys(), writes=crow.keys(), tag="cr")
        for q2 in range(2):
            for c in range(NCH):
                kb.dma("G", W1a.c(c, 0, 1024),
                       W1_d[e * DM + c * 128:e * DM + (c + 1) * 128,
                            q2 * 1024:(q2 + 1) * 1024],
                       reads=[], writes=W1a.keys(c), tag="w1a")
                kb.dma("G", W2s.c(c, 0, 1024),
                       W2_d[e * 2048 + q2 * 1024 + c * 128:
                            e * 2048 + q2 * 1024 + (c + 1) * 128, :],
                       reads=[], writes=W2s.keys(c), tag="w2l")
            for m in range(NCH):
                for it in range(4):
                    t0, t1 = it * TT, (it + 1) * TT
                    p1 = sc.pb[(m * 4 + it) % 2]
                    bc1 = b1sb[:, e * 32 + q2 * 8 + m:e * 32 + q2 * 8 + m + 1]
                    for c in range(NCH):
                        kb.op("T", lambda E, p=p1.h[:, 0:TT],
                              l=W1a.c(c, m * 128, (m + 1) * 128),
                              r=H2.c(c, t0, t1), s=(c == 0),
                              e2=(c == NCH - 1):
                              E.matmul(p, l, r, start=s, stop=e2),
                              reads=W1a.keys(c) + H2.keys(c, t0, t1),
                              writes=p1.keys(), tag="u1m")
                    kb.op("S", lambda E, a=ACT_.c(m, t0, t1),
                          p=p1.h[:, 0:TT], b=bc1:
                          E.activation(a, p, AF.Identity, bias=b),
                          reads=p1.keys() + [("b1k", 0, 0)],
                          writes=ACT_.keys(m, t0, t1), tag="u1ev")
            for c in range(NCH):
                kb.dma("G", W1a.c(c, 0, 1024),
                       W1_d[e * DM + c * 128:e * DM + (c + 1) * 128,
                            2048 + q2 * 1024:2048 + (q2 + 1) * 1024],
                       reads=[], writes=W1a.keys(c), tag="w1b")
            for m in range(NCH):
                for it in range(4):
                    t0, t1 = it * TT, (it + 1) * TT
                    p2 = sc.pb[(m * 4 + it) % 2]
                    pbc = sc.pb[4]
                    bc2 = b1sb[:, e * 32 + 16 + q2 * 8 + m:
                               e * 32 + 16 + q2 * 8 + m + 1]
                    for c in range(NCH):
                        kb.op("T", lambda E, p=p2.h[:, 0:TT],
                              l=W1a.c(c, m * 128, (m + 1) * 128),
                              r=H2.c(c, t0, t1), s=(c == 0),
                              e2=(c == NCH - 1):
                              E.matmul(p, l, r, start=s, stop=e2),
                              reads=W1a.keys(c) + H2.keys(c, t0, t1),
                              writes=p2.keys(), tag="u2m")
                    kb.op("T", lambda E, p=pbc.h[:, 0:TT],
                          o=C.ones_bf[0:1, 0:128],
                          r=crow.h[0:1, t0:t1]:
                          E.matmul(p, o, r, start=True, stop=True),
                          reads=crow.keys(), writes=pbc.keys(), tag="ohb")
                    kb.op("S", lambda E, a=sc.tt_f32.h[:, 0:TT],
                          p=p2.h[:, 0:TT], b=bc2:
                          E.activation(a, p, AF.Sigmoid, bias=b),
                          reads=p2.keys() + [("b1k", 0, 0)],
                          writes=sc.tt_f32.keys(), tag="sig")
                    kb.op("V", lambda E, a=sc.tt2.h[:, 0:TT],
                          b=sc.tt_f32.h[:, 0:TT], p=pbc.h[:, 0:TT]:
                          E.tensor_mul(a, b, p),
                          reads=sc.tt_f32.keys() + pbc.keys(),
                          writes=sc.tt2.keys(), tag="a1")
                    kb.op("V", lambda E, a=ACT_.c(m, t0, t1),
                          b=ACT_.c(m, t0, t1), m2=sc.tt2.h[:, 0:TT]:
                          E.tensor_mul(a, b, m2),
                          reads=sc.tt2.keys() + ACT_.keys(m, t0, t1),
                          writes=ACT_.keys(m, t0, t1), tag="a2")
            first = (e == 0 and q2 == 0)
            for m in range(NCH):
                for it in range(4):
                    t0, t1 = it * TT, (it + 1) * TT
                    ps3 = sc.pb[2 + (m * 4 + it) % 2]
                    for c in range(NCH):
                        kb.op("T", lambda E, p=ps3.h[:, 0:TT],
                              l=W2s.c(c, m * 128, (m + 1) * 128),
                              r=ACT_.c(c, t0, t1), s=(c == 0), e2=False:
                              E.matmul(p, l, r, start=s, stop=e2),
                              reads=W2s.keys(c) + ACT_.keys(c, t0, t1),
                              writes=ps3.keys(), tag="ymm")
                    if q2 == 0:
                        kb.op("T", lambda E, p=ps3.h[:, 0:TT],
                              l=b2sb[0:1, m * 128:(m + 1) * 128],
                              r=crow.h[0:1, t0:t1]:
                              E.matmul(p, l, r, start=False, stop=True),
                              reads=crow.keys() + [("b2k", 0, 0)],
                              writes=ps3.keys(), tag="yb")
                    else:
                        kb.op("T", lambda E, p=ps3.h[0:1, 0:1],
                              o=C.ones_bf[0:1, 0:1], r=C.ones_bf[0:1, 0:1]:
                              E.matmul(p, o, r, start=False, stop=True,
                                       skip_group_check=True),
                              reads=[], writes=ps3.keys(), tag="ycl")
                    if first:
                        kb.op("V", lambda E, a=MIX.c(m, t0, t1),
                              p=ps3.h[:, 0:TT]: E.tensor_copy(a, p),
                              reads=ps3.keys(), writes=MIX.keys(m, t0, t1),
                              tag="mx0")
                    else:
                        kb.op("V", lambda E, a=MIX.c(m, t0, t1),
                              p=ps3.h[:, 0:TT]: E.tensor_add(a, a, p),
                              reads=ps3.keys() + MIX.keys(m, t0, t1),
                              writes=MIX.keys(m, t0, t1), tag="mxa")
    if dbg:
        store_fm(kb, mix_d, MIX, DM, TC, tag="omix")
        store_tm(kb, oh_d, oh_tm, 16, 8, tag="ooh")
        store_tm(kb, lg_d, lg_tm, 16, 8, tag="olg")
        store_fm(kb, h2_d, H2, DM, TC, tag="oh2")
    for m in range(NCH):
        for it in range(4):
            t0, t1 = it * TT, (it + 1) * TT
            kb.op("V", lambda E, a=XI.c(m, t0, t1), b=XI.c(m, t0, t1),
                  m2=MIX.c(m, t0, t1): E.tensor_add(a, b, m2),
                  reads=MIX.keys(m, t0, t1) + XI.keys(m, t0, t1),
                  writes=XI.keys(m, t0, t1), tag="zu")
    kb.alias("xsq3", [("h2", c, i) for c in range(NCH) for i in range(4)])
    XSQ = FM(kb, "xsq3", NCH, TC, BF16, back=H2.h)
    xtiles = [TBufSimple(kb, "xt0", [128, TT], F32)]
    emit_ln_add_residual(kb, C, sc, XI, xtiles, xsq=XSQ, xb=None,
                         xres_dram=(xT, xout), tag="lnr")
    with nc.Block() as block:
        _finish(nc, kb, block)
    return nc


EXTW = 2306
KV0 = 1
NKV = 18


def build_l1f_pre():
    nc = _new_nc()
    xT = nc.dram_tensor("xT", [DM, TC], F32, kind="ExternalInput").ap()
    z_d = nc.dram_tensor("z_d", [DM, TC], F32, kind="ExternalOutput").ap()
    kb = Kb(nc)
    C = Consts(kb, nc)
    kb.set_setup_barrier()
    sc = Scratch(kb, TC, attn=False, cmb=False)
    SX = FM(kb, "sx", NCH, TC, BF16)
    XN = FM(kb, "xn", NCH, TC, BF16)
    load_fm(kb, SX, xT, DM, TC, dt_cast=True, tag="lx")
    emit_ln_fm(kb, C, sc, SX, XN, xsq=XN, xb=None, out_dram=z_d, tag="ln")
    with nc.Block() as block:
        _finish(nc, kb, block)
    return nc


def build_l1f_post():
    nc = _new_nc()
    xT = nc.dram_tensor("xT", [DM, TC], F32, kind="ExternalInput").ap()
    z_d = nc.dram_tensor("z_d", [DM, TC], F32, kind="ExternalInput").ap()
    xout = nc.dram_tensor("xout", [DM, TC], F32, kind="ExternalOutput").ap()
    kb = Kb(nc)
    C = Consts(kb, nc)
    kb.set_setup_barrier()
    sc = Scratch(kb, TC, attn=False, cmb=False)
    Z = FM(kb, "z", NCH, TC, BF16)
    XSQ = FM(kb, "xsq", NCH, TC, BF16)
    load_fm(kb, Z, z_d, DM, TC, dt_cast=True, tag="lz")
    xtiles = [TBufSimple(kb, f"xt{i}", [128, TT], F32) for i in range(2)]
    emit_ln_add_residual(kb, C, sc, Z, xtiles, xsq=XSQ, xb=None,
                         xres_dram=(xT, xout), tag="lnr")
    with nc.Block() as block:
        _finish(nc, kb, block)
    return nc


def build_l1f_attn():
    nc = _new_nc()
    zE = nc.dram_tensor("zE", [DM, EXTW], F32, kind="ExternalInput").ap()
    Wq_d = nc.dram_tensor("Wq", [DM, DM], F32, kind="ExternalInput").ap()
    Wk_d = nc.dram_tensor("Wk", [DM, DM], F32, kind="ExternalInput").ap()
    Wv_d = nc.dram_tensor("Wv", [DM, DM], F32, kind="ExternalInput").ap()
    Wo_d = nc.dram_tensor("Wo", [DM, DM], F32, kind="ExternalInput").ap()
    qs_d = nc.dram_tensor("qs", [128, 1], F32, kind="ExternalInput").ap()
    ks_d = nc.dram_tensor("ks", [128, 1], F32, kind="ExternalInput").ap()
    xqc = nc.dram_tensor("xqc", [128, TC], F32, kind="ExternalInput").ap()
    xqs = nc.dram_tensor("xqs", [128, TC], F32, kind="ExternalInput").ap()
    xkc = nc.dram_tensor("xkc", [128, 2304], F32, kind="ExternalInput").ap()
    xks = nc.dram_tensor("xks", [128, 2304], F32, kind="ExternalInput").ap()
    rot64 = nc.dram_tensor("rot64", [128, 128], F32,
                           kind="ExternalInput").ap()
    lwm = nc.dram_tensor("lwm", [4 * 128, 256], F32,
                         kind="ExternalInput").ap()
    vones = nc.dram_tensor("vones", [NKV * 128, 16], F32,
                           kind="ExternalInput").ap()
    zout = nc.dram_tensor("zout", [DM, TC], F32, kind="ExternalOutput").ap()

    kb = Kb(nc)
    C = Consts(kb, nc)
    kb.set_setup_barrier()
    sc = Scratch(kb, TC, attn=False, cmb=False, rows16=False, tt2=False)
    hrow_f = Row(kb, "l1hf", 16, 2304, F32)
    negB = Row(kb, "l1nb", 16, TC, BF16)
    kcol = Row(kb, "l1kc", 16, 1, F32)
    pt = [TBufSimple(kb, f"l1pt{i}", [128, 256], BF16) for i in range(2)]
    drow = Row(kb, "l1dr", 1, 512, F32)
    Z = FM(kb, "zext", NCH, EXTW, BF16)
    KE = FM(kb, "ke", NCH, 2304, BF16)
    VE = TM(kb, "ve", NKV, NH * VW, BF16)
    Q = FM(kb, "q", NCH, TC, BF16)
    W = FM(kb, "w", NCH, DM, BF16)
    ck_ = [kb.sb(f"ck{i}", [128, 2304], BF16) for i in range(2)]
    rotb = kb.sb("rotb", [128, 128], BF16)
    LWM = FM(kb, "lwm_sb", 4, 256, BF16)
    qsc = kb.sb("qsc", [128, 1], F32)
    ksc = kb.sb("ksc", [128, 1], F32)
    kb.dma("G", ck_[0][:, 0:TC], xqc, reads=[], writes=[("ck0", 0, 0)])
    kb.dma("G", ck_[1][:, 0:TC], xqs, reads=[], writes=[("ck1", 0, 0)])
    kb.dma("G", rotb[:, :], rot64, reads=[], writes=[("c_rot", 0, 0)])
    kb.dma("Y", qsc[:, :], qs_d, reads=[], writes=[("wrow", 0, 0)])
    kb.dma("Y", ksc[:, :], ks_d, reads=[], writes=[("wrow", 0, 0)])
    load_fm(kb, LWM, lwm, 4 * 128, 256, dt_cast=True, tag="lwm")
    load_fm(kb, Z, zE, DM, EXTW, dt_cast=True, tag="lz")

    def sh_rhs(c, t0, t1):
        if c < 4:
            return Z.c(c, t0, t1)
        return Z.c(c, t0 - 1, t1 - 1)

    def sh_keys(c, t0, t1):
        return Z.keys(c, max(0, t0 - 1), t1)

    def proj_from_shift(out, wfm, base, width, tm=False, colmap=None,
                        psum_ids=(0, 1), tag="p"):
        nt2 = width // TT
        if not tm:
            for m in range(NCH):
                for it in range(nt2):
                    t0 = base + it * TT
                    ps = sc.pb[psum_ids[(m * nt2 + it) % 2]]
                    for c in range(NCH):
                        kb.op("T", lambda E, p=ps.h[:, 0:TT],
                              l=wfm.c(c, m * 128, (m + 1) * 128),
                              r=sh_rhs(c, t0, t0 + TT), s=(c == 0),
                              e2=(c == NCH - 1):
                              E.matmul(p, l, r, start=s, stop=e2),
                              reads=wfm.keys(c) + sh_keys(c, t0, t0 + TT),
                              writes=ps.keys(), tag=tag + "mm")
                    kb.op("V", lambda E, a=out.c(m, it * TT, (it + 1) * TT),
                          p=ps.h[:, 0:TT]: E.tensor_copy(a, p),
                          reads=ps.keys(),
                          writes=out.keys(m, it * TT, (it + 1) * TT),
                          tag=tag + "ev")
        else:
            ntc2 = width // 128
            for tc in range(ntc2):
                t0 = base + tc * 128
                for nb in range(2):
                    n0, n1 = nb * TT, (nb + 1) * TT
                    ps = sc.pb[psum_ids[(tc * 2 + nb) % 2]]
                    for c in range(NCH):
                        kb.op("T", lambda E, p=ps.h[:, 0:TT],
                              l=sh_rhs(c, t0, t0 + 128),
                              r=wfm.c(c, n0, n1), s=(c == 0),
                              e2=(c == NCH - 1):
                              E.matmul(p, l, r, start=s, stop=e2),
                              reads=wfm.keys(c, n0, n1) +
                              sh_keys(c, t0, t0 + 128),
                              writes=ps.keys(), tag=tag + "mm")
                    dst = colmap(tc, n0, n1)
                    kb.op("V", lambda E, a=dst,
                          p=ps.h[:, 0:TT].rearrange("p (h w) -> p h w", w=64):
                          E.tensor_copy(a, p),
                          reads=ps.keys(), writes=out.keys(tc),
                          tag=tag + "ev")

    load_weight_fm(kb, W, Wq_d, DM, DM, tag="wq")
    proj_from_shift(Q, W, 257, TC, tag="q")
    emit_headwise_rms(kb, C, sc, Q, NCH, 16, qsc, sq_scratch=KE, Tw=TC,
                      l2mode=True, rows=hrow_f, tag="ql2")
    emit_head_norms(kb, C, sc, Q, NCH, 16, negB, sq_scratch=KE,
                    Tw=TC, tag="qn")
    emit_rope_tables(kb, C, sc, Q, ck_[0], ck_[1], rotb[:, :], NCH, Tw=TC,
                     ckeys=("ck0", "ck1", "c_rot"), tag="rq")
    kb.alias("w2", [("w", c, i) for c in range(NCH) for i in range(2)])
    W2 = FM(kb, "w2", NCH, DM, BF16, back=W.h)
    load_weight_fm(kb, W2, Wk_d, DM, DM, tag="wk")
    proj_from_shift(KE, W2, 1, 2304, psum_ids=(2, 3), tag="k")
    kb.alias("w3", [("w2", c, i) for c in range(NCH) for i in range(2)])
    W3 = FM(kb, "w3", NCH, DM, BF16, back=W.h)
    load_weight_fm(kb, W3, Wv_d, DM, DM, tag="wv")
    proj_from_shift(VE, W3, 1, 2304, tm=True, colmap=v_colmap(VE),
                    psum_ids=(4, 5), tag="v")
    vview = VE.h[:, :].rearrange("p (tc h w) -> p tc h w", h=16, w=VW)
    kb.dma("G", ck_[0][:, :], xkc, reads=[("ck0", 0, 0)],
           writes=[("ck0", 0, 0)])
    kb.dma("G", ck_[1][:, :], xks, reads=[("ck1", 0, 0)],
           writes=[("ck1", 0, 0)])
    emit_headwise_rms(kb, C, sc, KE, NCH, 16, ksc,
                      sq_scratch=FMOff(Z, 0), Tw=2304, l2mode=True,
                      rows=hrow_f, tag="kl2")
    emit_head_norms(kb, C, sc, KE, NCH, 16, hrow_f, sq_scratch=FMOff(Z, 0),
                    Tw=2304, tag="kn2")
    kb.op("V", lambda E, a=kcol.h[0:16, 0:1], b=hrow_f.h[0:16, 0:2304]:
          E.reduce_max(a, b, axis=AX.X),
          reads=hrow_f.keys(), writes=kcol.keys(), force=True, tag="kmax")
    kb.op("V", lambda E, a=negB.h[0:16, 0:TC], b=negB.h[0:16, 0:TC],
          s=kcol.h[0:16, 0:1]:
          E.tensor_scalar(a, b, s, -1.0, ALU.mult, ALU.mult),
          reads=negB.keys() + kcol.keys(), writes=negB.keys(), tag="negB")
    emit_rope_tables(kb, C, sc, KE, ck_[0], ck_[1], rotb[:, :], NCH, Tw=2304,
                     ckeys=("ck0", "ck1", "c_rot"), tag="rk")
    vob = ck_[0]
    for tc in range(NKV):
        kb.dma("G", vob[:, tc * 16:(tc + 1) * 16],
               vones[tc * 128:(tc + 1) * 128, :],
               reads=[], writes=[("ck0", 0, 0), ("vob", 0, 0)], tag="vol")
    for tc in range(NKV):
        kb.op("V", lambda E, a=vview[:, tc, :, 64:65],
              b=vob[:, tc * 16:(tc + 1) * 16].unsqueeze(2):
              E.tensor_copy(a, b),
              reads=[("vob", 0, 0), (VE.name, tc, 0)],
              writes=[(VE.name, tc, 0)], tag="vo")
    kb.alias("ctx", [(Z.name, c, i) for c in range(NCH) for i in range(5)])
    CTX = FM(kb, "ctx", NCH, TC, BF16, back=Z.h)
    for h in range(16):
        hc, hp = h // 2, (h % 2) * 64
        for w in range(8):
            q0 = w * 256
            ppv = sc.pb[(h + w) % 2]
            for j in range(4):
                ck2 = 2 * w + j
                pss = sc.pb[2 + (j % 2)]
                kb.op("T", lambda E, p=pss.h[:, 0:256],
                      l=KE.c(hc, ck2 * 128, (ck2 + 1) * 128, hp, hp + 64),
                      r=Q.c(hc, q0, q0 + 256, hp, hp + 64):
                      E.matmul(p, l, r, start=True, stop=False),
                      reads=[(KE.name, hc, ck2 // 4)] +
                      Q.keys(hc, q0, q0 + 256),
                      writes=pss.keys(), tag="s")
                kb.op("T", lambda E, p=pss.h[:, 0:256],
                      l=C.sel128[:, h * 128:(h + 1) * 128],
                      r=negB.h[0:16, q0:q0 + 256]:
                      E.matmul(p, l, r, start=False, stop=False),
                      reads=negB.keys(), writes=pss.keys(), tag="b")
                kb.op("T", lambda E, p=pss.h[:, 0:256],
                      l=C.eye_bf[:, :], r=LWM.c(j, 0, 256):
                      E.matmul(p, l, r, start=False, stop=True),
                      reads=LWM.keys(j), writes=pss.keys(), tag="am")
                pt_ = pt[j % 2]
                kb.op("S", lambda E, a=pt_.h[:, 0:256], p=pss.h[:, 0:256]:
                      E.activation(a, p, AF.Exp, scale=8.0),
                      reads=pss.keys(), writes=pt_.keys(), tag="e")
                kb.op("T", lambda E, p=ppv.h[0:65, 0:256],
                      l=VE.h[:, ck2 * (16 * VW) + h * VW:
                             ck2 * (16 * VW) + h * VW + VW],
                      r=pt_.h[:, 0:256], s=(j == 0), e2=(j == 3):
                      E.matmul(p, l, r, start=s, stop=e2),
                      reads=[(VE.name, ck2, 0)] + pt_.keys(),
                      writes=ppv.keys(), tag="pv")
            dsl = drow.h[0:1, (w % 2) * 256:(w % 2) * 256 + 256]
            kb.op("V", lambda E, a=dsl, p=ppv.h[64:65, 0:256]:
                  E.reciprocal(a, p),
                  reads=ppv.keys(), writes=[(drow.name, 0, w % 2)],
                  force=True, tag="rc")
            pbc = sc.pb[4 + (w % 2)]
            kb.op("T", lambda E, p=pbc.h[0:64, 0:256], o=C.ones_f[0:1, 0:64],
                  r=dsl:
                  E.matmul(p, o, r, start=True, stop=True),
                  reads=[(drow.name, 0, w % 2)], writes=pbc.keys(),
                  tag="bc")
            kb.op("S", lambda E, a=sc.tt_f32.h[0:64, 0:256],
                  p=ppv.h[0:64, 0:256]: E.copy(a, p),
                  reads=ppv.keys(), writes=sc.tt_f32.keys(), tag="cp")
            kb.op("V", lambda E, a=CTX.c(hc, q0, q0 + 256, hp, hp + 64),
                  b=sc.tt_f32.h[0:64, 0:256], p2=pbc.h[0:64, 0:256]:
                  E.tensor_mul(a, b, p2),
                  reads=sc.tt_f32.keys() + pbc.keys(),
                  writes=[("ctx", hc, q0 // TT, hp)], tag="dv")
    kb.alias("w4", [("w3", c, i) for c in range(NCH) for i in range(2)])
    W4 = FM(kb, "w4", NCH, DM, BF16, back=W.h)
    load_weight_fm(kb, W4, Wo_d, DM, DM, tag="wo")

    ctxkeys = [("ctx", c, i, hp) for c in range(NCH) for i in range(4)
               for hp in (0, 64)]
    for m in range(NCH):
        for it in range(4):
            t0, t1 = it * TT, (it + 1) * TT
            ps = sc.pb[6 + (it % 2)]
            for c in range(NCH):
                kb.op("T", lambda E, p=ps.h[:, 0:TT],
                      l=W4.c(c, m * 128, (m + 1) * 128),
                      r=CTX.c(c, t0, t1), s=(c == 0), e2=(c == NCH - 1):
                      E.matmul(p, l, r, start=s, stop=e2),
                      reads=W4.keys(c) + [("ctx", c, it, 0), ("ctx", c, it, 64)],
                      writes=ps.keys(), tag="omm")
            zt = sc.tt_f32
            kb.dma("Y", zt.h[:, 0:TT],
                   zE[m * 128:(m + 1) * 128, 257 + t0:257 + t1],
                   reads=zt.keys(), writes=zt.keys(), tag="zl")
            kb.op("V", lambda E, a=zt.h[:, 0:TT], b=zt.h[:, 0:TT],
                  p=ps.h[:, 0:TT]: E.tensor_add(a, b, p),
                  reads=ps.keys() + zt.keys(), writes=zt.keys(), tag="za")
            kb.dma("Y", zout[m * 128:(m + 1) * 128, t0:t1], zt.h[:, 0:TT],
                   reads=zt.keys(), writes=[], tag="zs")
    with nc.Block() as block:
        _finish(nc, kb, block)
    return nc

def build_l1f_ffn():
    nc = _new_nc()
    zE = nc.dram_tensor("zE", [DM, TC + 128], F32, kind="ExternalInput").ap()
    Wa_d = nc.dram_tensor("Wa", [DM, 2816], F32, kind="ExternalInput").ap()
    Wg_d = nc.dram_tensor("Wg", [DM, 2816], F32, kind="ExternalInput").ap()
    W2_d = nc.dram_tensor("W2", [2816, DM], F32, kind="ExternalInput").ap()
    b1a_d = nc.dram_tensor("b1a", [128, 22], F32, kind="ExternalInput").ap()
    b1g_d = nc.dram_tensor("b1g", [128, 22], F32, kind="ExternalInput").ap()
    b2_d = nc.dram_tensor("b2", [128, NCH], F32, kind="ExternalInput").ap()
    lng_d = nc.dram_tensor("lng", [128, NCH], F32, kind="ExternalInput").ap()
    lnb_d = nc.dram_tensor("lnb", [128, NCH], F32, kind="ExternalInput").ap()
    zout = nc.dram_tensor("zout", [DM, TC], F32, kind="ExternalOutput").ap()

    kb = Kb(nc)
    C = Consts(kb, nc)
    kb.set_setup_barrier()
    sc = Scratch(kb, TC, attn=False, cmb=False)
    SH = FM(kb, "sh", NCH, TC, BF16)
    XN = FM(kb, "xn", NCH, TC, BF16)
    AG = FM(kb, "ag", 4, TC, BF16)
    MIX = FM(kb, "mix", NCH, TC, BF16)
    WA = FM(kb, "wa", NCH, TT, BF16)
    WG = FM(kb, "wg", NCH, TT, BF16)
    W2s = FM(kb, "w2g", 4, DM, BF16)
    GG = FM(kb, "gg", 4, TC, BF16)
    b1ac = kb.sb("b1ac", [128, 22], F32)
    b1gc = kb.sb("b1gc", [128, 22], F32)
    b2c = kb.sb("b2c", [128, NCH], F32)
    lngc = kb.sb("lngc", [128, NCH], F32)
    lnbc = kb.sb("lnbc", [128, NCH], F32)
    for dst, srcd in [(b1ac, b1a_d), (b1gc, b1g_d), (b2c, b2_d),
                      (lngc, lng_d), (lnbc, lnb_d)]:
        kb.dma("Y", dst[:, :], srcd, reads=[], writes=[("wrow", 0, 0)])
    for c in range(NCH):
        off = 128 if c < 4 else 127
        kb.dma("G", SH.c(c, 0, TC), zE[c * 128:(c + 1) * 128, off:off + TC],
               reads=[], writes=SH.keys(c), tag="lsh")
    emit_ln_fm(kb, C, sc, SH, XN, xsq=XN, xb=None,
               gb=(lngc[:, :], lnbc[:, :]), tag="ln")
    ngrp = 6
    first = True
    for g4 in range(ngrp):
        f0 = g4 * 4
        nf = min(4, 22 - f0)
        for c in range(NCH):
            kb.dma("G", WA.c(c, 0, nf * 128),
                   Wa_d[c * 128:(c + 1) * 128, f0 * 128:(f0 + nf) * 128],
                   reads=[], writes=WA.keys(c), tag="wal")
            kb.dma("G", WG.c(c, 0, nf * 128),
                   Wg_d[c * 128:(c + 1) * 128, f0 * 128:(f0 + nf) * 128],
                   reads=[], writes=WG.keys(c), tag="wgl")
        for c2 in range(nf):
            kb.dma("G", W2s.c(c2, 0, DM),
                   W2_d[(f0 + c2) * 128:(f0 + c2 + 1) * 128, :],
                   reads=[], writes=W2s.keys(c2), tag="w2l")
        for m in range(nf):
            for it in range(4):
                t0, t1 = it * TT, (it + 1) * TT
                psg = sc.pb[(m * 4 + it) % 2]
                for c in range(NCH):
                    kb.op("T", lambda E, p=psg.h[:, 0:TT],
                          l=WG.c(c, m * 128, (m + 1) * 128),
                          r=XN.c(c, t0, t1), s=(c == 0), e2=(c == NCH - 1):
                          E.matmul(p, l, r, start=s, stop=e2),
                          reads=WG.keys(c) + XN.keys(c, t0, t1),
                          writes=psg.keys(), tag="gmm")
                kb.op("S", lambda E, a=GG.c(m, t0, t1), p=psg.h[:, 0:TT],
                      b=b1gc[:, f0 + m:f0 + m + 1]:
                      E.activation(a, p, AF.Gelu, bias=b),
                      reads=psg.keys() + [("wrow", 0, 0)],
                      writes=GG.keys(m, t0, t1), tag="gel")
        for m in range(nf):
            for it in range(4):
                t0, t1 = it * TT, (it + 1) * TT
                psa = sc.pb[2 + (m * 4 + it) % 2]
                for c in range(NCH):
                    kb.op("T", lambda E, p=psa.h[:, 0:TT],
                          l=WA.c(c, m * 128, (m + 1) * 128),
                          r=XN.c(c, t0, t1), s=(c == 0), e2=(c == NCH - 1):
                          E.matmul(p, l, r, start=s, stop=e2),
                          reads=WA.keys(c) + XN.keys(c, t0, t1),
                          writes=psa.keys(), tag="amm")
                kb.op("S", lambda E, a=sc.tt_f32.h[:, 0:TT],
                      p=psa.h[:, 0:TT], b=b1ac[:, f0 + m:f0 + m + 1]:
                      E.activation(a, p, AF.Identity, bias=b),
                      reads=psa.keys() + [("wrow", 0, 0)],
                      writes=sc.tt_f32.keys(), tag="aev")
                kb.op("V", lambda E, a=AG.c(m, t0, t1),
                      b=sc.tt_f32.h[:, 0:TT], m2=GG.c(m, t0, t1):
                      E.tensor_mul(a, b, m2),
                      reads=sc.tt_f32.keys() + GG.keys(m, t0, t1),
                      writes=AG.keys(m, t0, t1), tag="act")
        for m in range(NCH):
            for it in range(4):
                t0, t1 = it * TT, (it + 1) * TT
                ps3 = sc.pb[4 + (m * 4 + it) % 2]
                for c2 in range(nf):
                    kb.op("T", lambda E, p=ps3.h[:, 0:TT],
                          l=W2s.c(c2, m * 128, (m + 1) * 128),
                          r=AG.c(c2, t0, t1), s=(c2 == 0), e2=(c2 == nf - 1):
                          E.matmul(p, l, r, start=s, stop=e2),
                          reads=W2s.keys(c2) + AG.keys(c2, t0, t1),
                          writes=ps3.keys(), tag="ymm")
                if first:
                    kb.op("V", lambda E, a=MIX.c(m, t0, t1),
                          p=ps3.h[:, 0:TT]: E.tensor_copy(a, p),
                          reads=ps3.keys(), writes=MIX.keys(m, t0, t1),
                          tag="mx0")
                else:
                    kb.op("V", lambda E, a=MIX.c(m, t0, t1),
                          p=ps3.h[:, 0:TT]: E.tensor_add(a, a, p),
                          reads=ps3.keys() + MIX.keys(m, t0, t1),
                          writes=MIX.keys(m, t0, t1), tag="mxa")
        first = False
    for m in range(NCH):
        for it in range(4):
            t0, t1 = it * TT, (it + 1) * TT
            zt = sc.tt2
            kb.dma("Y", zt.h[:, 0:TT],
                   zE[m * 128:(m + 1) * 128, 128 + t0:128 + t1],
                   reads=zt.keys(), writes=zt.keys(), tag="zl")
            kb.op("V", lambda E, a=zt.h[:, 0:TT], b=zt.h[:, 0:TT],
                  m2=MIX.c(m, t0, t1): E.tensor_add(a, b, m2),
                  reads=MIX.keys(m, t0, t1) + zt.keys(), writes=zt.keys(),
                  tag="za")
            kb.op("V", lambda E, a=zt.h[:, 0:TT], b=zt.h[:, 0:TT],
                  s=b2c[:, m:m + 1]:
                  E.tensor_scalar(a, b, s, None, ALU.add),
                  reads=zt.keys() + [("wrow", 0, 0)], writes=zt.keys(),
                  tag="zb")
            kb.dma("Y", zout[m * 128:(m + 1) * 128, t0:t1], zt.h[:, 0:TT],
                   reads=zt.keys(), writes=[], tag="zs")
    with nc.Block() as block:
        _finish(nc, kb, block)
    return nc


def build_l2a_p1():
    nc = _new_nc()
    xT = nc.dram_tensor("xT", [DM, TC], F32, kind="ExternalInput").ap()
    Wq_d = nc.dram_tensor("Wq", [DM, DM], F32, kind="ExternalInput").ap()
    Wk_d = nc.dram_tensor("Wk", [DM, DM], F32, kind="ExternalInput").ap()
    Wv_d = nc.dram_tensor("Wv", [DM, DM], F32, kind="ExternalInput").ap()
    bq_d = nc.dram_tensor("bq", [128, NCH], F32, kind="ExternalInput").ap()
    bk_d = nc.dram_tensor("bk", [128, NCH], F32, kind="ExternalInput").ap()
    bv_d = nc.dram_tensor("bv", [1, DM], F32, kind="ExternalInput").ap()
    q_d = nc.dram_tensor("q_d", [DM, TC], BF16, kind="ExternalOutput").ap()
    k_d = nc.dram_tensor("k_d", [DM, TC], BF16, kind="ExternalOutput").ap()
    v_d = nc.dram_tensor("v_d", [TC, DM], BF16, kind="ExternalOutput").ap()
    ql_d = nc.dram_tensor("ql_d", [DM, 32], F32, kind="ExternalOutput").ap()
    kl_d = nc.dram_tensor("kl_d", [DM, 32], F32, kind="ExternalOutput").ap()
    km_d = nc.dram_tensor("km_d", [16, 1], F32, kind="ExternalOutput").ap()

    kb = Kb(nc)
    C = Consts(kb, nc)
    kb.set_setup_barrier()
    sc = Scratch(kb, TC)
    SX = FM(kb, "sx", NCH, TC, BF16)
    XN = FM(kb, "xn", NCH, TC, BF16)
    K = FM(kb, "slabk", NCH, TC, BF16)
    V = TM(kb, "slabv", 16, DM, BF16)
    W = FM(kb, "w", NCH, DM, BF16)
    qbc = kb.sb("qbc", [128, NCH], F32)
    kbc = kb.sb("kbc", [128, NCH], F32)
    vbr = kb.sb("vbr", [1, DM], BF16)
    lrow = kb.sb("lrow", [128, 64], F32)
    kb.dma("Y", qbc[:, :], bq_d, reads=[], writes=[("wrow", 0, 0)])
    kb.dma("Y", kbc[:, :], bk_d, reads=[], writes=[("wrow", 0, 0)])
    kb.dma("G", vbr[:, :], bv_d, reads=[], writes=[("wrow", 0, 0)])

    load_fm(kb, SX, xT, DM, TC, dt_cast=True, tag="lx")
    emit_ln_fm(kb, C, sc, SX, XN, xsq=XN, xb=None, tag="ln")
    load_weight_fm(kb, W, Wq_d, DM, DM, tag="wq")
    emit_proj_fm(kb, C, sc, SX, XN, W, NCH, NCH, bias_col=qbc[:, :],
                 psum_ids=(0, 1), tag="q")
    kb.alias("w2", [("w", c, i) for c in range(NCH) for i in range(2)])
    W2 = FM(kb, "w2", NCH, DM, BF16, back=W.h)
    load_weight_fm(kb, W2, Wk_d, DM, DM, tag="wk")
    emit_proj_fm(kb, C, sc, K, XN, W2, NCH, NCH, bias_col=kbc[:, :],
                 psum_ids=(2, 3), tag="k")
    kb.alias("w3", [("w2", c, i) for c in range(NCH) for i in range(2)])
    W3 = FM(kb, "w3", NCH, DM, BF16, back=W.h)
    load_weight_fm(kb, W3, Wv_d, DM, DM, tag="wv")
    emit_proj_tm(kb, C, sc, V, XN, W3, NCH, DM, bias_row=vbr[:, :],
                 psum_ids=(4, 5), tag="v")
    for src_fm, dst in [(SX, ql_d), (K, kl_d)]:
        for c in range(NCH):
            kb.op("V", lambda E, a=lrow[:, 0:32],
                  b=bass.AP(src_fm.h, c * TC * 0 + 0, [[0, 0]]) if False else
                  src_fm.c(c, 0, TC).rearrange("p (l s) -> p l s", s=64):
                  E.reduce_sum(a, b, axis=AX.X),
                  reads=src_fm.keys(c), writes=[("lrow", 0, 0)], force=True,
                  tag="lm")
            kb.op("V", lambda E, a=lrow[:, 32:64], b=lrow[:, 0:32]:
                  E.tensor_scalar_mul(a, b, 1.0 / 64),
                  reads=[("lrow", 0, 0)], writes=[("lrow", 0, 0)],
                  force=True, tag="lms")
            kb.dma("Y", dst[c * 128:(c + 1) * 128, :], lrow[:, 32:64],
                   reads=[("lrow", 0, 0)], writes=[("lrow", 0, 0)],
                   tag="lout")
    emit_head_norms(kb, C, sc, K, NCH, 16, sc.hrow_a, sq_scratch=XN, tag="kn")
    kb.op("V", lambda E, a=sc.hcol_a.h[0:16, 0:1], b=sc.hrow_a.h[0:16, 0:TC]:
          E.reduce_max(a, b, axis=AX.X),
          reads=sc.hrow_a.keys(), writes=sc.hcol_a.keys(), force=True,
          tag="kmax")
    store_fm(kb, q_d, SX, DM, TC, tag="oq")
    store_fm(kb, k_d, K, DM, TC, tag="ok")
    store_tm(kb, v_d, V, 16, DM, tag="ov")
    kb.dma("Y", km_d, sc.hcol_a.h[0:16, 0:1], reads=sc.hcol_a.keys(),
           writes=[], tag="okm")
    with nc.Block() as block:
        _finish(nc, kb, block)
    return nc


def build_l2a_p2():
    nc = _new_nc()
    k_dd = nc.dram_tensor("k_d", [DM, TC], BF16, kind="ExternalInput").ap()
    v_dd = nc.dram_tensor("v_d", [TC, DM], BF16, kind="ExternalInput").ap()
    qlA = nc.dram_tensor("qlA", [DM, 64], F32, kind="ExternalInput").ap()
    klA = nc.dram_tensor("klA", [DM, 64], F32, kind="ExternalInput").ap()
    km2 = nc.dram_tensor("km2", [16, 2], F32, kind="ExternalInput").ap()
    convA = nc.dram_tensor("convA", [16 * 5 * 128, TT], F32,
                           kind="ExternalInput").ap()
    vh_lo = nc.dram_tensor("vh_lo", [32, DM], BF16, kind="ExternalInput").ap()
    vh_hi = nc.dram_tensor("vh_hi", [32, DM], BF16, kind="ExternalInput").ap()
    k2_d = nc.dram_tensor("k2_d", [64, 16 * 64], BF16,
                          kind="ExternalOutput").ap()
    cs_d = nc.dram_tensor("cs_d", [1, 16], F32, kind="ExternalOutput").ap()
    n_d = nc.dram_tensor("n_d", [64, 16 * 64], F32,
                         kind="ExternalOutput").ap()
    d3_d = nc.dram_tensor("d3_d", [64, 16], F32, kind="ExternalOutput").ap()
    conv_d = nc.dram_tensor("conv_d", [DM, TC], BF16,
                            kind="ExternalOutput").ap()

    kb = Kb(nc)
    C = Consts(kb, nc)
    kb.set_setup_barrier()
    sc = Scratch(kb, TC, attn=False, cmb=False)
    K = FM(kb, "slabk", NCH, TC, BF16)
    VT = TM(kb, "vt", 16, DM, BF16)
    VX = TM(kb, "vx", 17, DM, BF16)
    QL = FM(kb, "ql", NCH, 64, BF16)
    KL = FM(kb, "kl", NCH, 64, BF16)
    P3 = TBufSimple(kb, "p3", [128, 1024], BF16)
    P3F = TBufSimple(kb, "p3f", [128, 1024], F32)
    S2 = TBufSimple(kb, "s2", [64, 1024], F32)
    K2 = TBufSimple(kb, "k2", [64, 1024], BF16)
    CB = FM(kb, "cb", 5, TT, BF16)
    CONV = FM(kb, "conv", NCH, TC, BF16)
    negb3 = Row(kb, "nb3", 16, 64, BF16)
    km_sb = kb.sb("kmsb", [16, 2], F32)
    s2r = kb.sb("s2r", [64, 16], F32)
    csr = kb.sb("csr", [1, 16], F32)
    kb.dma("Y", km_sb[:, :], km2, reads=[], writes=[("kmrow", 0, 0)])
    load_fm(kb, K, k_dd, DM, TC, tag="lk")
    load_tm(kb, VT, v_dd, 16, DM, tag="lvt")
    load_fm(kb, QL, qlA, DM, 64, dt_cast=True, tag="lql")
    load_fm(kb, KL, klA, DM, 64, dt_cast=True, tag="lkl")
    kb.op("V", lambda E, a=VX.c(0, 0, DM, 0, 64): E.memset(a, 0.0),
          reads=[], writes=[(VX.name, 0, 0)], force=True, tag="vz0")
    kb.op("V", lambda E, a=VX.c(16, 0, DM, 64, 128): E.memset(a, 0.0),
          reads=[], writes=[(VX.name, 16, 0)], force=True, tag="vz1")
    kb.dma("Y", VX.c(0, 0, DM, 32, 64), vh_lo, reads=[],
           writes=[(VX.name, 0, 0)], tag="vhl")
    kb.dma("Y", VX.c(16, 0, DM, 64, 96), vh_hi, reads=[],
           writes=[(VX.name, 16, 0)], tag="vhh")
    for tcx in range(17):
        t0 = tcx * 128 - 64
        for hb in (0, 64):
            tt0 = t0 + hb
            if tt0 < 0 or tt0 >= TC:
                continue
            kb.dma("Y", VX.c(tcx, 0, DM, hb, hb + 64),
                   v_dd[tt0:tt0 + 64, :], reads=[],
                   writes=[(VX.name, tcx, 0)], tag="vx")
    kb.op("V", lambda E, a=sc.hcol_a.h[0:16, 0:1], b=km_sb[0:16, 0:2]:
          E.reduce_max(a, b, axis=AX.X),
          reads=[("kmrow", 0, 0)], writes=sc.hcol_a.keys(), force=True,
          tag="gkm")
    emit_head_norms(kb, C, sc, QL, NCH, 16, sc.hrow_a, sq_scratch=CONV,
                    Tw=64, tag="qln")
    kb.op("V", lambda E, a=negb3.h[0:16, 0:64], b=sc.hrow_a.h[0:16, 0:64],
          s=sc.hcol_a.h[0:16, 0:1]:
          E.tensor_scalar(a, b, s, -1.0, ALU.mult, ALU.mult),
          reads=sc.hrow_a.keys() + sc.hcol_a.keys(), writes=negb3.keys(),
          tag="nb3")
    for ck in range(16):
        for h in range(16):
            hc, hp = h // 2, (h % 2) * 64
            ps = sc.pb[2 + (h % 2)]
            kb.op("T", lambda E, p=ps.h[:, 0:64],
                  l=K.c(hc, ck * 128, (ck + 1) * 128, hp, hp + 64),
                  r=QL.c(hc, 0, 64, hp, hp + 64):
                  E.matmul(p, l, r, start=True, stop=False),
                  reads=[(K.name, hc, ck)] + QL.keys(hc),
                  writes=ps.keys(), tag="s3")
            kb.op("T", lambda E, p=ps.h[:, 0:64],
                  l=C.sel128[:, h * 128:(h + 1) * 128],
                  r=negb3.h[0:16, 0:64]:
                  E.matmul(p, l, r, start=False, stop=True),
                  reads=negb3.keys(), writes=ps.keys(), tag="b3")
            kb.op("V", lambda E, a=P3F.h[:, h * 64:(h + 1) * 64],
                  p=ps.h[:, 0:64]: E.tensor_copy(a, p),
                  reads=ps.keys(), writes=[("p3f", 0, h)], tag="ev3")
        kb.op("S", lambda E, a=P3.h[:, 0:1024], b=P3F.h[:, 0:1024]:
              E.activation(a, b, AF.Exp, scale=0.125),
              reads=[("p3f", 0, h2) for h2 in range(16)],
              writes=[("p3", 0, 0)], tag="e3")
        for h in range(16):
            pn = sc.pb[4 + h // 8]
            kb.op("T", lambda E,
                  p=pn.h[0:64, (h % 8) * 64:(h % 8) * 64 + 64],
                  l=P3.h[:, h * 64:(h + 1) * 64],
                  r=VT.c(ck, h * 64, (h + 1) * 64), s=(ck == 0):
                  E.matmul(p, l, r, start=s, stop=True,
                           skip_group_check=True),
                  reads=[("p3", 0, 0), (VT.name, ck, 0)],
                  writes=pn.keys(), tag="nmm")
        ps6 = sc.pb[6]
        for h in range(16):
            kb.op("T", lambda E, p=ps6.h[0:64, h * 4:h * 4 + 1],
                  l=P3.h[:, h * 64:(h + 1) * 64],
                  r=C.ones_bf[0:128, 0:1], s=(ck == 0):
                  E.matmul(p, l, r, start=s, stop=True,
                           skip_group_check=True),
                  reads=[("p3", 0, 0)], writes=ps6.keys(), tag="dmm")
    for b in range(2):
        kb.op("V", lambda E, a=NSm(b) if False else None: None)             if False else None
    NSb = TBufSimple(kb, "nsb", [64, 512], F32)
    for b in range(2):
        kb.op("V", lambda E, a=NSb.h[0:64, 0:512],
              p=sc.pb[4 + b].h[0:64, 0:512]: E.tensor_copy(a, p),
              reads=sc.pb[4 + b].keys(), writes=[("nsb", 0, 0)],
              tag="nev")
        kb.dma("Y", n_d[0:64, b * 512:(b + 1) * 512], NSb.h[0:64, 0:512],
               reads=[("nsb", 0, 0)], writes=[("nsb", 0, 0)], tag="nout")
    kb.op("V", lambda E, a=s2r[0:64, 0:16],
          p=sc.pb[6].h[0:64, 0:64].rearrange("p (h x) -> p h x", x=4)[:, :, 0]:
          E.tensor_copy(a, p),
          reads=sc.pb[6].keys(), writes=[("s2r", 0, 0)], tag="d3ev")
    kb.dma("Y", d3_d, s2r[0:64, 0:16], reads=[("s2r", 0, 0)], writes=[],
           tag="d3out")
    for h in range(16):
        hc, hp = h // 2, (h % 2) * 64
        ps = sc.pb[2 + (h % 2)]
        kb.op("T", lambda E, p=ps.h[0:64, 0:64],
              l=QL.c(hc, 0, 64, hp, hp + 64),
              r=KL.c(hc, 0, 64, hp, hp + 64):
              E.matmul(p, l, r, start=True, stop=True),
              reads=QL.keys(hc) + KL.keys(hc), writes=ps.keys(), tag="s2mm")
        kb.op("V", lambda E, a=S2.h[0:64, h * 64:(h + 1) * 64],
              p=ps.h[0:64, 0:64]: E.tensor_copy(a, p),
              reads=ps.keys(), writes=[("s2", 0, h)], tag="s2ev")
    s2keys = [("s2", 0, h) for h in range(16)]
    kb.op("V", lambda E, a=s2r[0:64, 0:16],
          b=S2.h[0:64, 0:1024].rearrange("p (h x) -> p h x", x=64):
          E.reduce_max(a, b, axis=AX.X, negate=True),
          reads=s2keys, writes=[("s2r", 0, 0)], force=True, tag="s2m")
    for h in range(16):
        kb.op("V", lambda E, a=S2.h[0:64, h * 64:(h + 1) * 64],
              b=S2.h[0:64, h * 64:(h + 1) * 64], s=s2r[0:64, h:h + 1]:
              E.tensor_scalar(a, b, s, None, ALU.add),
              reads=s2keys + [("s2r", 0, 0)], writes=[("s2", 0, h)],
              force=True, tag="s2s")
    kb.op("S", lambda E, a=K2.h[0:64, 0:1024], b=S2.h[0:64, 0:1024]:
          E.activation(a, b, AF.Exp, scale=0.125),
          reads=s2keys, writes=[("k2", 0, 0)], tag="s2e")
    kb.op("V", lambda E, a=s2r[0:64, 0:16],
          b=K2.h[0:64, 0:1024].rearrange("p (h x) -> p h x", x=64):
          E.reduce_sum(a, b, axis=AX.X),
          reads=[("k2", 0, 0)], writes=[("s2r", 0, 0)], force=True,
          tag="s2sum")
    kb.op("V", lambda E, a=s2r[0:64, 0:16]: E.reciprocal(a, a),
          reads=[("s2r", 0, 0)], writes=[("s2r", 0, 0)], force=True,
          tag="s2rc")
    for h in range(16):
        kb.op("V", lambda E, a=K2.h[0:64, h * 64:(h + 1) * 64],
              b=K2.h[0:64, h * 64:(h + 1) * 64], s=s2r[0:64, h:h + 1]:
              E.tensor_scalar_mul(a, b, s),
              reads=[("k2", 0, 0), ("s2r", 0, 0)], writes=[("k2", 0, 0)],
              force=True, tag="s2n")
    kb.dma("Y", k2_d, K2.h[0:64, 0:1024], reads=[("k2", 0, 0)], writes=[],
           tag="k2out")
    for b in range(2):
        psx = sc.pb[7 - b]
        for h2 in range(8):
            h = b * 8 + h2
            kb.op("T", lambda E, p=psx.h[0:1, h2 * 64:(h2 + 1) * 64],
                  l=C.ones_bf[0:64, 0:1], r=K2.h[0:64, h * 64:(h + 1) * 64]:
                  E.matmul(p, l, r, start=True, stop=True,
                           skip_group_check=True),
                  reads=[("k2", 0, 0)], writes=psx.keys(), tag="csmm")
        kb.op("V", lambda E, a=csr[0:1, b * 8:(b + 1) * 8],
              b2=psx.h[0:1, 0:512].rearrange("p (h x) -> p h x", x=64):
              E.reduce_max(a, b2, axis=AX.X),
              reads=psx.keys(), writes=[("csr", 0, 0)], force=True,
              tag="csm")
    kb.dma("Y", cs_d, csr[0:1, 0:16], reads=[("csr", 0, 0)], writes=[],
           tag="csout")
    for h in range(16):
        for j in range(5):
            kb.dma("G", CB.c(j, 0, TT),
                   convA[(h * 5 + j) * 128:(h * 5 + j + 1) * 128, :],
                   reads=CB.keys(j), writes=CB.keys(j), tag="cal")
        hc, hp = h // 2, (h % 2) * 64
        for tt in range(4):
            ps = sc.pb[tt % 2]
            for j in range(5):
                ckx = 4 * tt + j
                kb.op("T", lambda E, p=ps.h[0:64, 0:TT],
                      l=VX.c(ckx, h * 64, (h + 1) * 64),
                      r=CB.c(j, 0, TT), s=(j == 0), e2=(j == 4):
                      E.matmul(p, l, r, start=s, stop=e2),
                      reads=[(VX.name, ckx, 0)] + CB.keys(j),
                      writes=ps.keys(), tag="cmm")
            kb.op("V", lambda E, a=CONV.c(hc, tt * TT, (tt + 1) * TT,
                                          hp, hp + 64),
                  p=ps.h[0:64, 0:TT]: E.tensor_copy(a, p),
                  reads=ps.keys(), writes=[("conv", hc, tt, hp)],
                  tag="cev")
    store_fm(kb, conv_d, CONV, DM, TC, tag="oconv",
             read_keys=[("conv", c, i, hp) for c in range(NCH)
                        for i in range(4) for hp in (0, 64)])
    with nc.Block() as block:
        _finish(nc, kb, block)
    return nc

def build_l2a_p3():
    nc = _new_nc()
    xT = nc.dram_tensor("xT", [DM, TC], F32, kind="ExternalInput").ap()
    q_dd = nc.dram_tensor("q_d", [DM, TC], BF16, kind="ExternalInput").ap()
    klA = nc.dram_tensor("klA", [DM, 64], F32, kind="ExternalInput").ap()
    k2_d = nc.dram_tensor("k2_d", [64, 16 * 64], BF16,
                          kind="ExternalInput").ap()
    n_s = nc.dram_tensor("n_s", [64, 16 * 64], F32,
                         kind="ExternalInput").ap()
    n_p = nc.dram_tensor("n_p", [64, 16 * 64], F32,
                         kind="ExternalInput").ap()
    d3_s = nc.dram_tensor("d3_s", [64, 16], F32, kind="ExternalInput").ap()
    d3_p = nc.dram_tensor("d3_p", [64, 16], F32, kind="ExternalInput").ap()
    cs8 = nc.dram_tensor("cs8", [1, 128], F32, kind="ExternalInput").ap()
    conv_dd = nc.dram_tensor("conv_d", [DM, TC], BF16,
                             kind="ExternalInput").ap()
    xout = nc.dram_tensor("xout", [DM, TC], F32, kind="ExternalOutput").ap()

    kb = Kb(nc)
    C = Consts(kb, nc)
    kb.set_setup_barrier()
    sc = Scratch(kb, TC)
    Q = FM(kb, "slabq", NCH, TC, BF16)
    KL = FM(kb, "kl", NCH, 64, BF16)
    CONV = FM(kb, "conv", NCH, TC, BF16)
    CTX = FM(kb, "ctx", NCH, TC, BF16)
    NT = TBufSimple(kb, "nt", [64, 1024], F32)
    K2S = TBufSimple(kb, "k2s", [64, 1024], BF16)
    IV = TBufSimple(kb, "iv", [64, 1024], F32)
    IVT = TBufSimple(kb, "ivt", [64, 1024], F32)
    MH = TBufSimple(kb, "mh", [64, 1024], BF16)
    T1_ = TBufSimple(kb, "t1b", [64, 64], F32)
    T2_ = TBufSimple(kb, "t2b", [64, 64], F32)
    KVb = TBufSimple(kb, "kvb", [64, 64], F32)
    KVTb = TBufSimple(kb, "kvtb", [64, 64], F32)
    K2F = TBufSimple(kb, "k2f", [64, 64], F32)
    K2TF = TBufSimple(kb, "k2tf", [64, 64], F32)
    EY7 = TBufSimple(kb, "ey7", [64, 192], F32)
    d3r = kb.sb("d3r", [64, 16], F32)
    csr = kb.sb("csr", [1, 128], F32)
    sbc = kb.sb("sbc", [64, 1], F32)
    negb1 = Row(kb, "nb1", 16, TC, BF16)
    load_fm(kb, Q, q_dd, DM, TC, tag="lq")
    load_fm(kb, KL, klA, DM, 64, dt_cast=True, tag="lkl")
    load_fm(kb, CONV, conv_dd, DM, TC, tag="lcv")
    kb.dma("Y", csr[:, :], cs8, reads=[], writes=[("csr", 0, 0)])
    kb.dma("Y", d3r[:, :], d3_s, reads=[], writes=[("d3r", 0, 0)])
    kb.dma("G", K2S.h[0:64, :], k2_d, reads=[], writes=[("k2s", 0, 0)])
    kb.dma("Y", NT.h[0:64, :], n_s, reads=[], writes=[("nt", 0, 0)])
    d3p_sb = kb.sb("d3p", [64, 16], F32)
    ntp_sb = TBufSimple(kb, "ntp", [64, 1024], F32)
    kb.dma("Y", d3p_sb[:, :], d3_p, reads=[], writes=[("d3p", 0, 0)])
    kb.dma("Y", ntp_sb.h[0:64, :], n_p, reads=[], writes=[("ntp", 0, 0)])
    kb.op("V", lambda E, a=d3r[:, :], b=d3r[:, :], m=d3p_sb[:, :]:
          E.tensor_add(a, b, m),
          reads=[("d3r", 0, 0), ("d3p", 0, 0)], writes=[("d3r", 0, 0)],
          force=True, tag="d3a")
    kb.op("V", lambda E, a=NT.h[0:64, :], b=NT.h[0:64, :],
          m=ntp_sb.h[0:64, :]: E.tensor_add(a, b, m),
          reads=[("nt", 0, 0), ("ntp", 0, 0)], writes=[("nt", 0, 0)],
          force=True, tag="nta")
    kb.op("V", lambda E, a=d3r[:, :]: E.reciprocal(a, a),
          reads=[("d3r", 0, 0)], writes=[("d3r", 0, 0)], force=True,
          tag="d3rc")
    for h in range(16):
        kb.op("V", lambda E, a=NT.h[0:64, h * 64:(h + 1) * 64],
              b=NT.h[0:64, h * 64:(h + 1) * 64], s=d3r[:, h:h + 1]:
              E.tensor_scalar_mul(a, b, s),
              reads=[("nt", 0, 0), ("d3r", 0, 0)], writes=[("nt", 0, 0)],
              force=True, tag="k3v")
    kb.op("V", lambda E, a=csr[0:1, 0:1], b=csr[0:1, 0:128]:
          E.reduce_max(a, b, axis=AX.X),
          reads=[("csr", 0, 0)], writes=[("csr", 0, 0)], force=True,
          tag="csm")
    kb.op("V", lambda E, a=csr[0:1, 0:1], b=csr[0:1, 0:1]:
          E.reciprocal(a, b),
          reads=[("csr", 0, 0)], writes=[("csr", 0, 0)], force=True,
          tag="csrc")
    ps0 = sc.pb[0]
    kb.op("T", lambda E, p=ps0.h[0:64, 0:1], o=C.ones_f[0:1, 0:64],
          r=csr[0:1, 0:1]: E.matmul(p, o, r, start=True, stop=True),
          reads=[("csr", 0, 0)], writes=ps0.keys(), tag="sbc")
    kb.op("V", lambda E, a=sbc[:, :], p=ps0.h[0:64, 0:1]:
          E.tensor_copy(a, p),
          reads=ps0.keys(), writes=[("sbcK", 0, 0)], force=True, tag="sbcv")
    for i, vl in enumerate((7.0, 15.0, 13.0)):
        kb.op("V", lambda E, a=EY7.h[0:64, i * 64:(i + 1) * 64],
              b=C.eye_f[0:64, 0:64], v=vl: E.tensor_scalar_mul(a, b, v),
              reads=[("c_eyef", 0, 0)], writes=[("ey7", 0, 0)], force=True,
              tag="eys")
    for h in range(16):
        hk = [("k2f", 0, h)]
        kb.op("V", lambda E, a=K2F.h[0:64, 0:64],
              b=K2S.h[0:64, h * 64:(h + 1) * 64]: E.tensor_copy(a, b),
              reads=[("k2s", 0, 0)], writes=hk, force=True, tag="k2c")
        kb.op("T", lambda E, p=sc.pb[2].h[0:64, 0:64],
              b=K2F.h[0:64, 0:64]:
              E.transpose(p, b, C.eye_f[0:64, 0:64]),
              reads=hk + [("c_eyef", 0, 0)],
              writes=sc.pb[2].keys(), tag="k2t")
        kb.op("V", lambda E, a=K2TF.h[0:64, 0:64], p=sc.pb[2].h[0:64, 0:64]:
              E.tensor_copy(a, p),
              reads=sc.pb[2].keys(), writes=[("k2tf", 0, h)], force=True,
              tag="k2tc")
        kb.op("V", lambda E, a=IV.h[0:64, h * 64:(h + 1) * 64],
              b=K2TF.h[0:64, 0:64], s=sbc[:, 0:1]:
              E.tensor_scalar_mul(a, b, s),
              reads=[("k2tf", 0, h), ("sbcK", 0, 0)], writes=[("iv", 0, h)],
              force=True, tag="v0")
        kb.op("V", lambda E, a=IVT.h[0:64, h * 64:(h + 1) * 64],
              b=K2F.h[0:64, 0:64], s=sbc[:, 0:1]:
              E.tensor_scalar_mul(a, b, s),
              reads=hk + [("sbcK", 0, 0)], writes=[("ivt", 0, h)],
              force=True, tag="v0t")
        for it in range(6):
            ivk = [("iv", 0, h)]
            ivtk = [("ivt", 0, h)]
            ps = sc.pb[1]
            kb.op("T", lambda E, p=ps.h[0:64, 0:64],
                  l=K2TF.h[0:64, 0:64], r=IV.h[0:64, h * 64:(h + 1) * 64]:
                  E.matmul(p, l, r, start=True, stop=True),
                  reads=[("k2tf", 0, h)] + ivk, writes=ps.keys(), tag="kv")
            kb.op("V", lambda E, a=KVb.h[0:64, 0:64], p=ps.h[0:64, 0:64]:
                  E.tensor_copy(a, p),
                  reads=ps.keys(), writes=[("kvb", 0, 0)], force=True,
                  tag="kvc")
            kb.op("T", lambda E, p=pk2t.h[0:64, 0:64] if False else
                  sc.pb[2].h[0:64, 0:64], b=KVb.h[0:64, 0:64]:
                  E.transpose(p, b, C.eye_f[0:64, 0:64]),
                  reads=[("kvb", 0, 0), ("c_eyef", 0, 0)],
                  writes=sc.pb[2].keys(), tag="kvt")
            kb.op("V", lambda E, a=KVTb.h[0:64, 0:64],
                  p=sc.pb[2].h[0:64, 0:64]: E.tensor_copy(a, p),
                  reads=sc.pb[2].keys(), writes=[("kvtb", 0, 0)],
                  force=True, tag="kvtc")
            kb.op("V", lambda E, a=T1_.h[0:64, 0:64],
                  b=EY7.h[0:64, 0:64], m=KVb.h[0:64, 0:64]:
                  E.tensor_sub(a, b, m),
                  reads=[("ey7", 0, 0), ("kvb", 0, 0)],
                  writes=[("t1b", 0, 0)], force=True, tag="m1")
            kb.op("T", lambda E, p=ps.h[0:64, 0:64],
                  l=KVTb.h[0:64, 0:64], r=T1_.h[0:64, 0:64]:
                  E.matmul(p, l, r, start=True, stop=True),
                  reads=[("kvtb", 0, 0), ("t1b", 0, 0)], writes=ps.keys(),
                  tag="m2")
            kb.op("V", lambda E, a=T2_.h[0:64, 0:64],
                  b=EY7.h[0:64, 64:128], p=ps.h[0:64, 0:64]:
                  E.scalar_tensor_tensor(a, b, 1.0, p, ALU.mult,
                                         ALU.subtract),
                  reads=[("ey7", 0, 0)] + ps.keys(), writes=[("t2b", 0, 0)],
                  force=True, tag="m3")
            kb.op("T", lambda E, p=ps.h[0:64, 0:64],
                  l=KVTb.h[0:64, 0:64], r=T2_.h[0:64, 0:64]:
                  E.matmul(p, l, r, start=True, stop=True),
                  reads=[("kvtb", 0, 0), ("t2b", 0, 0)], writes=ps.keys(),
                  tag="m4")
            kb.op("V", lambda E, a=T1_.h[0:64, 0:64],
                  b=EY7.h[0:64, 128:192], p=ps.h[0:64, 0:64]:
                  E.scalar_tensor_tensor(a, b, 1.0, p, ALU.mult,
                                         ALU.subtract),
                  reads=[("ey7", 0, 0)] + ps.keys(), writes=[("t1b", 0, 0)],
                  force=True, tag="m5")
            kb.op("T", lambda E, p=ps.h[0:64, 0:64],
                  l=IVT.h[0:64, h * 64:(h + 1) * 64], r=T1_.h[0:64, 0:64]:
                  E.matmul(p, l, r, start=True, stop=True),
                  reads=ivtk + [("t1b", 0, 0)], writes=ps.keys(), tag="vn")
            kb.op("V", lambda E, a=IV.h[0:64, h * 64:(h + 1) * 64],
                  p=ps.h[0:64, 0:64]: E.tensor_scalar_mul(a, p, 0.25),
                  reads=ps.keys(), writes=ivk, force=True, tag="vns")
            kb.op("T", lambda E, p=sc.pb[2].h[0:64, 0:64],
                  b=IV.h[0:64, h * 64:(h + 1) * 64]:
                  E.transpose(p, b, C.eye_f[0:64, 0:64]),
                  reads=ivk + [("c_eyef", 0, 0)], writes=sc.pb[2].keys(),
                  tag="vnt")
            kb.op("V", lambda E, a=IVT.h[0:64, h * 64:(h + 1) * 64],
                  p=sc.pb[2].h[0:64, 0:64]: E.tensor_copy(a, p),
                  reads=sc.pb[2].keys(), writes=ivtk, force=True,
                  tag="vntc")
        kb.op("T", lambda E, p=sc.pb[1].h[0:64, 0:64],
              l=IVT.h[0:64, h * 64:(h + 1) * 64],
              r=NT.h[0:64, h * 64:(h + 1) * 64]:
              E.matmul(p, l, r, start=True, stop=True),
              reads=[("ivt", 0, h), ("nt", 0, 0)], writes=sc.pb[1].keys(),
              tag="mh")
        kb.op("V", lambda E, a=MH.h[0:64, h * 64:(h + 1) * 64],
              p=sc.pb[1].h[0:64, 0:64]: E.tensor_copy(a, p),
              reads=sc.pb[1].keys(), writes=[("mh", 0, h)], force=True,
              tag="mhc")
    emit_head_norms(kb, C, sc, Q, NCH, 16, sc.hrow_a, sq_scratch=CTX,
                    Tw=TC, tag="qn")
    emit_head_norms(kb, C, sc, KL, NCH, 16, sc.hrow_f, sq_scratch=CTX,
                    Tw=64, tag="kln")
    kb.op("V", lambda E, a=sc.hcol_a.h[0:16, 0:1],
          b=sc.hrow_f.h[0:16, 0:64]: E.reduce_max(a, b, axis=AX.X),
          reads=sc.hrow_f.keys(), writes=sc.hcol_a.keys(), force=True,
          tag="klm")
    kb.op("V", lambda E, a=negb1.h[0:16, 0:TC], b=sc.hrow_a.h[0:16, 0:TC],
          s=sc.hcol_a.h[0:16, 0:1]:
          E.tensor_scalar(a, b, s, -1.0, ALU.mult, ALU.mult),
          reads=sc.hrow_a.keys() + sc.hcol_a.keys(), writes=negb1.keys(),
          tag="nb1")
    for h in range(16):
        hc, hp = h // 2, (h % 2) * 64
        for tt in range(4):
            t0, t1 = tt * TT, (tt + 1) * TT
            ps = sc.pb[3 + (tt % 2)]
            kb.op("T", lambda E, p=ps.h[0:64, 0:TT],
                  l=KL.c(hc, 0, 64, hp, hp + 64),
                  r=Q.c(hc, t0, t1, hp, hp + 64):
                  E.matmul(p, l, r, start=True, stop=False),
                  reads=KL.keys(hc) + Q.keys(hc, t0, t1),
                  writes=ps.keys(), tag="s1")
            kb.op("T", lambda E, p=ps.h[0:64, 0:TT],
                  l=C.sel128[0:16, h * 128:h * 128 + 64],
                  r=negb1.h[0:16, t0:t1]:
                  E.matmul(p, l, r, start=False, stop=True),
                  reads=negb1.keys(), writes=ps.keys(), tag="b1")
            pt_ = sc.ptile[tt % 2]
            kb.op("S", lambda E, a=pt_.h[0:64, 0:TT], p=ps.h[0:64, 0:TT]:
                  E.activation(a, p, AF.Exp, scale=0.125),
                  reads=ps.keys(), writes=pt_.keys(), tag="e1")
            psd = sc.pb[5]
            kb.op("T", lambda E, p=psd.h[0:1, 0:TT],
                  o=C.ones_bf[0:64, 0:1], r=pt_.h[0:64, 0:TT]:
                  E.matmul(p, o, r, start=True, stop=True,
                           skip_group_check=True),
                  reads=pt_.keys(), writes=psd.keys(), tag="d1")
            psc_ = sc.pb[6]
            kb.op("T", lambda E, p=psc_.h[0:64, 0:TT],
                  l=MH.h[0:64, h * 64:(h + 1) * 64], r=pt_.h[0:64, 0:TT]:
                  E.matmul(p, l, r, start=True, stop=True),
                  reads=[("mh", 0, h)] + pt_.keys(), writes=psc_.keys(),
                  tag="cx")
            kb.op("V", lambda E, a=sc.row_t1.r(0, 1, 0, TT),
                  p=psd.h[0:1, 0:TT]: E.reciprocal(a, p),
                  reads=psd.keys(), writes=sc.row_t1.keys(), force=True,
                  tag="rd1")
            psb = sc.pb[7]
            kb.op("T", lambda E, p=psb.h[0:64, 0:TT],
                  o=C.ones_f[0:1, 0:64], r=sc.row_t1.r(0, 1, 0, TT):
                  E.matmul(p, o, r, start=True, stop=True),
                  reads=sc.row_t1.keys(), writes=psb.keys(), tag="bd1")
            kb.op("S", lambda E, a=sc.tt_f32.h[0:64, 0:TT],
                  p=psc_.h[0:64, 0:TT]: E.copy(a, p),
                  reads=psc_.keys(), writes=sc.tt_f32.keys(), tag="cxc")
            kb.op("V", lambda E, a=sc.tt2.h[hp:hp + 64, 0:TT],
                  b=sc.tt_f32.h[0:64, 0:TT], p=psb.h[0:64, 0:TT]:
                  E.tensor_mul(a, b, p),
                  reads=sc.tt_f32.keys() + psb.keys(),
                  writes=sc.tt2.keys(), tag="cxd")
            kb.op("V", lambda E, a=CTX.c(hc, t0, t1, hp, hp + 64),
                  b=sc.tt2.h[hp:hp + 64, 0:TT],
                  m=CONV.c(hc, t0, t1, hp, hp + 64): E.tensor_add(a, b, m),
                  reads=sc.tt2.keys() + CONV.keys(hc, t0, t1),
                  writes=[("ctx", hc, tt, hp)], tag="cxa")
    kb.alias("xsq4", [(CONV.name, c, i) for c in range(NCH)
                      for i in range(4)])
    XSQ = FM(kb, "xsq4", NCH, TC, BF16, back=CONV.h)
    kb.alias("ctxr", [("ctx", c, i, hp) for c in range(NCH)
                      for i in range(4) for hp in (0, 64)])
    CTXr = FM(kb, "ctxr", NCH, TC, BF16, back=CTX.h)
    xtiles = [TBufSimple(kb, f"xt{i}", [128, TT], F32) for i in range(2)]
    emit_ln_add_residual(kb, C, sc, CTXr, xtiles, xsq=XSQ, xb=None,
                         xres_dram=(xT, xout), tag="lnr")
    with nc.Block() as block:
        _finish(nc, kb, block)
    return nc


_BUILD_CACHE = {}


def _build_pretouched(fn, *a):
    Consts.RECORD = []
    Consts.PRETOUCH = None
    try:
        fn(*a)
        used = list(Consts.RECORD)
    finally:
        Consts.RECORD = None
    Consts.PRETOUCH = used
    try:
        return fn(*a)
    finally:
        Consts.PRETOUCH = None


def _get(name, fn, *a):
    key = (name,) + a
    if key not in _BUILD_CACHE:
        _BUILD_CACHE[key] = _build_pretouched(fn, *a)
    return _BUILD_CACHE[key]


def _col(b):
    return np.ascontiguousarray(np.asarray(b, np.float32).reshape(-1, 128).T)


def _cont(a):
    return np.ascontiguousarray(np.asarray(a, np.float32))


def _run(nc, maps):
    return run_bass_kernel_spmd(nc, maps, core_ids=list(range(8))).results


def kernel(**inputs):
    g = lambda n: np.asarray(inputs[n])
    consts = Consts.host_inputs()
    x = g("L0_attn_query").astype(np.float32)
    NCORE = 8
    xT = [np.ascontiguousarray(x[c // 2, (c % 2) * TC:(c % 2 + 1) * TC].T)
          for c in range(NCORE)]
    peer = [c ^ 1 for c in range(NCORE)]

    nc1 = _get("l0aq", build_l0a_qkv)
    m1 = [dict(xT=xT[c], qwT=_cont(g("L0_attn_q_w").T),
               kwT=_cont(g("L0_attn_k_w").T), vwT=_cont(g("L0_attn_v_w").T),
               qb=_col(g("L0_attn_q_b")), kbb=_col(g("L0_attn_k_b")),
               vb=_cont(g("L0_attn_v_b").reshape(1, -1)), **consts)
          for c in range(NCORE)]
    r1 = _run(nc1, m1)
    nc2 = _get("l0aa", build_l0a_attn)
    m2 = []
    for c in range(NCORE):
        me, pe = r1[c], r1[peer[c]]
        km2 = np.concatenate([me["kmax_d"], pe["kmax_d"]], axis=1)
        m2.append(dict(q_d=me["q_d"], ks_d=me["k_d"], kp_d=pe["k_d"],
                       vs_d=me["v_d"], vp_d=pe["v_d"], km2=km2,
                       owT=_cont(g("L0_attn_out_w").T),
                       ob=_col(g("L0_attn_out_b")), xin=xT[c], **consts))
    r2 = _run(nc2, m2)
    xT = [r2[c]["xout"] for c in range(NCORE)]
    _dbg("l0a", xT)

    nc3 = _get("l0f", build_l0f)
    tr = l0f_host_consts()
    m3 = [dict(xT=xT[c], wr=g("L0_ffn_w_freq_real"),
               wi=g("L0_ffn_w_freq_imag"), fbias=_col(g("L0_ffn_bias")),
               **tr, **consts) for c in range(NCORE)]
    r3 = _run(nc3, m3)
    xT = [r3[c]["xout"] for c in range(NCORE)]
    _dbg("l0f", xT)

    qkv_w = g("L1_attn_qkv_w"); qkv_b = g("L1_attn_qkv_b")
    qcols = np.concatenate([np.arange(h * 192, h * 192 + 64)
                            for h in range(16)])
    kcols, vcols = qcols + 64, qcols + 128
    HD = 64
    inv_freq = 1.0 / 10000.0 ** (np.arange(0, HD, 2) / HD)
    tgrid = np.arange(2048.0)
    fr = np.outer(tgrid, inv_freq)
    cos_t, sin_t = np.cos(fr), np.sin(fr)
    ropeM = np.zeros((2 * 8 * 128, 128), np.float32)
    for cch in range(8):
        blk = np.zeros((128, 128), np.float32)
        for hh in range(2):
            h = 2 * cch + hh
            R = np.zeros((64, 64), np.float32)
            R[np.arange(32), np.arange(32)] = cos_t[h]
            R[np.arange(32), np.arange(32) + 32] = -sin_t[h]
            R[np.arange(32) + 32, np.arange(32)] = sin_t[h]
            R[np.arange(32) + 32, np.arange(32) + 32] = cos_t[h]
            blk[hh * 64:(hh + 1) * 64, hh * 64:(hh + 1) * 64] = R
        ropeM[cch * 128:(cch + 1) * 128, :] = blk.T
        ropeM[(8 + cch) * 128:(9 + cch) * 128, :] = blk.T
    nc4 = _get("l1aq", build_qkv_generic, "l1a", True, True)
    m4 = [dict(xT=xT[c], qwT=_cont(qkv_w[:, qcols]),
               kwT=_cont(qkv_w[:, kcols]), vwT=_cont(qkv_w[:, vcols]),
               qb=_col(qkv_b[qcols]), kbb=_col(qkv_b[kcols]),
               vb=_cont(qkv_b[vcols].reshape(1, -1)), ropeM=ropeM, **consts)
          for c in range(NCORE)]
    r4 = _run(nc4, m4)
    m5 = []
    for c in range(NCORE):
        me, pe = r4[c], r4[peer[c]]
        km2 = np.concatenate([me["kmax_d"], pe["kmax_d"]], axis=1)
        m5.append(dict(q_d=me["q_d"], ks_d=me["k_d"], kp_d=pe["k_d"],
                       vs_d=me["v_d"], vp_d=pe["v_d"], km2=km2,
                       owT=_cont(g("L1_attn_out_w")),
                       ob=_col(g("L1_attn_out_b")), xin=xT[c], **consts))
    r5 = _run(nc2, m5)
    xT = [r5[c]["xout"] for c in range(NCORE)]
    _dbg("l1a", xT)

    ncp = _get("l1fp", build_l1f_pre)
    rp = _run(ncp, [dict(xT=xT[c], **consts) for c in range(NCORE)])
    z = [rp[c]["z_d"] for c in range(NCORE)]
    def xpos_tabs(core):
        hh = core % 2
        base_scale = (np.arange(0, 64, 2) + 0.4 * 64) / (1.4 * 64)
        def tabs(tglob):
            frq = np.outer(tglob, inv_freq)
            sc_ = base_scale[None, :] ** ((tglob - 2048.0)[:, None] / 128.0)
            cosv, sinv = np.cos(frq), np.sin(frq)
            def tile(tab):
                m = np.zeros((128, len(tglob)), np.float32)
                for j in range(64):
                    m[j] = tab[:, j % 32]
                    m[j + 64] = tab[:, j % 32]
                return m
            return (tile(cosv * sc_), tile(sinv * sc_),
                    tile(cosv / sc_), tile(sinv / sc_))
        tq = np.arange(hh * 2048.0, hh * 2048.0 + 2048.0)
        tk = np.arange(hh * 2048.0 - 256.0, hh * 2048.0 + 2048.0)
        qc, qs_, _, _ = tabs(tq)
        _, _, kc, ks_ = tabs(tk)
        return qc, qs_, kc, ks_
    xpos = [xpos_tabs(c) for c in range(NCORE)]
    rot = np.zeros((128, 128), np.float32)
    for b0 in range(2):
        o = b0 * 64
        for j in range(32):
            rot[o + j, o + 32 + j] = -1.0
            rot[o + 32 + j, o + j] = 1.0
    rotT = _cont(rot.T)
    lwm = np.full((4 * 128, 256), -1e30, np.float32)
    for jj in range(4):
        for i in range(128):
            for j2 in range(256):
                qk = 256 - jj * 128 + j2 - i
                if 0 <= qk <= 256:
                    lwm[jj * 128 + i, j2] = 0.0
    vones = [np.ones((NKV * 128, 16), np.float32) for c in range(NCORE)]
    for c in range(NCORE):
        if c % 2 == 0:
            vones[c][0:256, :] = 0.0
    nca = _get("l1fa", build_l1f_attn)
    ncf = _get("l1ff", build_l1f_ffn)
    Wqkv6 = g("L1_ffn_Wqkv"); Wo6 = g("L1_ffn_Wo")
    qs6 = g("L1_ffn_q_scale"); ks6 = g("L1_ffn_k_scale")
    lng6 = g("L1_ffn_ln_g"); lnb6 = g("L1_ffn_ln_b")
    W16 = g("L1_ffn_W1"); b16 = g("L1_ffn_b1")
    W26 = g("L1_ffn_W2"); b26 = g("L1_ffn_b2")
    percol64 = lambda v: _cont(np.tile(np.asarray(v), 2).reshape(128, 1))
    for li in range(6):
        Wl = Wqkv6[li]
        ma = []
        for c in range(NCORE):
            zEx = np.zeros((DM, EXTW), np.float32)
            zEx[:, 257:257 + TC] = z[c]
            if c % 2 == 1:
                zEx[:, 0:257] = z[peer[c]][:, TC - 257:]
            ma.append(dict(zE=zEx, Wq=_cont(Wl[:, 0:1024]),
                           Wk=_cont(Wl[:, 1024:2048]),
                           Wv=_cont(Wl[:, 2048:3072]),
                           Wo=_cont(Wo6[li]),
                           qs=percol64(qs6[li]), ks=percol64(ks6[li]),
                           xqc=xpos[c][0], xqs=xpos[c][1],
                           xkc=xpos[c][2], xks=xpos[c][3],
                           rot64=rotT, lwm=lwm, vones=vones[c], **consts))
        ra = _run(nca, ma)
        z = [ra[c]["zout"] for c in range(NCORE)]
        _dbg(f"l1f_attn{li}", z)
        Wa = np.zeros((DM, 2816), np.float32); Wa[:, :2730] = W16[li][:, :2730]
        Wg_ = np.zeros((DM, 2816), np.float32)
        Wg_[:, :2730] = W16[li][:, 2730:5460]
        W2p = np.zeros((2816, DM), np.float32); W2p[:2730] = W26[li]
        b1a = np.zeros(2816, np.float32); b1a[:2730] = b16[li][:2730]
        b1g = np.zeros(2816, np.float32); b1g[:2730] = b16[li][2730:5460]
        mf = []
        for c in range(NCORE):
            zEx = np.zeros((DM, TC + 128), np.float32)
            zEx[:, 128:] = z[c]
            if c % 2 == 1:
                zEx[:, 127:128] = z[peer[c]][:, TC - 1:]
            mf.append(dict(zE=zEx, Wa=Wa, Wg=Wg_, W2=W2p,
                           b1a=_cont(b1a.reshape(22, 128).T),
                           b1g=_cont(b1g.reshape(22, 128).T),
                           b2=_col(b26[li]), lng=_col(lng6[li]),
                           lnb=_col(lnb6[li]), **consts))
        rf = _run(ncf, mf)
        z = [rf[c]["zout"] for c in range(NCORE)]
        _dbg(f"l1f_ffn{li}", z)
    nco = _get("l1fo", build_l1f_post)
    ro = _run(nco, [dict(xT=xT[c], z_d=z[c], **consts)
                    for c in range(NCORE)])
    xT = [ro[c]["xout"] for c in range(NCORE)]
    _dbg("l1f_post", xT)

    nc6 = _get("l2a1", build_l2a_p1)
    m6 = [dict(xT=xT[c], Wq=_cont(g("L2_attn_Wq")), Wk=_cont(g("L2_attn_Wk")),
               Wv=_cont(g("L2_attn_Wv")), bq=_col(g("L2_attn_bq")),
               bk=_col(g("L2_attn_bk")),
               bv=_cont(g("L2_attn_bv").reshape(1, -1)), **consts)
          for c in range(NCORE)]
    r6 = _run(nc6, m6)
    convw = g("L2_attn_conv_w")[:, 0, :, 0]
    convA = np.zeros((16, 5, 128, 512), np.float32)
    for h in range(16):
        for jj in range(5):
            for i in range(128):
                for j2 in range(512):
                    k2 = i - j2 + jj * 128 - 32
                    if 0 <= k2 < 65:
                        convA[h, jj, i, j2] = convw[h, k2]
    convA = convA.reshape(16 * 5 * 128, 512)
    nc7 = _get("l2a2", build_l2a_p2)
    m7 = []
    zer32 = np.zeros((32, DM), np.float32).astype(BF)
    for c in range(NCORE):
        me, pe = r6[c], r6[peer[c]]
        qlA = np.concatenate([r6[c - c % 2]["ql_d"],
                              r6[c - c % 2 + 1]["ql_d"]], axis=1)
        klA = np.concatenate([r6[c - c % 2]["kl_d"],
                              r6[c - c % 2 + 1]["kl_d"]], axis=1)
        km2 = np.concatenate([me["km_d"], pe["km_d"]], axis=1)
        vh_lo = pe["v_d"][TC - 32:] if c % 2 == 1 else zer32
        vh_hi = pe["v_d"][0:32] if c % 2 == 0 else zer32
        m7.append(dict(k_d=me["k_d"], v_d=me["v_d"], qlA=qlA, klA=klA,
                       km2=km2, convA=convA, vh_lo=vh_lo, vh_hi=vh_hi,
                       **consts))
    r7 = _run(nc7, m7)
    nc8 = _get("l2a3", build_l2a_p3)
    cs8 = np.concatenate([r7[c]["cs_d"] for c in range(NCORE)],
                         axis=1)
    m8 = []
    for c in range(NCORE):
        me, pe = r7[c], r7[peer[c]]
        qlA = np.concatenate([r6[c - c % 2]["ql_d"],
                              r6[c - c % 2 + 1]["ql_d"]], axis=1)
        klA = np.concatenate([r6[c - c % 2]["kl_d"],
                              r6[c - c % 2 + 1]["kl_d"]], axis=1)
        m8.append(dict(xT=xT[c], q_d=r6[c]["q_d"], klA=klA,
                       k2_d=me["k2_d"], n_s=me["n_d"], n_p=pe["n_d"],
                       d3_s=me["d3_d"], d3_p=pe["d3_d"], cs8=cs8,
                       conv_d=me["conv_d"], **consts))
    r8 = _run(nc8, m8)
    xT = [r8[c]["xout"] for c in range(NCORE)]
    _dbg("l2a", xT)

    HDv = 64
    frq = np.outer(np.arange(4096.0), inv_freq)
    cosT, sinT = np.cos(frq), np.sin(frq)
    def ropetile(tab, hh):
        m = np.zeros((128, 2048), np.float32)
        sl = tab[hh * 2048:(hh + 1) * 2048]
        for j in range(64):
            m[j] = sl[:, j % 32]
            m[j + 64] = sl[:, j % 32]
        return m
    trilm = np.zeros((4 * 128, 512), np.float32)
    for cc in range(4):
        for i in range(128):
            trilm[cc * 128 + i, cc * 128 + i:] = 1.0
    nc9 = _get("l2f1", build_l2f_p1)
    m9 = [dict(xT=xT[c], WqT=_cont(g("L2_ffn_Wq")), WkT=_cont(g("L2_ffn_Wk")),
               WvT=_cont(g("L2_ffn_Wv")), bq=_col(g("L2_ffn_bq")),
               bk=_cont(g("L2_ffn_bk").reshape(2, 128).T),
               bv=_cont(g("L2_ffn_bv").reshape(1, -1)),
               ln1=_col(g("L2_ffn_ln1_w")), qnw=percol64(g("L2_ffn_qn_w")),
               knw=percol64(g("L2_ffn_kn_w")),
               ropeC=ropetile(cosT, c % 2), ropeS=ropetile(sinT, c % 2),
               rot64=rotT, **consts) for c in range(NCORE)]
    r9 = _run(nc9, m9)
    nc10 = _get("l2f2", build_l2f_p2)
    m10 = []
    for c in range(NCORE):
        me, pe = r9[c], r9[peer[c]]
        km2 = np.zeros((16, 2), np.float32)
        km2[:, 0] = np.repeat(me["kmax_d"][0:4, 0], 4)
        km2[:, 1] = (np.repeat(pe["kmax_d"][0:4, 0], 4) if c % 2 == 1
                     else km2[:, 0])
        kp = pe["k_d"] if c % 2 == 1 else np.zeros_like(pe["k_d"])
        vp = pe["v_d"] if c % 2 == 1 else np.zeros_like(pe["v_d"])
        m10.append(dict(u0_d=me["u0_d"], q_d=me["q_d"], ks_d=me["k_d"],
                        kp_d=kp, vs_d=me["v_d"], vp_d=vp, km2=km2,
                        trilm=trilm, Wg=_cont(g("L2_ffn_Wg")),
                        bg=percol64(g("L2_ffn_bg")),
                        Wo=_cont(g("L2_ffn_Wo")), bo=_col(g("L2_ffn_bo")),
                        **consts))
    r10 = _run(nc10, m10)
    b1c = np.zeros((128, 8 * 32), np.float32)
    b1_ = g("L2_ffn_b1")
    for e in range(8):
        b1c[:, e * 32:(e + 1) * 32] = b1_[e].reshape(32, 128).T
    nc11 = _get("l2f3", build_l2f_p3)
    m11 = [dict(xT=xT[c], xi_d=r10[c]["xi_d"], ln2=_col(g("L2_ffn_ln2_w")),
                WrT=_cont(g("L2_ffn_Wr")),
                brr=_cont(g("L2_ffn_br").reshape(1, -1)),
                W1=_cont(g("L2_ffn_W1").reshape(8 * 1024, 4096)),
                W2=_cont(g("L2_ffn_W2").reshape(8 * 2048, 1024)),
                b1c=b1c, b2r=_cont(g("L2_ffn_b2")), **consts)
           for c in range(NCORE)]
    r11 = _run(nc11, m11)
    xT = [r11[c]["xout"] for c in range(NCORE)]
    _dbg("l2f", xT)

    nc12 = _get("l3aq", build_l3a_qkv)
    m12 = [dict(xT=xT[c], qwT=_cont(g("L3_attn_Wq").T),
                kwT=_cont(g("L3_attn_Wk").T), vwT=_cont(g("L3_attn_Wv").T),
                qb=_col(g("L3_attn_bq")), kbb=_col(g("L3_attn_bk")),
                vb=_cont(g("L3_attn_bv").reshape(1, -1)), **consts)
           for c in range(NCORE)]
    r12 = _run(nc12, m12)
    eyem = np.ones((4 * 128, 512), np.float32)
    for cc in range(4):
        for i in range(128):
            j2 = i + cc * 128
            if j2 < 512:
                eyem[cc * 128 + i, j2] = 0.0
    nc13 = _get("l3aa", build_l3a_attn, 0)
    m13 = []
    for c in range(NCORE):
        me, pe = r12[c], r12[peer[c]]
        km2 = np.concatenate([me["kmax_d"], pe["kmax_d"]], axis=1)
        m13.append(dict(q_d=me["q_d"], ks_d=me["k_d"], kp_d=pe["k_d"],
                        vs_d=me["v_d"], vp_d=pe["v_d"], km2=km2, eyem=eyem,
                        xin=xT[c], **consts))
    r13 = _run(nc13, m13)
    xT = [r13[c]["xout"] for c in range(NCORE)]
    _dbg("l3a", xT)

    nc14 = _get("l3f", build_l3f)
    w1T = _cont(g("L3_ffn_w1").transpose(0, 2, 1).reshape(8 * 1024, 2048))
    w2T = _cont(g("L3_ffn_w2").transpose(0, 2, 1).reshape(8 * 2048, 1024))
    wrt = _cont(np.concatenate([g("L3_ffn_w_macro").T,
                                g("L3_ffn_w_micro").reshape(8, 1024).T],
                               axis=1))
    m14 = [dict(xT=xT[c], w1T=w1T, w2T=w2T, wrt=wrt,
                nw=_col(g("L3_ffn_norm_w")), **consts)
           for c in range(NCORE)]
    r14 = _run(nc14, m14)
    xT = [r14[c]["xout"] for c in range(NCORE)]
    _dbg("l3f", xT)

    out = np.zeros((4, 4096, 1024), np.float32)
    for c in range(NCORE):
        out[c // 2, (c % 2) * TC:(c % 2 + 1) * TC] = xT[c].T
    return out
